# version 09
# baseline (speedup 1.0000x reference)
import numpy as np
import ml_dtypes
import concourse.bass as bass
import concourse.mybir as mybir
from concourse.bass_utils import run_bass_kernel_spmd

F32 = mybir.dt.float32
BF16 = mybir.dt.bfloat16
F16 = mybir.dt.float16
I8 = mybir.dt.int8
I16 = mybir.dt.int16
AF = mybir.ActivationFunctionType
ALU = mybir.AluOpType

D = 2048
S = 4096
H = 16
NT = 16
E_QKV = 2304
NEC = 18
SCALE = float(1.0 / np.sqrt(128.0))

ENGS = ("sync", "scalar", "vector", "tensor", "gpsimd")
DBG_STAGE = 0


class Sem:
    def __init__(self, h):
        self.h = h
        self.n = 0
        self.wmax = 0
        self.guarded = 0
        self.queue = None


class Bld:
    def __init__(self):
        self.th = {e: [] for e in ENGS}

    def emit(self, e, fn):
        self.th[e].append(fn)

    def wait(self, e, sem, val):
        if val > 0:
            sem.wmax = max(sem.wmax, val)
            self.th[e].append(lambda eng, s=sem.h, v=val: eng.wait_ge(s, v))

    def drain(self, e):
        self.th[e].append(lambda eng: eng.drain())

    def run(self, block):
        for e in ENGS:
            def mk(ths):
                def body(eng):
                    for f in ths:
                        f(eng)
                return body
            getattr(block, e)(mk(self.th[e]))


def attn_tiles(kind, qc):
    out = []
    if kind == "a":
        for js in range(8 * qc + 8):
            msk = js - 8 * qc if js >= 8 * qc else None
            out.append((js, msk))
    else:
        for ph in range(2):
            for m_ in range(4 * qc + 4):
                msk = ph * 4 + (m_ - 4 * qc) if m_ >= 4 * qc else None
                out.append(((ph, m_), msk))
    return out


def build_nc(dbg_stage=0):
    nc = bass.Bass("TRN2", target_bir_lowering=False)

    def din(name, shape, dt):
        return nc.dram_tensor(name, shape, dt, kind="ExternalInput").ap()

    x0own = din("x0own", [2048, D], F32)
    x0stg = din("x0stg", [S, D], F32)
    wqkv = din("wqkv", [E_QKV, D], F32)
    bqkv_f = din("bqkv_f", [128, NEC], F32)
    outw = din("outw", [D, D], F32)
    ffnw = din("ffnw", [D, D], I8)
    wscale_t = din("wscale_t", [128, 16], F32)
    wq_in = din("wq", [H, D, 128], F32)
    wk_in = din("wk", [H, D, 128], F32)
    wv_in = din("wv", [H, D, 128], F32)
    wo_in = din("wo", [H, 128, D], F32)
    bq_t = din("bq_t", [128, H], F32)
    bk_t = din("bk_t", [128, H], F32)
    brows_in = din("brows", [4, D], F32)
    bva_in = din("bva", [1, 128], F32)
    ones_in = din("ones_f32", [1, 128], F32)
    eps_in = din("epscol", [128, 1], F32)
    ident_in = din("ident", [128, 128], BF16)
    onesm_in = din("onesm", [128, 128], BF16)
    maskq_in = din("maskq", [8, 128, 512], BF16)
    maskb_in = din("maskb", [8, 128, 512], BF16)

    y = nc.dram_tensor("y", [2048, D], F32, kind="ExternalOutput").ap()

    cc_kv_in = nc.dram_tensor("cc_kv_in", [4096, D], BF16, kind="Internal").ap()
    cc_kv_out = nc.dram_tensor("cc_kv_out", [8192, D], BF16, kind="Internal",
                               addr_space="Local").ap()
    cc_nrm_in = nc.dram_tensor("cc_nrm_in", [1, H], F32, kind="Internal").ap()
    cc_nrm_out = nc.dram_tensor("cc_nrm_out", [2, H], F32, kind="Internal",
                                addr_space="Local").ap()
    wqkvT_d = nc.dram_tensor("wqkvT_d", [D, E_QKV], BF16, kind="Internal").ap()
    outwT_d = nc.dram_tensor("outwT_d", [D, D], BF16, kind="Internal").ap()
    ffnwT_d = nc.dram_tensor("ffnwT_d", [D, D], BF16, kind="Internal").ap()
    wqb_d = nc.dram_tensor("wqb_d", [H * D, 128], BF16, kind="Internal").ap()
    wkb_d = nc.dram_tensor("wkb_d", [H * D, 128], BF16, kind="Internal").ap()
    wvg_d = nc.dram_tensor("wvg_d", [D, D], BF16, kind="Internal").ap()
    wob_d = nc.dram_tensor("wob_d", [H * 128, D], BF16, kind="Internal").ap()
    RG = [[0, 1], [2, 3], [4, 5], [6, 7]]

    import contextlib
    with contextlib.ExitStack() as stk:
        def sbuf(name, shape, dt):
            return stk.enter_context(nc.sbuf_tensor(name, shape, dt))

        def mksem(name):
            return Sem(stk.enter_context(nc.semaphore(name)))

        x_sb = sbuf("x_sb", [128, NT * D], BF16)
        masks = sbuf("masksb", [128, 8 * 512], BF16)
        ident = sbuf("identb", [128, 128], BF16)
        onesm = sbuf("onesmb", [128, 128], BF16)
        b1row = sbuf("b1row", [1, 2304], F32)
        bq_sb = sbuf("bq_sbb", [128, H], F32)
        bk_sb = sbuf("bk_sbb", [128, H], F32)
        consts = sbuf("constsb", [128, 40], F32)
        sc1 = sbuf("sc1b", [128, 128], F32)
        sc2 = sbuf("sc2b", [128, 192], F32)
        nrm2 = sbuf("nrm2b", [1, 32], F32)
        negC = sbuf("negCb", [128, H], F32)
        fp16t = sbuf("fp16tb", [128, 512], F16)
        i16t = sbuf("i16tb", [128, 1024], I16)
        i8st = sbuf("i8stb", [128, D], I8)
        dscr = sbuf("dscrb", [128, 1024], F32)
        sqs = sbuf("sqsb", [128, D], BF16)
        AR = 56832
        ar = sbuf("arena", [128, AR], BF16)
        QH = 0
        WCOL = 16384
        PSL = 16384
        WOUT = 16384
        XHO = 20480
        VA = 24576
        KA = 28672
        XT = 32768
        KVB = 32768
        PSL3 = 40960
        XHI = 49152
        SM3 = 49152
        WC3 = 53248
        KST = 55296

        psum = stk.enter_context(nc.psum_tensor("psumT", [128, 3072], F32))
        psT = stk.enter_context(nc.psum_tensor("psumB", [128, 2048], BF16))

        sT = mksem("sT")
        sV = mksem("sV")
        sS = mksem("sS")
        dQ = mksem("dQ")
        dS = mksem("dS2")
        dV = mksem("dV2")
        dG = mksem("dG2")
        cC = mksem("cC")

        b = Bld()

        def dma(q, qsem, out, in_):
            qsem.queue = q
            if qsem.wmax > qsem.guarded:
                g = qsem.wmax
                b.th[q].append(lambda eng, s=qsem.h, v=g: eng.wait_ge(s, v))
                qsem.guarded = g
            b.emit(q, lambda eng, o=out, i=in_, s=qsem.h:
                   eng.dma_start(out=o, in_=i).then_inc(s, 16))
            qsem.n += 16
            return qsem.n

        def op(e, esem, f, inc=True):
            if inc:
                esem.n += 1
                b.emit(e, lambda eng, f=f, s=esem.h: f(eng).then_inc(s, 1))
                return esem.n
            b.emit(e, lambda eng, f=f: f(eng))
            return esem.n

        W = b.wait

        def ckpt(qname, qsem):
            v = qsem.n
            if v > qsem.guarded:
                b.th[qname].append(lambda eng, s=qsem.h, vv=v: eng.wait_ge(s, vv))
                qsem.guarded = v
            qsem.wmax = max(qsem.wmax, v)
            return v

        def mm(col, w, lhsT, rhs, start, stop, inc=None):
            if inc is None:
                inc = stop
            return op("tensor", sT,
                      lambda eng, c=col, w=w, l=lhsT, r=rhs, st=start, sp=stop:
                      eng.matmul(psum[:, c:c + w], l, r, start=st, stop=sp),
                      inc=inc)

        def biasmm(col, w, rhs_bias, stop=True, inc=None):
            if inc is None:
                inc = stop
            return op("tensor", sT,
                      lambda eng, c=col, w=w, r=rhs_bias:
                      eng.matmul(psum[:, c:c + w], b1row[0:1, 0:128], r,
                                 start=False, stop=stop), inc=inc)

        dma("sync", dQ, ident[:], ident_in[:])
        dma("sync", dQ, onesm[:], onesm_in[:])
        dma("sync", dQ, b1row[0:1, 0:128], ones_in[:])
        for r in range(8):
            dma("sync", dQ, masks[:, r * 512:(r + 1) * 512], maskq_in[r])
        dma("sync", dQ, consts[:, 0:NEC], bqkv_f[:])
        dma("sync", dQ, consts[:, 18:34], wscale_t[:])
        dma("sync", dQ, bq_sb[:], bq_t[:])
        dma("sync", dQ, bk_sb[:], bk_t[:])
        dma("sync", dQ, consts[:, 34:35], eps_in[:])
        v_consts = ckpt("sync", dQ)
        for e_ in ("tensor", "vector", "scalar", "gpsimd"):
            W(e_, dQ, v_consts)
        dma("scalar", dS, b1row[0:1, 128:256], bva_in[:])
        v_bva = dS.n
        for i in range(NT):
            dma("gpsimd", dG, x_sb[:, i * D:(i + 1) * D],
                x0own[i * 128:(i + 1) * 128, :])
        v_xown = ckpt("gpsimd", dG)

        RAW = [0, 2304]
        EVA = [4608, 6912]
        war_raw = [0, 0]
        war_i8 = [0, 0]
        war_eva = [0, 0]
        war_bank = {}

        def bank_wait(e, col):
            if col in war_bank:
                s_, v_ = war_bank[col]
                W(e, s_, v_)

        def prep_chunk(src_row_ap, dst_ap, ncols_c, sl, load_q, load_sem,
                       scale_col=None, rawbase=None, evabase=None):
            rb_ = RAW[sl] if rawbase is None else rawbase[sl]
            eb_ = EVA[sl] if evabase is None else evabase[sl]
            raw = ar[:, rb_:rb_ + ncols_c * 128]
            eva = ar[:, eb_:eb_ + ncols_c * 128]
            if scale_col is None:
                W(load_q, sT, war_raw[sl])
                vld = dma(load_q, load_sem, raw, src_row_ap)
                vsc = None
            else:
                io = (0 if sl == 0 else 1024)
                W(load_q, sS, war_i8[sl])
                vld = dma(load_q, load_sem,
                          i8st[:, io:io + ncols_c * 128], src_row_ap)
                W("scalar", load_sem, vld)
                W("scalar", sT, war_raw[sl])
                vsc = op("scalar", sS, lambda eng, raw=raw, c=scale_col, io=io,
                         n=ncols_c * 128:
                         eng.activation(raw, i8st[:, io:io + n], AF.Identity,
                                        scale=consts[:, c:c + 1]))
            vv_last = 0
            for g in range((ncols_c + 3) // 4):
                bank = g % 2
                n_in = min(4, ncols_c - 4 * g)
                if vsc is None:
                    W("tensor", load_sem, vld)
                else:
                    W("tensor", sS, vsc)
                bank_wait("tensor", ("t", bank))
                for k in range(n_in):
                    c = 4 * g + k
                    vt = op("tensor", sT, lambda eng, raw=raw, c=c, bank=bank, k=k:
                            eng.transpose(psT[:, bank * 1024 + k * 128:
                                              bank * 1024 + (k + 1) * 128],
                                          raw[:, c * 128:(c + 1) * 128], ident[:]),
                            inc=(k == n_in - 1))
                war_raw[sl] = vt
                if scale_col is not None:
                    war_i8[sl] = vsc
                W("vector", sT, vt)
                W("vector", dS, dS.n)
                for k in range(n_in):
                    c = 4 * g + k
                    vv = op("vector", sV, lambda eng, eva=eva, c=c, bank=bank, k=k:
                            eng.tensor_copy(eva[:, c * 128:(c + 1) * 128],
                                            psT[:, bank * 1024 + k * 128:
                                                bank * 1024 + (k + 1) * 128]),
                            inc=(k == n_in - 1))
                war_bank[("t", bank)] = (sV, vv)
                vv_last = vv
            W("scalar", sV, vv_last)
            war_eva[sl] = dma("scalar", dS, dst_ap,
                              eva.rearrange("p (c e) -> p c e", e=128))
            return war_eva[sl]

        for rb in range(NEC):
            prep_chunk(wqkv[rb * 128:(rb + 1) * 128, :],
                       wqkvT_d.rearrange("(c dd) e -> dd c e", dd=128)
                       [:, :, rb * 128:(rb + 1) * 128], 16, rb % 2, "gpsimd", dG)
        v_wqkvT = ckpt("scalar", dS)
        for rb in range(16):
            prep_chunk(outw[rb * 128:(rb + 1) * 128, :],
                       outwT_d.rearrange("(c dd) e -> dd c e", dd=128)
                       [:, :, rb * 128:(rb + 1) * 128], 16, rb % 2, "gpsimd", dG)
        v_outwT = ckpt("scalar", dS)
        v_wprep = [0]
        v_ffnwT = [0]
        late_done = [False]

        def emit_late_preps():
            if late_done[0]:
                return
            late_done[0] = True
        v_prep_arena_free = 0

        ffnwT_started = [False]

        def emit_ffnwT_part(i):
            if not ffnwT_started[0]:
                ffnwT_started[0] = True
                W("sync", sS, sS.n)
                W("sync", sV, sV.n)
                W("sync", dQ, v_consts)
            rb2 = [XHI, XHI + 1024]
            eb2 = [XHI + 2048, XHI + 3072]
            rb, hb = i // 2, i % 2
            prep_chunk(ffnw[rb * 128:(rb + 1) * 128,
                            hb * 1024:(hb + 1) * 1024],
                       ffnwT_d[hb * 1024:, :]
                       .rearrange("(c dd) e -> dd c e", dd=128)
                       [:, 0:8, rb * 128:(rb + 1) * 128],
                       8, i % 2, "sync", dQ,
                       scale_col=18 + rb, rawbase=rb2, evabase=eb2)

        def layernorm_tile(src, dst, sl):
            for c in range(4):
                op("vector", sV, lambda eng, s=src, sl=sl, c=c: eng.bn_stats(
                    sc2[:, 24 * sl + 6 * c:24 * sl + 6 * c + 6],
                    s[:, c * 512:(c + 1) * 512]), inc=False)
            b.drain("vector")
            vst = op("vector", sV, lambda eng, sl=sl: eng.bn_aggr(
                sc1[:, 2 * sl:2 * sl + 2], sc2[:, 24 * sl:24 * sl + 24]))
            W("scalar", sV, vst)
            sst = op("scalar", sS, lambda eng, sl=sl: eng.activation(
                sc1[:, 16 + sl:17 + sl], sc1[:, 2 * sl + 1:2 * sl + 2],
                AF.Sqrt, bias=consts[:, 34:35]))
            W("vector", sS, sst)
            op("vector", sV, lambda eng, sl=sl: eng.reciprocal(
                sc1[:, 24 + sl:25 + sl], sc1[:, 16 + sl:17 + sl]), inc=False)
            b.drain("vector")
            vr = op("vector", sV, lambda eng, sl=sl: eng.tensor_scalar(
                sc1[:, 32 + sl:33 + sl], sc1[:, 2 * sl:2 * sl + 1],
                sc1[:, 24 + sl:25 + sl], -1.0, ALU.mult, ALU.mult))
            W("scalar", sV, vr)
            return op("scalar", sS, lambda eng, s=src, d=dst, sl=sl: eng.activation(
                d, s, AF.Identity, bias=sc1[:, 32 + sl:33 + sl],
                scale=sc1[:, 24 + sl:25 + sl]))

        def tile_to_xT(src, tcol, ready_pairs, war_v=0):
            vv_last = 0
            for g in range(4):
                bank = g % 2
                for (s_, v_) in ready_pairs:
                    W("tensor", s_, v_)
                bank_wait("tensor", ("t", bank))
                for k in range(4):
                    c = 4 * g + k
                    vt = op("tensor", sT, lambda eng, s=src, c=c, bank=bank, k=k:
                            eng.transpose(psT[:, bank * 1024 + k * 128:
                                              bank * 1024 + (k + 1) * 128],
                                          s[:, c * 128:(c + 1) * 128], ident[:]),
                            inc=(k == 3))
                W("vector", sT, max(vt, war_v))
                for k in range(4):
                    c = 4 * g + k
                    vv = op("vector", sV, lambda eng, c=c, bank=bank, k=k, t=tcol:
                            eng.tensor_copy(
                                ar[:, XT + c * 1024 + t * 128:
                                   XT + c * 1024 + (t + 1) * 128],
                                psT[:, bank * 1024 + k * 128:
                                    bank * 1024 + (k + 1) * 128]),
                            inc=(k == 3))
                war_bank[("t", bank)] = (sV, vv)
                vv_last = vv
            return vv_last

        def normmax(src, ncols, pcol, out_col, extra_sq_wait=()):
            nch = ncols // 512
            cdone = 0
            for big in range(0, ncols, 2048):
                bw = min(2048, ncols - big)
                for (s_, v_) in extra_sq_wait:
                    W("scalar", s_, v_)
                W("scalar", sT, sqs_war[0])
                vsq = op("scalar", sS, lambda eng, s=src, b0=big, bw=bw:
                         eng.activation(sqs[:, 0:bw], s[:, b0:b0 + bw], AF.Square))
                for cc2 in range(bw // 512):
                    cch = cdone
                    cdone += 1
                    W("tensor", sS, vsq)
                    bank_wait("tensor", pcol)
                    vmm = mm(pcol, 512, onesm[:],
                             sqs[:, cc2 * 512:(cc2 + 1) * 512], True, True)
                    sqs_war[0] = vmm
                    W("vector", sT, vmm)
                    b.drain("vector")
                    vrd = op("vector", sV,
                             lambda eng, cch=cch, pc=pcol, oc=out_col:
                             eng.reduce_max(
                                 sc1[:, 64 + oc * 8 + cch:65 + oc * 8 + cch],
                                 psum[:, pc:pc + 512],
                                 axis=mybir.AxisListType.X))
                    war_bank[pcol] = (sV, vrd)
            b.drain("vector")
            vfin = op("vector", sV, lambda eng, oc=out_col, n=nch:
                      eng.tensor_reduce(sc1[:, 40 + oc:41 + oc],
                                        sc1[:, 64 + oc * 8:64 + oc * 8 + n],
                                        axis=mybir.AxisListType.X, op=ALU.max))
            return 40 + out_col, vfin

        xhi_war = [0, 0]
        xho_war = [0, 0]
        wcol_war = [0, 0]
        v_out_a = [0, 0]

        def s1_gq(gq):
            hh = gq // 2
            vxt = 0
            for j in range(8):
                js = gq * 8 + j
                sl = js % 2
                xin = ar[:, XHI + sl * D:XHI + sl * D + D]
                W("gpsimd", sS, xhi_war[sl])
                vld = dma("gpsimd", dG, xin, x0stg[js * 128:(js + 1) * 128, :])
                W("vector", dG, vld)
                xout = ar[:, XHO + sl * D:XHO + sl * D + D]
                W("vector", sT, max(xho_war[sl], war_wout[0]))
                W("scalar", sT, max(xho_war[sl], war_wout[0]))
                vn = layernorm_tile(xin, xout, js % 8)
                xhi_war[sl] = vn
                vxt = tile_to_xT(xout, j, [(sS, vn)])
                xho_war[sl] = sT.n
            for e in range(16):
                wsl = e % 2
                wc = ar[:, WCOL + wsl * 2048:WCOL + wsl * 2048 + 2048]
                W("sync", sT, max(wcol_war[wsl], war_wout[0]))
                W("sync", dS, v_wqkvT)
                vw = dma("sync", dQ, wc.rearrange("p (c e) -> p c e", e=128),
                         wqkvT_d.rearrange("(c dd) e -> dd c e", dd=128)
                         [:, :, e * 128:(e + 1) * 128])
                pc = (e % 2) * 512
                W("tensor", dQ, vw)
                W("tensor", sV, vxt)
                bank_wait("tensor", pc)
                for d in range(16):
                    xc = ar[:, XT + d * 1024:XT + (d + 1) * 1024]
                    rhs = xc.rearrange("p (a c) -> p a c", c=128)[:, 0:8:2, :]
                    vmm = mm(pc, 512, wc[:, d * 128:(d + 1) * 128], rhs,
                             start=(d == 0), stop=(d == 15))
                wcol_war[wsl] = vmm
                W("scalar", sT, vmm)
                if gq < 2 and e <= 8:
                    W("scalar", dQ, v_prep_arena_free)
                W("scalar", sT, war_wout[0])
                qdst = ar[:, QH + e * 1024 + (gq % 2) * 512:
                          QH + e * 1024 + (gq % 2) * 512 + 512]
                vev = op("scalar", sS, lambda eng, dd=qdst, pc=pc, e=e:
                         eng.activation(dd, psum[:, pc:pc + 512], AF.Identity,
                                        bias=consts[:, e:e + 1]))
                war_bank[pc] = (sS, vev)
            v_qh = sS.n
            wc = ar[:, WCOL:WCOL + 2048]
            W("sync", sT, max(wcol_war[0], war_wout[0]))
            W("sync", dS, v_wqkvT)
            vw = dma("sync", dQ, wc.rearrange("p (c e) -> p c e", e=128),
                     wqkvT_d.rearrange("(c dd) e -> dd c e", dd=128)
                     [:, :, 16 * 128:17 * 128])
            W("tensor", dQ, vw)
            W("tensor", sV, vxt)
            bank_wait("tensor", 1024)
            bank_wait("tensor", 1536)
            for d in range(16):
                xc = ar[:, XT + d * 1024:XT + (d + 1) * 1024]
                mm(1024, 512, wc[:, d * 128:(d + 1) * 128], xc[:, 0:512],
                   start=(d == 0), stop=(d == 15), inc=False)
                vmm = mm(1536, 512, wc[:, d * 128:(d + 1) * 128], xc[:, 512:1024],
                         start=(d == 0), stop=(d == 15), inc=(d == 15))
            wcol_war[0] = vmm
            W("scalar", sT, vmm)
            vk = op("scalar", sS, lambda eng, gq=gq: eng.activation(
                ar[:, KA + gq * 1024:KA + (gq + 1) * 1024],
                psum[:, 1024:2048], AF.Identity, bias=consts[:, 16:17]))
            war_bank[1024] = (sS, vk)
            war_bank[1536] = (sS, vk)
            wc2 = ar[:, WCOL + 2048:WCOL + 4096]
            W("sync", sT, max(wcol_war[1], war_wout[0]))
            vw2 = dma("sync", dQ, wc2.rearrange("p (c e) -> p c e", e=128),
                      wqkvT_d.rearrange("(c dd) e -> dd c e", dd=128)
                      [:, :, 17 * 128:18 * 128])
            W("tensor", dQ, vw2)
            W("tensor", dS, v_bva)
            for tt in range(8):
                pc = 2048 + (tt % 2) * 512
                bank_wait("tensor", pc)
                for d in range(16):
                    xc = ar[:, XT + d * 1024:XT + (d + 1) * 1024]
                    mm(pc, 128, xc[:, tt * 128:(tt + 1) * 128],
                       wc2[:, d * 128:(d + 1) * 128],
                       start=(d == 0), stop=False, inc=False)
                vb = biasmm(pc, 128, b1row[0:1, 128:256], stop=True)
                wcol_war[1] = vb
                W("vector", sT, vb)
                vva = op("vector", sV, lambda eng, gq=gq, tt=tt, pc=pc:
                         eng.tensor_copy(
                             ar[:, VA + (gq * 8 + tt) * 128:
                                VA + (gq * 8 + tt + 1) * 128],
                             psum[:, pc:pc + 128]))
                war_bank[pc] = (sV, vva)
            return v_qh

        ka_ready = {}
        sqs_war = [0]
        attn_war = {"p": [0] * 8, "s": [0] * 4}

        def normmax_to(src_ap, ncols, pcol, dst_col, sq_slot, wait_pairs):
            nch = ncols // 512
            cdone = 0
            red_marks = []
            for big in range(0, ncols, 1024):
                bw = min(1024, ncols - big)
                for (s_, v_) in wait_pairs:
                    W("scalar", s_, v_)
                W("scalar", sT, sqs_war[0])
                W("scalar", sV, sqs_rd[sq_slot])
                vsq = op("scalar", sS, lambda eng, s=src_ap, b0=big, bw=bw,
                         o=sq_slot * 1024:
                         eng.activation(sqs[:, o:o + bw], s[:, b0:b0 + bw],
                                        AF.Square))
                for cc2 in range(bw // 512):
                    cch = cdone
                    cdone += 1
                    W("tensor", sS, vsq)
                    bank_wait("tensor", pcol)
                    vmm = mm(pcol, 512, onesm[:],
                             sqs[:, sq_slot * 1024 + cc2 * 512:
                                 sq_slot * 1024 + (cc2 + 1) * 512], True, True)
                    sqs_war[0] = vmm
                    sqs_rd[sq_slot] = 0
                    W("vector", sT, vmm)
                    b.drain("vector")
                    vrd = op("vector", sV, lambda eng, cch=cch, pc=pcol,
                             dc=64 + sq_slot * 8 + cch % 8:
                             eng.reduce_max(sc1[:, dc:dc + 1],
                                            psum[:, pc:pc + 512],
                                            axis=mybir.AxisListType.X))
                    war_bank[pcol] = (sV, vrd)
                    red_marks.append(vrd)
            b.drain("vector")
            vfin = op("vector", sV, lambda eng, dc=dst_col, n=nch,
                      o=64 + sq_slot * 8:
                      eng.tensor_reduce(sc1[:, dc:dc + 1], sc1[:, o:o + n],
                                        axis=mybir.AxisListType.X, op=ALU.max))
            return vfin

        sqs_rd = [0, 0]

        def s1_attention(hh, v_qh, filler=None):
            attn_war["p"] = [max(v, war_wout[0]) for v in attn_war["p"]]
            ksrc = ar[:, KA:KA + 16 * (hh + 1) * 128]
            vnk = normmax_to(ksrc, 16 * (hh + 1) * 128, 512, 96, 0,
                             [(sS, ka_ready[hh])])
            for h in range(H):
                qsrc = ar[:, QH + h * 1024:QH + (h + 1) * 1024]
                vq_ = normmax_to(qsrc, 1024, 512, 100 + h, h % 2,
                                 [(sS, v_qh)])
            b.drain("vector")
            for h in range(H):
                op("vector", sV, lambda eng, h=h: eng.tensor_scalar(
                    sc1[:, 100 + h:101 + h], sc1[:, 100 + h:101 + h],
                    sc1[:, 96:97], 1e-20, ALU.mult, ALU.max), inc=False)
            b.drain("vector")
            vpr = op("vector", sV, lambda eng: eng.memset(sc1[:, 127:128], 0.0))
            W("scalar", sV, vpr)
            vsq = op("scalar", sS, lambda eng: eng.activation(
                sc1[:, 80:96], sc1[:, 100:116], AF.Sqrt))
            W("vector", sS, vsq)
            b.drain("vector")
            vC = op("vector", sV, lambda eng: eng.tensor_scalar(
                negC[:, 0:16], sc1[:, 80:96], -SCALE, 0.0,
                ALU.mult, ALU.bypass))
            vctx = 0
            for h in range(H):
                if filler is not None:
                    filler(2 * h)
                    filler(2 * h + 1)
                for qc in (2 * hh, 2 * hh + 1):
                    vctx = run_attn_inner("a", hh, h, qc, vC, v_qh)
            return vctx

        def run_attn_inner(kind, hh, h, qc, vC, v_qsrc, kv_marks=None):
            tiles = attn_tiles(kind, qc)
            qbase = QH + h * (1024 if kind == "a" else 2048)
            qoff = (qc - 2 * hh) * 512 if kind == "a" else qc * 512
            qap = ar[:, qbase + qoff:qbase + qoff + 512]
            psl = PSL if kind == "a" else PSL3
            pslot_war = attn_war["p"]
            sbank_war = attn_war["s"]
            n = len(tiles)
            def kap_of(src):
                if kind == "a":
                    return ar[:, KA + src * 128:KA + (src + 1) * 128]
                ph, m_ = src
                return kv_buf_ap("k", ph, m_)

            def vap_of(src):
                if kind == "a":
                    return ar[:, VA + src * 128:VA + (src + 1) * 128]
                ph, m_ = src
                return kv_buf_ap("v", ph, m_)

            vs_marks = [0] * n
            v_avlast = 0
            for ii in range(n):
                src, msk = tiles[ii]
                sb = 1024 + (ii % 4) * 512
                W("tensor", sS, sbank_war[ii % 4])
                if kind == "a":
                    W("tensor", sS, v_qsrc)
                else:
                    W("tensor", dQ, max(kv_marks[("k", 0)], kv_marks[("k", 1)]))
                vs_marks[ii] = mm(sb, 512, kap_of(src), qap, True, True)
                if ii % 2 == 1 or ii == n - 1:
                    lo = ii - (ii % 2)
                    wdt = 512 * (ii - lo + 1)
                    pap = ar[:, psl + (lo % 8) * 512:psl + (lo % 8) * 512 + wdt]
                    W("scalar", sT, vs_marks[ii])
                    W("scalar", sT, max(pslot_war[lo % 8], pslot_war[ii % 8]))
                    W("scalar", sV, vC)
                    ve = op("scalar", sS, lambda eng, p=pap, sb=1024 + (lo % 4) * 512,
                            wdt=wdt, h=h:
                            eng.activation(p, psum[:, sb:sb + wdt], AF.Exp,
                                           bias=negC[:, h:h + 1], scale=SCALE))
                    for jj in range(lo, ii + 1):
                        sbank_war[jj % 4] = ve
                    for jj in range(lo, ii + 1):
                        src2, msk2 = tiles[jj]
                        pap1 = ar[:, psl + (jj % 8) * 512:psl + (jj % 8) * 512 + 512]
                        if msk2 is not None:
                            W("vector", sS, ve)
                            vmk = op("vector", sV, lambda eng, p=pap1, r=msk2:
                                     eng.tensor_tensor(
                                         p, p, masks[:, r * 512:(r + 1) * 512],
                                         ALU.mult))
                            W("tensor", sV, vmk)
                        else:
                            W("tensor", sS, ve)
                        st = (jj == 0)
                        if st:
                            bank_wait("tensor", 512)
                            bank_wait("tensor", 0)
                        last = (jj == n - 1)
                        mm(512, 512, onesm[:], pap1, st, last, inc=False)
                        if kind == "b":
                            W("tensor", dS, max(kv_marks[("v", 0)],
                                                kv_marks[("v", 1)]))
                        vav = mm(0, 512, vap_of(src2), pap1, st, last, inc=True)
                        pslot_war[jj % 8] = vav
                        if last:
                            v_avlast = vav
            W("vector", sT, v_avlast)
            op("vector", sV, lambda eng: eng.reciprocal(
                dscr[:, 0:512], psum[:, 512:1024]), inc=False)
            b.drain("vector")
            cdst = ar[:, qbase + qoff:qbase + qoff + 512]
            vnorm = op("vector", sV, lambda eng, cd=cdst:
                       eng.tensor_tensor(cd, psum[:, 0:512], dscr[:, 0:512],
                                         ALU.mult))
            war_bank[0] = (sV, vnorm)
            war_bank[512] = (sV, vnorm)
            return vnorm

        def kv_buf_ap(which, ph, m_):
            base = KVB + (0 if which == "k" else 4096)
            return ar[:, base + ph * 2048 + m_ * 128:
                      base + ph * 2048 + (m_ + 1) * 128]

        war_wout = [0]

        def out_proj(wdram_fn, brow_idx, tiles_, qstride, tcol_fn, vctx,
                     vx_extra=None, wready=0, load_q="sync"):
            war_wout[0] = max(war_wout[0], sT.n)
            lsem = dQ if load_q == "sync" else dG
            vload_b = dma("scalar", dS, b1row[0:1, 256:2304],
                          brows_in[brow_idx:brow_idx + 1, :])
            for n_ in range(4):
                wsl_ap = ar[:, WOUT:WOUT + 8192]
                W(load_q, sT, war_wout[0])
                W(load_q, dS, wready)
                vw = dma(load_q, lsem, wsl_ap.rearrange("p (c e) -> p c e", e=512),
                         wdram_fn(n_))
                for ji, ti in enumerate(tiles_):
                    pc = (ji % 4) * 512
                    W("tensor", lsem, vw)
                    W("tensor", sV, vctx)
                    W("tensor", dS, vload_b)
                    bank_wait("tensor", pc)
                    for c in range(16):
                        mm(pc, 512, ar[:, QH + c * qstride + tcol_fn(ti) * 128:
                                       QH + c * qstride + (tcol_fn(ti) + 1) * 128],
                           wsl_ap[:, c * 512:(c + 1) * 512],
                           start=(c == 0), stop=False, inc=False)
                    vb = biasmm(pc, 512,
                                b1row[0:1, 256 + n_ * 512:768 + n_ * 512])
                    W("vector", sT, vb)
                    if vx_extra is not None:
                        W("vector", vx_extra[0], vx_extra[1])
                    xsl = x_sb[:, ti * D + n_ * 512:ti * D + (n_ + 1) * 512]
                    vep = op("vector", sV, lambda eng, xs=xsl, pc=pc:
                             eng.tensor_tensor(xs, xs, psum[:, pc:pc + 512],
                                               ALU.add))
                    war_bank[pc] = (sV, vep)
                war_wout[0] = sT.n
            return sV.n

        def s1_out(hh, vctx):
            return out_proj(
                lambda n_: outwT_d.rearrange("(c dd) e -> dd c e", dd=128)
                [:, :, n_ * 512:(n_ + 1) * 512],
                0, list(range(8 * hh, 8 * hh + 8)), 1024,
                lambda ti: ti - 8 * hh, vctx, wready=v_outwT)

        fp16_war = [0]

        def ffn_mm(hh, vxt):
            war_wout[0] = max(war_wout[0], sT.n)
            vload_b = dma("scalar", dS, b1row[0:1, 256:2304],
                          brows_in[1:2, :])
            for n_ in range(4):
                wsl_ap = ar[:, WOUT:WOUT + 8192]
                W("sync", sT, war_wout[0])
                W("sync", dS, v_ffnwT[0])
                vw = dma("sync", dQ, wsl_ap.rearrange("p (c e) -> p c e", e=512),
                         ffnwT_d.rearrange("(c dd) e -> dd c e", dd=128)
                         [:, :, n_ * 512:(n_ + 1) * 512])
                for ji in range(8):
                    ti = 8 * hh + ji
                    pc = (ji % 4) * 512
                    W("tensor", dQ, vw)
                    W("tensor", sV, vxt)
                    W("tensor", dS, vload_b)
                    bank_wait("tensor", pc)
                    for c in range(16):
                        mm(pc, 512, ar[:, XT + c * 1024 + ji * 128:
                                       XT + c * 1024 + (ji + 1) * 128],
                           wsl_ap[:, c * 512:(c + 1) * 512],
                           start=(c == 0), stop=False, inc=False)
                    vb = biasmm(pc, 512,
                                b1row[0:1, 256 + n_ * 512:768 + n_ * 512])
                    W("scalar", sT, vb)
                    W("scalar", sV, fp16_war[0])
                    vf = op("scalar", sS, lambda eng, pc=pc: eng.copy(
                        fp16t[:, 0:512], psum[:, pc:pc + 512]))
                    W("vector", sS, vf)
                    xsl = x_sb[:, ti * D + n_ * 512:ti * D + (n_ + 1) * 512]
                    vep = op("vector", sV, lambda eng, xs=xsl: eng.tensor_tensor(
                        xs, xs, fp16t[:, 0:512], ALU.add))
                    fp16_war[0] = vep
                    war_bank[pc] = (sV, vep)
                war_wout[0] = sT.n
            return sV.n

        quant_war = {"xh": [0, 0], "i16": 0}

        def quant_tile(src, dst, sl):
            W("scalar", sV, quant_war["xh"][sl % 2])
            vn = layernorm_tile(src, ar[:, SM3 + (sl % 2) * D:
                                        SM3 + (sl % 2) * D + D], sl)
            xh = ar[:, SM3 + (sl % 2) * D:SM3 + (sl % 2) * D + D]
            W("vector", sS, vn)
            vam = op("vector", sV, lambda eng, x=xh, sl=sl: eng.tensor_reduce(
                sc1[:, 56 + sl:57 + sl], x, axis=mybir.AxisListType.X,
                op=ALU.max, apply_absolute_value=True))
            b.drain("vector")
            op("vector", sV, lambda eng, sl=sl: eng.tensor_scalar(
                sc1[:, 56 + sl:57 + sl], sc1[:, 56 + sl:57 + sl], 1e-7, 0.0,
                ALU.max, ALU.bypass), inc=False)
            b.drain("vector")
            op("vector", sV, lambda eng, sl=sl: eng.reciprocal(
                sc1[:, 48 + sl:49 + sl], sc1[:, 56 + sl:57 + sl]), inc=False)
            b.drain("vector")
            vsq_ = op("vector", sV, lambda eng, sl=sl: eng.tensor_scalar(
                sc1[:, 48 + sl:49 + sl], sc1[:, 48 + sl:49 + sl], 127.0, 0.0,
                ALU.mult, ALU.bypass))
            b.drain("vector")
            vnis = op("vector", sV, lambda eng, sl=sl: eng.tensor_scalar(
                sc1[:, 56 + sl:57 + sl], sc1[:, 56 + sl:57 + sl],
                float(-1.0 / 127.0), 0.0, ALU.mult, ALU.bypass))
            vq = 0
            for hb in range(2):
                xhh = xh[:, hb * 1024:(hb + 1) * 1024]
                dsth = dst[:, hb * 1024:(hb + 1) * 1024]
                W("scalar", sV, vsq_ if hb == 0 else vq)
                W("scalar", sV, quant_war["i16"])
                vi = op("scalar", sS, lambda eng, x=xhh, sl=sl: eng.activation(
                    i16t[:], x, AF.Identity, scale=sc1[:, 48 + sl:49 + sl]))
                W("vector", sS, vi)
                op("vector", sV, lambda eng, x=xhh, sl=sl:
                   eng.scalar_tensor_tensor(
                       dscr[:], x, sc1[:, 48 + sl:49 + sl], i16t[:],
                       ALU.mult, ALU.subtract), inc=False)
                b.drain("vector")
                op("vector", sV, lambda eng: eng.scalar_tensor_tensor(
                    sqs[:, 0:1024], dscr[:], 0.5, i16t[:], ALU.is_ge, ALU.add),
                   inc=False)
                b.drain("vector")
                op("vector", sV, lambda eng: eng.scalar_tensor_tensor(
                    i16t[:], dscr[:], -0.5, sqs[:, 0:1024], ALU.is_le,
                    ALU.subtract), inc=False)
                b.drain("vector")
                vq = op("vector", sV, lambda eng, d=dsth, sl=sl:
                        eng.tensor_scalar(d, i16t[:], sc1[:, 56 + sl:57 + sl],
                                          0.0, ALU.mult, ALU.bypass))
                quant_war["i16"] = vq
            quant_war["xh"][sl % 2] = vq
            return vq

        xt_war = [0]

        xq_war = [0]

        def quant_half(hh):
            XQ = 0
            XH8 = 0
            W("vector", sT, xq_war[0])
            W("scalar", sT, xq_war[0])
            for j in range(8):
                src_ = x_sb[:, (8 * hh + j) * D:(8 * hh + j + 1) * D]
                for c in range(4):
                    op("vector", sV, lambda eng, s=src_, j=j, c=c: eng.bn_stats(
                        sc2[:, 24 * j + 6 * c:24 * j + 6 * c + 6],
                        s[:, c * 512:(c + 1) * 512]), inc=False)
            b.drain("vector")
            for j in range(8):
                op("vector", sV, lambda eng, j=j: eng.bn_aggr(
                    sc1[:, 2 * j:2 * j + 2], sc2[:, 24 * j:24 * j + 24]),
                   inc=False)
            b.drain("vector")
            vst = op("vector", sV, lambda eng: eng.memset(sc1[:, 126:127], 0.0))
            W("scalar", sV, vst)
            vsd = op("scalar", sS, lambda eng: eng.activation(
                sc1[:, 16:24], sc1[:, 1:17].rearrange("p (a b) -> p (a b)")
                [:, 0:16:2] if False else sc1[:, 1:16:2], AF.Sqrt,
                bias=consts[:, 34:35]))
            W("vector", sS, vsd)
            op("vector", sV, lambda eng: eng.reciprocal(
                sc1[:, 24:32], sc1[:, 16:24]), inc=False)
            b.drain("vector")
            op("vector", sV, lambda eng: eng.tensor_tensor(
                sc1[:, 32:40], sc1[:, 0:15:2], sc1[:, 24:32], ALU.mult),
               inc=False)
            b.drain("vector")
            vnm = op("vector", sV, lambda eng: eng.tensor_scalar(
                sc1[:, 32:40], sc1[:, 32:40], -1.0, 0.0, ALU.mult, ALU.bypass))
            W("scalar", sV, vnm)
            vnorm8 = 0
            for j in range(8):
                src_ = x_sb[:, (8 * hh + j) * D:(8 * hh + j + 1) * D]
                vnorm8 = op("scalar", sS, lambda eng, s=src_, j=j: eng.activation(
                    ar[:, XH8 + j * D:XH8 + (j + 1) * D], s, AF.Identity,
                    bias=sc1[:, 32 + j:33 + j], scale=sc1[:, 24 + j:25 + j]))
            W("vector", sS, vnorm8)
            for j in range(8):
                op("vector", sV, lambda eng, j=j: eng.tensor_reduce(
                    sc1[:, 56 + j:57 + j], ar[:, XH8 + j * D:XH8 + (j + 1) * D],
                    axis=mybir.AxisListType.X, op=ALU.max,
                    apply_absolute_value=True), inc=False)
            b.drain("vector")
            op("vector", sV, lambda eng: eng.tensor_scalar(
                sc1[:, 56:64], sc1[:, 56:64], 1e-7, 0.0, ALU.max, ALU.bypass),
               inc=False)
            b.drain("vector")
            op("vector", sV, lambda eng: eng.reciprocal(
                sc1[:, 48:56], sc1[:, 56:64]), inc=False)
            b.drain("vector")
            op("vector", sV, lambda eng: eng.tensor_scalar(
                sc1[:, 48:56], sc1[:, 48:56], 127.0, 0.0, ALU.mult, ALU.bypass),
               inc=False)
            b.drain("vector")
            vsc_ = op("vector", sV, lambda eng: eng.tensor_scalar(
                sc1[:, 56:64], sc1[:, 56:64], float(-1.0 / 127.0), 0.0,
                ALU.mult, ALU.bypass))
            vq = vsc_
            vi_prev = 0
            for j in range(8):
                xh = ar[:, XH8 + j * D:XH8 + (j + 1) * D]
                dst = xh
                for hb in range(2):
                    xhh = xh[:, hb * 1024:(hb + 1) * 1024]
                    dsth = dst[:, hb * 1024:(hb + 1) * 1024]
                    W("scalar", sV, vq)
                    vi = op("scalar", sS, lambda eng, x=xhh, j=j: eng.activation(
                        i16t[:], x, AF.Identity, scale=sc1[:, 48 + j:49 + j]))
                    W("vector", sS, vi)
                    op("vector", sV, lambda eng, x=xhh, j=j:
                       eng.scalar_tensor_tensor(
                           dscr[:], x, sc1[:, 48 + j:49 + j], i16t[:],
                           ALU.mult, ALU.subtract), inc=False)
                    b.drain("vector")
                    op("vector", sV, lambda eng: eng.scalar_tensor_tensor(
                        sqs[:, 0:1024], dscr[:], 0.5, i16t[:], ALU.is_ge,
                        ALU.add), inc=False)
                    b.drain("vector")
                    op("vector", sV, lambda eng: eng.scalar_tensor_tensor(
                        i16t[:], dscr[:], -0.5, sqs[:, 0:1024], ALU.is_le,
                        ALU.subtract), inc=False)
                    b.drain("vector")
                    vq = op("vector", sV, lambda eng, d=dsth, j=j:
                            eng.tensor_scalar(d, i16t[:],
                                              sc1[:, 56 + j:57 + j], 0.0,
                                              ALU.mult, ALU.bypass))
            return vq

        def ffn_sublayer():
            emit_late_preps()
            xq_war[0] = max(xq_war[0], sT.n)
            vdone = 0
            vq = [0, 0]
            vq[0] = quant_half(0)
            for hh in range(2):
                vxt = 0
                for j in range(8):
                    vxt = tile_to_xT(ar[:, j * D:(j + 1) * D], j,
                                     [(sV, vq[hh])], war_v=xt_war[0])
                xq_war[0] = sT.n
                if hh == 0:
                    vq[1] = quant_half(1)
                vdone = ffn_mm(hh, vxt)
                xt_war[0] = sT.n
            return vdone

        def s3_sublayer():
            W("gpsimd", sS, sS.n)
            W("gpsimd", sV, sV.n)
            kq_war = [sT.n]
            kq_war2 = [sT.n, sT.n]
            s3_w_done = [0]
            kst_war = [0, 0]
            knorm_done = 0
            for hh in range(2):
                vxt = 0
                for j in range(8):
                    ti = 8 * hh + j
                    sl = j % 2
                    b.drain("vector")
                    xout = ar[:, SM3 + sl * D:SM3 + sl * D + D]
                    W("vector", sT, xho_war[sl])
                    W("scalar", sT, xho_war[sl])
                    vn = layernorm_tile(x_sb[:, ti * D:(ti + 1) * D], xout, j)
                    vxt = tile_to_xT(xout, j, [(sS, vn)], war_v=xt_war[0])
                    xho_war[sl] = sT.n
                xt_war[0] = 0
                for which in ("k",):
                    for h in range(H):
                        wc = ar[:, 8192 + (h % 2) * 2048:
                                8192 + (h % 2) * 2048 + 2048]
                        W("gpsimd", sT, kq_war2[h % 2])
                        vw = dma("gpsimd", dG,
                                 wc.rearrange("p (c e) -> p c e", e=128),
                                 wk_in[h].rearrange("(c dd) e -> dd c e", dd=128))
                        pc = (h % 2) * 1024
                        W("tensor", dG, vw)
                        W("tensor", sV, vxt)
                        bank_wait("tensor", pc)
                        bank_wait("tensor", pc + 512)
                        for d in range(16):
                            xc = ar[:, XT + d * 1024:XT + (d + 1) * 1024]
                            mm(pc, 512, wc[:, d * 128:(d + 1) * 128],
                               xc[:, 0:512], start=(d == 0), stop=(d == 15),
                               inc=False)
                            vmm = mm(pc + 512, 512, wc[:, d * 128:(d + 1) * 128],
                                     xc[:, 512:1024], start=(d == 0),
                                     stop=(d == 15), inc=(d == 15))
                        kq_war2[h % 2] = vmm
                        W("scalar", sT, vmm)
                        if True:
                            sl2 = 0
                            kst = ar[:, KST:KST + 1024]
                            W("scalar", dQ, dQ.n)
                            vev = op("scalar", sS, lambda eng, ks=kst, pc=pc, h=h:
                                     eng.activation(ks, psum[:, pc:pc + 1024],
                                                    AF.Identity,
                                                    bias=bk_sb[:, h:h + 1]))
                            war_bank[pc] = (sS, vev)
                            war_bank[pc + 512] = (sS, vev)
                            W("scalar", sT, sqs_war[0])
                            vsq2 = op("scalar", sS, lambda eng, ks=kst:
                                      eng.activation(sqs[:, 0:1024], ks,
                                                     AF.Square))
                            for cch in range(2):
                                W("tensor", sS, vsq2)
                                bank_wait("tensor", 2048)
                                vmm2 = mm(2048, 512, onesm[:],
                                          sqs[:, cch * 512:(cch + 1) * 512],
                                          True, True)
                                sqs_war[0] = vmm2
                                W("vector", sT, vmm2)
                                b.drain("vector")
                                vrd = op("vector", sV, lambda eng, cch=cch:
                                         eng.reduce_max(sc1[:, 64 + cch:65 + cch],
                                                        psum[:, 2048:2560],
                                                        axis=mybir.AxisListType.X))
                                war_bank[2048] = (sV, vrd)
                            b.drain("vector")
                            if hh == 0:
                                vkn = op("vector", sV, lambda eng, h=h:
                                         eng.tensor_reduce(
                                             sc1[:, 96 + h:97 + h],
                                             sc1[:, 64:66],
                                             axis=mybir.AxisListType.X,
                                             op=ALU.max))
                            else:
                                op("vector", sV, lambda eng, h=h:
                                   eng.tensor_reduce(
                                       sc1[:, 64:65], sc1[:, 64:66],
                                       axis=mybir.AxisListType.X, op=ALU.max),
                                   inc=False)
                                b.drain("vector")
                                vkn = op("vector", sV, lambda eng, h=h:
                                         eng.tensor_tensor(
                                             sc1[:, 96 + h:97 + h],
                                             sc1[:, 96 + h:97 + h],
                                             sc1[:, 64:65], ALU.max))
                            knorm_done = vkn
                            W("sync", sS, vev)
                            kst_war[sl2] = dma("sync", dQ,
                                               cc_kv_in[h * 128:(h + 1) * 128,
                                                        hh * 1024:(hh + 1) * 1024],
                                               kst)
                vload_bv = dma("scalar", dS, b1row[0:1, 256:2304],
                               brows_in[2:3, :])
                for hq in range(4):
                    wvb = ar[:, 0:8192]
                    W("gpsimd", sT, kq_war[0])
                    for h4 in range(4):
                        vw = dma("gpsimd", dG,
                                 wvb.rearrange("p (c e) -> p c e", e=512)
                                 [:, :, h4 * 128:(h4 + 1) * 128],
                                 wv_in[hq * 4 + h4]
                                 .rearrange("(c dd) e -> dd c e", dd=128))
                    for ttg in range(2):
                        for tt in range(4):
                            bank_wait("tensor", tt * 512)
                        W("tensor", dG, vw)
                        W("tensor", sV, vxt)
                        W("tensor", dS, vload_bv)
                        for d in range(16):
                            xc = ar[:, XT + d * 1024:XT + (d + 1) * 1024]
                            for tt in range(4):
                                j2 = ttg * 4 + tt
                                vmm = mm(tt * 512, 512,
                                         xc[:, j2 * 128:(j2 + 1) * 128],
                                         wvb[:, d * 512:(d + 1) * 512],
                                         start=(d == 0), stop=False,
                                         inc=(tt == 3))
                        kq_war[0] = sT.n
                        for tt in range(4):
                            ti = 8 * hh + ttg * 4 + tt
                            pc = tt * 512
                            vb = biasmm(pc, 512,
                                        b1row[0:1, 256 + hq * 512:
                                              768 + hq * 512])
                            W("vector", sT, vb)
                            vst_ap = ar[:, KST + 1024:KST + 1536]
                            W("vector", dS, dS.n)
                            vev = op("vector", sV, lambda eng, v=vst_ap, pc=pc:
                                     eng.tensor_copy(v, psum[:, pc:pc + 512]))
                            war_bank[pc] = (sV, vev)
                            W("scalar", sV, vev)
                            kst_war[1] = dma(
                                "scalar", dS,
                                cc_kv_in[2048 + ti * 128:2048 + (ti + 1) * 128,
                                         hq * 512:(hq + 1) * 512],
                                vst_ap)
                xt_war[0] = sT.n
            s3_w_done[0] = sT.n
            W("sync", sV, knorm_done)
            v_nrm_w = dma("sync", dQ, cc_nrm_in[:], sc1[0:1, 96:112])
            W("gpsimd", dQ, dQ.n)
            W("gpsimd", dS, dS.n)
            cC.n += 1
            b.emit("gpsimd", lambda eng, s=cC.h: eng.collective_compute(
                "AllGather", ALU.bypass, replica_groups=RG,
                ins=[cc_kv_in[:]], outs=[cc_kv_out[:]]).then_inc(s, 1))
            v_ag_kv = cC.n
            cC.n += 1
            b.emit("gpsimd", lambda eng, s=cC.h: eng.collective_compute(
                "AllGather", ALU.bypass, replica_groups=RG,
                ins=[cc_nrm_in[:]], outs=[cc_nrm_out[:]]).then_inc(s, 1))
            v_ag_nrm = cC.n

            for hh in range(2):
                vxt = 0
                for j in range(8):
                    ti = 8 * hh + j
                    sl = j % 2
                    b.drain("vector")
                    xout = ar[:, SM3 + sl * D:SM3 + sl * D + D]
                    W("vector", sT, xho_war[sl])
                    W("scalar", sT, xho_war[sl])
                    vn = layernorm_tile(x_sb[:, ti * D:(ti + 1) * D], xout, j)
                    vxt = tile_to_xT(xout, j, [(sS, vn)], war_v=xt_war[0])
                    xho_war[sl] = sT.n
                xt_war[0] = 0
                for h in range(H):
                    wc = ar[:, WC3:WC3 + 2048]
                    W("gpsimd", sT, kq_war[0])
                    vw = dma("gpsimd", dG,
                             wc.rearrange("p (c e) -> p c e", e=128),
                             wq_in[h].rearrange("(c dd) e -> dd c e", dd=128))
                    pc = (h % 2) * 1024
                    W("tensor", dG, vw)
                    W("tensor", sV, vxt)
                    bank_wait("tensor", pc)
                    bank_wait("tensor", pc + 512)
                    for d in range(16):
                        xc = ar[:, XT + d * 1024:XT + (d + 1) * 1024]
                        mm(pc, 512, wc[:, d * 128:(d + 1) * 128],
                           xc[:, 0:512], start=(d == 0), stop=(d == 15),
                           inc=False)
                        vmm = mm(pc + 512, 512, wc[:, d * 128:(d + 1) * 128],
                                 xc[:, 512:1024], start=(d == 0),
                                 stop=(d == 15), inc=(d == 15))
                    kq_war[0] = vmm
                    W("scalar", sT, max(vmm, s3_w_done[0]))
                    qdst = ar[:, QH + h * 2048 + hh * 1024:
                              QH + h * 2048 + hh * 1024 + 1024]
                    vev = op("scalar", sS, lambda eng, dd=qdst, pc=pc, h=h:
                             eng.activation(dd, psum[:, pc:pc + 1024],
                                            AF.Identity,
                                            bias=bq_sb[:, h:h + 1]))
                    war_bank[pc] = (sS, vev)
                    war_bank[pc + 512] = (sS, vev)
                xt_war[0] = sT.n
            v_qfull = sS.n

            for h in range(H):
                qsrc = ar[:, QH + h * 2048:QH + (h + 1) * 2048]
                normmax_to(qsrc, 2048, 2048, 32 + h, h % 2, [(sS, v_qfull)])
            W("sync", cC, v_ag_nrm)
            vn0 = dma("sync", dQ, nrm2[0:1, 0:16], cc_nrm_out[0:1, :])
            vn1 = dma("sync", dQ, nrm2[0:1, 16:32], cc_nrm_out[1:2, :])
            W("vector", dQ, vn1)
            b.drain("vector")
            vnk = op("vector", sV, lambda eng: eng.tensor_tensor(
                nrm2[0:1, 0:16], nrm2[0:1, 0:16], nrm2[0:1, 16:32], ALU.max))
            W("tensor", sV, vnk)
            bank_wait("tensor", 2048)
            vbb = mm(2048, 16, b1row[0:1, 0:128], nrm2[0:1, 0:16], True, True)
            W("vector", sT, vbb)
            b.drain("vector")
            vkb = op("vector", sV, lambda eng: eng.tensor_copy(
                sc1[:, 96:112], psum[:, 2048:2064]))
            war_bank[2048] = (sV, vkb)
            b.drain("vector")
            op("vector", sV, lambda eng: eng.scalar_tensor_tensor(
                sc1[:, 48:64], sc1[:, 32:48], 1.0, sc1[:, 96:112],
                ALU.mult, ALU.mult), inc=False)
            b.drain("vector")
            vpr = op("vector", sV, lambda eng: eng.tensor_scalar(
                sc1[:, 48:64], sc1[:, 48:64], 1e-20, 0.0, ALU.max, ALU.bypass))
            W("scalar", sV, vpr)
            vsq4 = op("scalar", sS, lambda eng: eng.activation(
                sc1[:, 80:96], sc1[:, 48:64], AF.Sqrt))
            W("vector", sS, vsq4)
            b.drain("vector")
            vC_all = op("vector", sV, lambda eng: eng.tensor_scalar(
                negC[:, 0:16], sc1[:, 80:96], -SCALE, 0.0,
                ALU.mult, ALU.bypass))

            kvload_war = [sT.n]
            attn_war["p"] = [max(v, sT.n) for v in attn_war["p"]]
            vctx = 0
            for h in range(H):
                for qc in range(4):
                    ext = 4 * qc + 4
                    kv_marks = {}
                    W("sync", cC, v_ag_kv)
                    W("scalar", cC, v_ag_kv)
                    W("sync", sT, kvload_war[0])
                    W("scalar", sT, kvload_war[0])
                    for ph in range(2):
                        kv_marks[("k", ph)] = dma(
                            "sync", dQ, ar[:, KVB + ph * 2048:
                                           KVB + ph * 2048 + ext * 128],
                            cc_kv_out[ph * 4096 + h * 128:
                                      ph * 4096 + (h + 1) * 128, 0:ext * 128])
                        kv_marks[("v", ph)] = dma(
                            "scalar", dS,
                            ar[:, KVB + 4096 + ph * 2048:
                               KVB + 4096 + ph * 2048 + ext * 128],
                            cc_kv_out[ph * 4096 + 2048:ph * 4096 + 2048
                                      + ext * 128, h * 128:(h + 1) * 128]
                            .rearrange("(m tt) e -> tt m e", tt=128))
                    vctx = run_attn_inner("b", 0, h, qc, vC_all, v_qfull,
                                          kv_marks=kv_marks)
                    kvload_war[0] = sT.n
            vdone = out_proj(
                lambda n_: wo_in.rearrange("h e d -> e h d")
                [:, :, n_ * 512:(n_ + 1) * 512],
                3, list(range(16)), 2048, lambda ti: ti, vctx,
                load_q="gpsimd")
            return vdone

        for gq in range(2):
            v_qh = s1_gq(gq)
        ka_ready[0] = sS.n
        vctx0 = s1_attention(0, sS.n)
        s1_out(0, vctx0)
        for gq in range(2, 4):
            v_qh = s1_gq(gq)
        ka_ready[1] = sS.n
        vctx1 = s1_attention(1, ka_ready[1], filler=emit_ffnwT_part)
        v_ffnwT[0] = ckpt("scalar", dS)
        v_s1 = s1_out(1, vctx1)

        if dbg_stage != 1:
            W("sync", sV, v_s1)
            for r in range(8):
                dma("sync", dQ, masks[:, r * 512:(r + 1) * 512], maskb_in[r])
            v_s2 = ffn_sublayer()
            if dbg_stage != 2:
                v_s3 = s3_sublayer()
                if dbg_stage != 3:
                    v_s4 = ffn_sublayer()

        b.drain("vector")
        vfin = op("vector", sV, lambda eng: eng.memset(sc1[:, 127:128], 0.0))
        W("gpsimd", sV, vfin)
        dO = mksem("dO")
        for i in range(NT):
            b.emit("gpsimd", lambda eng, i=i, s=dO.h: eng.dma_start(
                out=y[i * 128:(i + 1) * 128, :],
                in_=x_sb[:, i * D:(i + 1) * D]).then_inc(s, 16))
            dO.n += 16
        W("gpsimd", dO, dO.n)

        with nc.Block() as block:
            b.run(block)

    return nc


_NC_CACHE = {}


def _get_nc():
    key = DBG_STAGE
    if key not in _NC_CACHE:
        _NC_CACHE[key] = build_nc(DBG_STAGE)
    return _NC_CACHE[key]


def _build_masks(p):
    trikc = (np.arange(128)[:, None] <= np.arange(128)[None, :]).astype(np.float32)
    maskq = np.zeros((8, 128, 512), np.float32)
    for r in range(8):
        for s_ in range(4):
            js_even = (r % 2 == 0)
            gk = r + p if js_even else r - p
            g = 2 * s_ + p
            blk = maskq[r][:, s_ * 128:(s_ + 1) * 128]
            if gk < g:
                blk[:] = 1.0
            elif gk == g:
                blk[:] = trikc
    maskb = np.zeros((8, 128, 512), np.float32)
    for ph in range(2):
        for r in range(4):
            for s_ in range(4):
                gk = 2 * r + ph
                g = 2 * s_ + p
                blk = maskb[ph * 4 + r][:, s_ * 128:(s_ + 1) * 128]
                if gk < g:
                    blk[:] = 1.0
                elif gk == g:
                    blk[:] = trikc
    return (maskq.astype(ml_dtypes.bfloat16), maskb.astype(ml_dtypes.bfloat16))


def kernel(L0_attn_a_x, L0_attn_a_Wqkv_w, L0_attn_a_Wqkv_b, L0_attn_a_out_w,
           L0_attn_a_out_b, L0_attn_b_W_Q, L0_attn_b_W_K, L0_attn_b_W_V,
           L0_attn_b_W_O, L0_attn_b_b_Q, L0_attn_b_b_K, L0_attn_b_b_V,
           L0_attn_b_b_O, L0_ffn_weight, L0_ffn_w_scale, L0_ffn_bias):
    nc = _get_nc()
    x = np.ascontiguousarray(L0_attn_a_x)
    wqkv = np.ascontiguousarray(L0_attn_a_Wqkv_w)
    bqkv = np.ascontiguousarray(L0_attn_a_Wqkv_b)
    outw = np.ascontiguousarray(L0_attn_a_out_w)
    bouta = np.ascontiguousarray(L0_attn_a_out_b)
    wq = np.ascontiguousarray(L0_attn_b_W_Q)
    wk = np.ascontiguousarray(L0_attn_b_W_K)
    wv = np.ascontiguousarray(L0_attn_b_W_V)
    wo = np.ascontiguousarray(L0_attn_b_W_O)
    bQ = np.ascontiguousarray(L0_attn_b_b_Q)
    bK = np.ascontiguousarray(L0_attn_b_b_K)
    bV = np.ascontiguousarray(L0_attn_b_b_V)
    bO = np.ascontiguousarray(L0_attn_b_b_O)
    fw = np.ascontiguousarray(L0_ffn_weight)
    ws = np.ascontiguousarray(L0_ffn_w_scale)
    fb = np.ascontiguousarray(L0_ffn_bias)

    bqkv_f = bqkv.reshape(NEC, 128).T.copy()
    wscale_t = ws[:, 0].reshape(16, 128).T.copy()
    bq_t = bQ.T.copy()
    bk_t = bK.T.copy()
    brows = np.stack([bouta, fb, bV.reshape(-1), bO]).astype(np.float32)
    bva = bqkv[2176:2304].reshape(1, 128).copy()
    ones512 = np.ones((1, 128), np.float32)
    epscol = np.full((128, 1), 1e-5, np.float32)
    ident = np.eye(128, dtype=ml_dtypes.bfloat16)
    onesm = np.ones((128, 128), dtype=ml_dtypes.bfloat16)

    in_maps = []
    for c in range(8):
        bb, p = c // 2, c % 2
        xb = x[bb]
        tiles = xb.reshape(32, 128, D)
        own = tiles[p::2]
        x0own = own.reshape(2048, D).copy()
        order = []
        for i in range(16):
            order.append(2 * i + p)
            order.append(2 * i + (1 - p))
        x0stg = tiles[order].reshape(S, D).copy()
        maskq, maskb = _build_masks(p)
        in_maps.append({
            "x0own": x0own, "x0stg": x0stg, "wqkv": wqkv, "bqkv_f": bqkv_f,
            "outw": outw, "ffnw": fw, "wscale_t": wscale_t,
            "wq": wq, "wk": wk, "wv": wv, "wo": wo,
            "bq_t": bq_t, "bk_t": bk_t, "brows": brows, "bva": bva,
            "ones_f32": ones512, "epscol": epscol, "ident": ident, "onesm": onesm,
            "maskq": maskq, "maskb": maskb,
        })
    res = run_bass_kernel_spmd(nc, in_maps, core_ids=list(range(8)))
    out = np.empty((4, S, D), np.float32)
    for c in range(8):
        bb, p = c // 2, c % 2
        yt = res.results[c]["y"].reshape(16, 128, D)
        ot = out[bb].reshape(32, 128, D)
        ot[p::2] = yt
    return out



# version 11
# speedup vs baseline: 1.0509x; 1.0509x over previous
import numpy as np
import ml_dtypes
import concourse.bass as bass
import concourse.mybir as mybir
from concourse.bass_utils import run_bass_kernel_spmd

F32 = mybir.dt.float32
BF16 = mybir.dt.bfloat16
F16 = mybir.dt.float16
I8 = mybir.dt.int8
I16 = mybir.dt.int16
AF = mybir.ActivationFunctionType
ALU = mybir.AluOpType

D = 2048
S = 4096
H = 16
NT = 16
E_QKV = 2304
NEC = 18
SCALE = float(1.0 / np.sqrt(128.0))

ENGS = ("sync", "scalar", "vector", "tensor", "gpsimd")
DBG_STAGE = 0


class Sem:
    def __init__(self, h):
        self.h = h
        self.n = 0
        self.wmax = 0
        self.guarded = 0
        self.queue = None


class Bld:
    def __init__(self):
        self.th = {e: [] for e in ENGS}

    def emit(self, e, fn):
        self.th[e].append(fn)

    def wait(self, e, sem, val):
        if val > 0:
            sem.wmax = max(sem.wmax, val)
            self.th[e].append(lambda eng, s=sem.h, v=val: eng.wait_ge(s, v))

    def drain(self, e):
        self.th[e].append(lambda eng: eng.drain())

    def run(self, block):
        for e in ENGS:
            def mk(ths):
                def body(eng):
                    for f in ths:
                        f(eng)
                return body
            getattr(block, e)(mk(self.th[e]))


def attn_tiles(kind, qc):
    out = []
    if kind == "a":
        for js in range(8 * qc + 8):
            msk = js - 8 * qc if js >= 8 * qc else None
            out.append((js, msk))
    else:
        for ph in range(2):
            for m_ in range(4 * qc + 4):
                msk = ph * 4 + (m_ - 4 * qc) if m_ >= 4 * qc else None
                out.append(((ph, m_), msk))
    return out


def build_nc(dbg_stage=0):
    nc = bass.Bass("TRN2", target_bir_lowering=False)

    def din(name, shape, dt):
        return nc.dram_tensor(name, shape, dt, kind="ExternalInput").ap()

    x0own = din("x0own", [2048, D], F32)
    x0stg = din("x0stg", [S, D], F32)
    wqkv = din("wqkv", [E_QKV, D], F32)
    bqkv_f = din("bqkv_f", [128, NEC], F32)
    outw = din("outw", [D, D], F32)
    ffnw = din("ffnw", [D, D], I8)
    wscale_t = din("wscale_t", [128, 16], F32)
    wq_in = din("wq", [H, D, 128], F32)
    wk_in = din("wk", [H, D, 128], F32)
    wv_in = din("wv", [H, D, 128], F32)
    wo_in = din("wo", [H, 128, D], F32)
    bq_t = din("bq_t", [128, H], F32)
    bk_t = din("bk_t", [128, H], F32)
    brows_in = din("brows", [4, D], F32)
    bva_in = din("bva", [1, 128], F32)
    ones_in = din("ones_f32", [1, 128], F32)
    eps_in = din("epscol", [128, 1], F32)
    ident_in = din("ident", [128, 128], BF16)
    onesm_in = din("onesm", [128, 128], BF16)
    maskq_in = din("maskq", [8, 128, 512], BF16)
    maskb_in = din("maskb", [8, 128, 512], BF16)

    y = nc.dram_tensor("y", [2048, D], F32, kind="ExternalOutput").ap()

    cc_kv_in = nc.dram_tensor("cc_kv_in", [4096, D], BF16, kind="Internal").ap()
    cc_kv_out = nc.dram_tensor("cc_kv_out", [8192, D], BF16, kind="Internal",
                               addr_space="Local").ap()
    cc_nrm_in = nc.dram_tensor("cc_nrm_in", [1, H], F32, kind="Internal").ap()
    cc_nrm_out = nc.dram_tensor("cc_nrm_out", [2, H], F32, kind="Internal",
                                addr_space="Local").ap()
    wqkvT_d = nc.dram_tensor("wqkvT_d", [D, E_QKV], BF16, kind="Internal").ap()
    outwT_d = nc.dram_tensor("outwT_d", [D, D], BF16, kind="Internal").ap()
    ffnwT_d = nc.dram_tensor("ffnwT_d", [D, D], BF16, kind="Internal").ap()
    wqb_d = nc.dram_tensor("wqb_d", [H * D, 128], BF16, kind="Internal").ap()
    wkb_d = nc.dram_tensor("wkb_d", [H * D, 128], BF16, kind="Internal").ap()
    wvg_d = nc.dram_tensor("wvg_d", [D, D], BF16, kind="Internal").ap()
    wob_d = nc.dram_tensor("wob_d", [H * 128, D], BF16, kind="Internal").ap()
    RG = [[0, 1], [2, 3], [4, 5], [6, 7]]

    import contextlib
    with contextlib.ExitStack() as stk:
        def sbuf(name, shape, dt):
            return stk.enter_context(nc.sbuf_tensor(name, shape, dt))

        def mksem(name):
            return Sem(stk.enter_context(nc.semaphore(name)))

        x_sb = sbuf("x_sb", [128, NT * D], BF16)
        masks = sbuf("masksb", [128, 8 * 512], BF16)
        ident = sbuf("identb", [128, 128], BF16)
        onesm = sbuf("onesmb", [128, 128], BF16)
        b1row = sbuf("b1row", [1, 2304], F32)
        bq_sb = sbuf("bq_sbb", [128, H], F32)
        bk_sb = sbuf("bk_sbb", [128, H], F32)
        consts = sbuf("constsb", [128, 40], F32)
        sc1 = sbuf("sc1b", [128, 128], F32)
        sc2 = sbuf("sc2b", [128, 192], F32)
        nrm2 = sbuf("nrm2b", [1, 32], F32)
        negC = sbuf("negCb", [128, H], F32)
        fp16t = sbuf("fp16tb", [128, 512], F16)
        i16t = sbuf("i16tb", [128, 1024], I16)
        i8st = sbuf("i8stb", [128, D], I8)
        dscr = sbuf("dscrb", [128, 1024], F32)
        sqs = sbuf("sqsb", [128, D], BF16)
        AR = 56832
        ar = sbuf("arena", [128, AR], BF16)
        QH = 0
        WCOL = 16384
        PSL = 16384
        WOUT = 16384
        XHO = 20480
        VA = 24576
        KA = 28672
        XT = 32768
        KVB = 32768
        PSL3 = 49152
        XHI = 49152
        SM3 = 49152
        WC3 = 53248
        KST = 55296

        psum = stk.enter_context(nc.psum_tensor("psumT", [128, 3072], F32))
        psT = stk.enter_context(nc.psum_tensor("psumB", [128, 2048], BF16))

        sT = mksem("sT")
        sV = mksem("sV")
        sS = mksem("sS")
        dQ = mksem("dQ")
        dS = mksem("dS2")
        dV = mksem("dV2")
        dG = mksem("dG2")
        cC = mksem("cC")

        b = Bld()

        def dma(q, qsem, out, in_):
            qsem.queue = q
            if qsem.wmax > qsem.guarded:
                g = qsem.wmax
                b.th[q].append(lambda eng, s=qsem.h, v=g: eng.wait_ge(s, v))
                qsem.guarded = g
            b.emit(q, lambda eng, o=out, i=in_, s=qsem.h:
                   eng.dma_start(out=o, in_=i).then_inc(s, 16))
            qsem.n += 16
            return qsem.n

        def op(e, esem, f, inc=True):
            if inc:
                esem.n += 1
                b.emit(e, lambda eng, f=f, s=esem.h: f(eng).then_inc(s, 1))
                return esem.n
            b.emit(e, lambda eng, f=f: f(eng))
            return esem.n

        W = b.wait

        def ckpt(qname, qsem):
            v = qsem.n
            if v > qsem.guarded:
                b.th[qname].append(lambda eng, s=qsem.h, vv=v: eng.wait_ge(s, vv))
                qsem.guarded = v
            qsem.wmax = max(qsem.wmax, v)
            return v

        def mm(col, w, lhsT, rhs, start, stop, inc=None):
            if inc is None:
                inc = stop
            return op("tensor", sT,
                      lambda eng, c=col, w=w, l=lhsT, r=rhs, st=start, sp=stop:
                      eng.matmul(psum[:, c:c + w], l, r, start=st, stop=sp),
                      inc=inc)

        def biasmm(col, w, rhs_bias, stop=True, inc=None):
            if inc is None:
                inc = stop
            return op("tensor", sT,
                      lambda eng, c=col, w=w, r=rhs_bias:
                      eng.matmul(psum[:, c:c + w], b1row[0:1, 0:128], r,
                                 start=False, stop=stop), inc=inc)

        dma("sync", dQ, ident[:], ident_in[:])
        dma("sync", dQ, onesm[:], onesm_in[:])
        dma("sync", dQ, b1row[0:1, 0:128], ones_in[:])
        for r in range(8):
            dma("sync", dQ, masks[:, r * 512:(r + 1) * 512], maskq_in[r])
        dma("sync", dQ, consts[:, 0:NEC], bqkv_f[:])
        dma("sync", dQ, consts[:, 18:34], wscale_t[:])
        dma("sync", dQ, bq_sb[:], bq_t[:])
        dma("sync", dQ, bk_sb[:], bk_t[:])
        dma("sync", dQ, consts[:, 34:35], eps_in[:])
        v_consts = ckpt("sync", dQ)
        for e_ in ("tensor", "vector", "scalar", "gpsimd"):
            W(e_, dQ, v_consts)
        dma("scalar", dS, b1row[0:1, 128:256], bva_in[:])
        v_bva = dS.n
        for i in range(NT):
            dma("gpsimd", dG, x_sb[:, i * D:(i + 1) * D],
                x0own[i * 128:(i + 1) * 128, :])
        v_xown = ckpt("gpsimd", dG)

        RAW = [0, 2304]
        EVA = [4608, 6912]
        war_raw = [0, 0]
        war_i8 = [0, 0]
        war_eva = [0, 0]
        war_bank = {}

        def bank_wait(e, col):
            if col in war_bank:
                s_, v_ = war_bank[col]
                W(e, s_, v_)

        def prep_chunk(src_row_ap, dst_ap, ncols_c, sl, load_q, load_sem,
                       scale_col=None, rawbase=None, evabase=None):
            rb_ = RAW[sl] if rawbase is None else rawbase[sl]
            eb_ = EVA[sl] if evabase is None else evabase[sl]
            raw = ar[:, rb_:rb_ + ncols_c * 128]
            eva = ar[:, eb_:eb_ + ncols_c * 128]
            if scale_col is None:
                W(load_q, sT, war_raw[sl])
                vld = dma(load_q, load_sem, raw, src_row_ap)
                vsc = None
            else:
                io = (0 if sl == 0 else 1024)
                W(load_q, sS, war_i8[sl])
                vld = dma(load_q, load_sem,
                          i8st[:, io:io + ncols_c * 128], src_row_ap)
                W("scalar", load_sem, vld)
                W("scalar", sT, war_raw[sl])
                vsc = op("scalar", sS, lambda eng, raw=raw, c=scale_col, io=io,
                         n=ncols_c * 128:
                         eng.activation(raw, i8st[:, io:io + n], AF.Identity,
                                        scale=consts[:, c:c + 1]))
            vv_last = 0
            for g in range((ncols_c + 3) // 4):
                bank = g % 2
                n_in = min(4, ncols_c - 4 * g)
                if vsc is None:
                    W("tensor", load_sem, vld)
                else:
                    W("tensor", sS, vsc)
                bank_wait("tensor", ("t", bank))
                for k in range(n_in):
                    c = 4 * g + k
                    vt = op("tensor", sT, lambda eng, raw=raw, c=c, bank=bank, k=k:
                            eng.transpose(psT[:, bank * 1024 + k * 128:
                                              bank * 1024 + (k + 1) * 128],
                                          raw[:, c * 128:(c + 1) * 128], ident[:]),
                            inc=(k == n_in - 1))
                war_raw[sl] = vt
                if scale_col is not None:
                    war_i8[sl] = vsc
                W("vector", sT, vt)
                W("vector", dS, dS.n)
                for k in range(n_in):
                    c = 4 * g + k
                    vv = op("vector", sV, lambda eng, eva=eva, c=c, bank=bank, k=k:
                            eng.tensor_copy(eva[:, c * 128:(c + 1) * 128],
                                            psT[:, bank * 1024 + k * 128:
                                                bank * 1024 + (k + 1) * 128]),
                            inc=(k == n_in - 1))
                war_bank[("t", bank)] = (sV, vv)
                vv_last = vv
            W("scalar", sV, vv_last)
            war_eva[sl] = dma("scalar", dS, dst_ap,
                              eva.rearrange("p (c e) -> p c e", e=128))
            return war_eva[sl]

        for rb in range(NEC):
            prep_chunk(wqkv[rb * 128:(rb + 1) * 128, :],
                       wqkvT_d.rearrange("(c dd) e -> dd c e", dd=128)
                       [:, :, rb * 128:(rb + 1) * 128], 16, rb % 2, "gpsimd", dG)
        v_wqkvT = ckpt("scalar", dS)
        for rb in range(16):
            prep_chunk(outw[rb * 128:(rb + 1) * 128, :],
                       outwT_d.rearrange("(c dd) e -> dd c e", dd=128)
                       [:, :, rb * 128:(rb + 1) * 128], 16, rb % 2, "gpsimd", dG)
        v_outwT = ckpt("scalar", dS)
        v_wprep = [0]
        v_ffnwT = [0]
        late_done = [False]

        def emit_late_preps():
            if late_done[0]:
                return
            late_done[0] = True
        v_prep_arena_free = 0

        ffnwT_started = [False]

        def emit_ffnwT_part(i):
            if not ffnwT_started[0]:
                ffnwT_started[0] = True
                W("sync", sS, sS.n)
                W("sync", sV, sV.n)
                W("sync", dQ, v_consts)
            rb2 = [XHI, XHI + 1024]
            eb2 = [XHI + 2048, XHI + 3072]
            rb, hb = i // 2, i % 2
            prep_chunk(ffnw[rb * 128:(rb + 1) * 128,
                            hb * 1024:(hb + 1) * 1024],
                       ffnwT_d[hb * 1024:, :]
                       .rearrange("(c dd) e -> dd c e", dd=128)
                       [:, 0:8, rb * 128:(rb + 1) * 128],
                       8, i % 2, "sync", dQ,
                       scale_col=18 + rb, rawbase=rb2, evabase=eb2)

        def layernorm_tile(src, dst, sl):
            for c in range(4):
                op("vector", sV, lambda eng, s=src, sl=sl, c=c: eng.bn_stats(
                    sc2[:, 24 * sl + 6 * c:24 * sl + 6 * c + 6],
                    s[:, c * 512:(c + 1) * 512]), inc=False)
            b.drain("vector")
            vst = op("vector", sV, lambda eng, sl=sl: eng.bn_aggr(
                sc1[:, 2 * sl:2 * sl + 2], sc2[:, 24 * sl:24 * sl + 24]))
            W("scalar", sV, vst)
            sst = op("scalar", sS, lambda eng, sl=sl: eng.activation(
                sc1[:, 16 + sl:17 + sl], sc1[:, 2 * sl + 1:2 * sl + 2],
                AF.Sqrt, bias=consts[:, 34:35]))
            W("vector", sS, sst)
            op("vector", sV, lambda eng, sl=sl: eng.reciprocal(
                sc1[:, 24 + sl:25 + sl], sc1[:, 16 + sl:17 + sl]), inc=False)
            b.drain("vector")
            vr = op("vector", sV, lambda eng, sl=sl: eng.tensor_scalar(
                sc1[:, 32 + sl:33 + sl], sc1[:, 2 * sl:2 * sl + 1],
                sc1[:, 24 + sl:25 + sl], -1.0, ALU.mult, ALU.mult))
            W("scalar", sV, vr)
            return op("scalar", sS, lambda eng, s=src, d=dst, sl=sl: eng.activation(
                d, s, AF.Identity, bias=sc1[:, 32 + sl:33 + sl],
                scale=sc1[:, 24 + sl:25 + sl]))

        def tile_to_xT(src, tcol, ready_pairs, war_v=0):
            vv_last = 0
            for g in range(4):
                bank = g % 2
                for (s_, v_) in ready_pairs:
                    W("tensor", s_, v_)
                bank_wait("tensor", ("t", bank))
                for k in range(4):
                    c = 4 * g + k
                    vt = op("tensor", sT, lambda eng, s=src, c=c, bank=bank, k=k:
                            eng.transpose(psT[:, bank * 1024 + k * 128:
                                              bank * 1024 + (k + 1) * 128],
                                          s[:, c * 128:(c + 1) * 128], ident[:]),
                            inc=(k == 3))
                W("vector", sT, max(vt, war_v))
                for k in range(4):
                    c = 4 * g + k
                    vv = op("vector", sV, lambda eng, c=c, bank=bank, k=k, t=tcol:
                            eng.tensor_copy(
                                ar[:, XT + c * 1024 + t * 128:
                                   XT + c * 1024 + (t + 1) * 128],
                                psT[:, bank * 1024 + k * 128:
                                    bank * 1024 + (k + 1) * 128]),
                            inc=(k == 3))
                war_bank[("t", bank)] = (sV, vv)
                vv_last = vv
            return vv_last

        def normmax(src, ncols, pcol, out_col, extra_sq_wait=()):
            nch = ncols // 512
            cdone = 0
            for big in range(0, ncols, 2048):
                bw = min(2048, ncols - big)
                for (s_, v_) in extra_sq_wait:
                    W("scalar", s_, v_)
                W("scalar", sT, sqs_war[0])
                vsq = op("scalar", sS, lambda eng, s=src, b0=big, bw=bw:
                         eng.activation(sqs[:, 0:bw], s[:, b0:b0 + bw], AF.Square))
                for cc2 in range(bw // 512):
                    cch = cdone
                    cdone += 1
                    W("tensor", sS, vsq)
                    bank_wait("tensor", pcol)
                    vmm = mm(pcol, 512, onesm[:],
                             sqs[:, cc2 * 512:(cc2 + 1) * 512], True, True)
                    sqs_war[0] = vmm
                    W("vector", sT, vmm)
                    b.drain("vector")
                    vrd = op("vector", sV,
                             lambda eng, cch=cch, pc=pcol, oc=out_col:
                             eng.reduce_max(
                                 sc1[:, 64 + oc * 8 + cch:65 + oc * 8 + cch],
                                 psum[:, pc:pc + 512],
                                 axis=mybir.AxisListType.X))
                    war_bank[pcol] = (sV, vrd)
            b.drain("vector")
            vfin = op("vector", sV, lambda eng, oc=out_col, n=nch:
                      eng.tensor_reduce(sc1[:, 40 + oc:41 + oc],
                                        sc1[:, 64 + oc * 8:64 + oc * 8 + n],
                                        axis=mybir.AxisListType.X, op=ALU.max))
            return 40 + out_col, vfin

        xhi_war = [0, 0]
        xho_war = [0, 0]
        wcol_war = [0, 0]
        v_out_a = [0, 0]

        def s1_gq(gq):
            hh = gq // 2
            vxt = 0
            for j in range(8):
                js = gq * 8 + j
                sl = js % 2
                xin = ar[:, XHI + sl * D:XHI + sl * D + D]
                W("gpsimd", sS, xhi_war[sl])
                vld = dma("gpsimd", dG, xin, x0stg[js * 128:(js + 1) * 128, :])
                W("vector", dG, vld)
                xout = ar[:, XHO + sl * D:XHO + sl * D + D]
                W("vector", sT, max(xho_war[sl], war_wout[0]))
                W("scalar", sT, max(xho_war[sl], war_wout[0]))
                vn = layernorm_tile(xin, xout, js % 8)
                xhi_war[sl] = vn
                vxt = tile_to_xT(xout, j, [(sS, vn)])
                xho_war[sl] = sT.n
            for e in range(16):
                wsl = e % 2
                wc = ar[:, WCOL + wsl * 2048:WCOL + wsl * 2048 + 2048]
                W("sync", sT, max(wcol_war[wsl], war_wout[0]))
                W("sync", dS, v_wqkvT)
                vw = dma("sync", dQ, wc.rearrange("p (c e) -> p c e", e=128),
                         wqkvT_d.rearrange("(c dd) e -> dd c e", dd=128)
                         [:, :, e * 128:(e + 1) * 128])
                pc = (e % 2) * 512
                W("tensor", dQ, vw)
                W("tensor", sV, vxt)
                bank_wait("tensor", pc)
                for d in range(16):
                    xc = ar[:, XT + d * 1024:XT + (d + 1) * 1024]
                    rhs = xc.rearrange("p (a c) -> p a c", c=128)[:, 0:8:2, :]
                    vmm = mm(pc, 512, wc[:, d * 128:(d + 1) * 128], rhs,
                             start=(d == 0), stop=(d == 15))
                wcol_war[wsl] = vmm
                W("scalar", sT, vmm)
                if gq < 2 and e <= 8:
                    W("scalar", dQ, v_prep_arena_free)
                W("scalar", sT, war_wout[0])
                qdst = ar[:, QH + e * 1024 + (gq % 2) * 512:
                          QH + e * 1024 + (gq % 2) * 512 + 512]
                vev = op("scalar", sS, lambda eng, dd=qdst, pc=pc, e=e:
                         eng.activation(dd, psum[:, pc:pc + 512], AF.Identity,
                                        bias=consts[:, e:e + 1]))
                war_bank[pc] = (sS, vev)
            v_qh = sS.n
            wc = ar[:, WCOL:WCOL + 2048]
            W("sync", sT, max(wcol_war[0], war_wout[0]))
            W("sync", dS, v_wqkvT)
            vw = dma("sync", dQ, wc.rearrange("p (c e) -> p c e", e=128),
                     wqkvT_d.rearrange("(c dd) e -> dd c e", dd=128)
                     [:, :, 16 * 128:17 * 128])
            W("tensor", dQ, vw)
            W("tensor", sV, vxt)
            bank_wait("tensor", 1024)
            bank_wait("tensor", 1536)
            for d in range(16):
                xc = ar[:, XT + d * 1024:XT + (d + 1) * 1024]
                mm(1024, 512, wc[:, d * 128:(d + 1) * 128], xc[:, 0:512],
                   start=(d == 0), stop=(d == 15), inc=False)
                vmm = mm(1536, 512, wc[:, d * 128:(d + 1) * 128], xc[:, 512:1024],
                         start=(d == 0), stop=(d == 15), inc=(d == 15))
            wcol_war[0] = vmm
            W("scalar", sT, vmm)
            vk = op("scalar", sS, lambda eng, gq=gq: eng.activation(
                ar[:, KA + gq * 1024:KA + (gq + 1) * 1024],
                psum[:, 1024:2048], AF.Identity, bias=consts[:, 16:17]))
            war_bank[1024] = (sS, vk)
            war_bank[1536] = (sS, vk)
            wc2 = ar[:, WCOL + 2048:WCOL + 4096]
            W("sync", sT, max(wcol_war[1], war_wout[0]))
            vw2 = dma("sync", dQ, wc2.rearrange("p (c e) -> p c e", e=128),
                      wqkvT_d.rearrange("(c dd) e -> dd c e", dd=128)
                      [:, :, 17 * 128:18 * 128])
            W("tensor", dQ, vw2)
            W("tensor", dS, v_bva)
            for tt in range(8):
                pc = 2048 + (tt % 2) * 512
                bank_wait("tensor", pc)
                for d in range(16):
                    xc = ar[:, XT + d * 1024:XT + (d + 1) * 1024]
                    mm(pc, 128, xc[:, tt * 128:(tt + 1) * 128],
                       wc2[:, d * 128:(d + 1) * 128],
                       start=(d == 0), stop=False, inc=False)
                vb = biasmm(pc, 128, b1row[0:1, 128:256], stop=True)
                wcol_war[1] = vb
                W("vector", sT, vb)
                vva = op("vector", sV, lambda eng, gq=gq, tt=tt, pc=pc:
                         eng.tensor_copy(
                             ar[:, VA + (gq * 8 + tt) * 128:
                                VA + (gq * 8 + tt + 1) * 128],
                             psum[:, pc:pc + 128]))
                war_bank[pc] = (sV, vva)
            return v_qh

        ka_ready = {}
        sqs_war = [0]
        attn_war = {"p": [0] * 8, "s": [0] * 4}

        def normmax_to(src_ap, ncols, pcol, dst_col, sq_slot, wait_pairs):
            nch = ncols // 512
            cdone = 0
            red_marks = []
            for big in range(0, ncols, 1024):
                bw = min(1024, ncols - big)
                for (s_, v_) in wait_pairs:
                    W("scalar", s_, v_)
                W("scalar", sT, sqs_war[0])
                W("scalar", sV, sqs_rd[sq_slot])
                vsq = op("scalar", sS, lambda eng, s=src_ap, b0=big, bw=bw,
                         o=sq_slot * 1024:
                         eng.activation(sqs[:, o:o + bw], s[:, b0:b0 + bw],
                                        AF.Square))
                for cc2 in range(bw // 512):
                    cch = cdone
                    cdone += 1
                    W("tensor", sS, vsq)
                    bank_wait("tensor", pcol)
                    vmm = mm(pcol, 512, onesm[:],
                             sqs[:, sq_slot * 1024 + cc2 * 512:
                                 sq_slot * 1024 + (cc2 + 1) * 512], True, True)
                    sqs_war[0] = vmm
                    sqs_rd[sq_slot] = 0
                    W("vector", sT, vmm)
                    b.drain("vector")
                    vrd = op("vector", sV, lambda eng, cch=cch, pc=pcol,
                             dc=64 + sq_slot * 8 + cch % 8:
                             eng.reduce_max(sc1[:, dc:dc + 1],
                                            psum[:, pc:pc + 512],
                                            axis=mybir.AxisListType.X))
                    war_bank[pcol] = (sV, vrd)
                    red_marks.append(vrd)
            b.drain("vector")
            vfin = op("vector", sV, lambda eng, dc=dst_col, n=nch,
                      o=64 + sq_slot * 8:
                      eng.tensor_reduce(sc1[:, dc:dc + 1], sc1[:, o:o + n],
                                        axis=mybir.AxisListType.X, op=ALU.max))
            return vfin

        sqs_rd = [0, 0]

        def s1_attention(hh, v_qh, filler=None):
            attn_war["p"] = [max(v, war_wout[0]) for v in attn_war["p"]]
            ksrc = ar[:, KA:KA + 16 * (hh + 1) * 128]
            vnk = normmax_to(ksrc, 16 * (hh + 1) * 128, 512, 96, 0,
                             [(sS, ka_ready[hh])])
            for h in range(H):
                qsrc = ar[:, QH + h * 1024:QH + (h + 1) * 1024]
                vq_ = normmax_to(qsrc, 1024, 512, 100 + h, h % 2,
                                 [(sS, v_qh)])
            b.drain("vector")
            for h in range(H):
                op("vector", sV, lambda eng, h=h: eng.tensor_scalar(
                    sc1[:, 100 + h:101 + h], sc1[:, 100 + h:101 + h],
                    sc1[:, 96:97], 1e-20, ALU.mult, ALU.max), inc=False)
            b.drain("vector")
            vpr = op("vector", sV, lambda eng: eng.memset(sc1[:, 127:128], 0.0))
            W("scalar", sV, vpr)
            vsq = op("scalar", sS, lambda eng: eng.activation(
                sc1[:, 80:96], sc1[:, 100:116], AF.Sqrt))
            W("vector", sS, vsq)
            b.drain("vector")
            vC = op("vector", sV, lambda eng: eng.tensor_scalar(
                negC[:, 0:16], sc1[:, 80:96], -SCALE, 0.0,
                ALU.mult, ALU.bypass))
            vctx = 0
            for h in range(H):
                if filler is not None:
                    filler(2 * h)
                    filler(2 * h + 1)
                for qc in (2 * hh, 2 * hh + 1):
                    vctx = run_attn_inner("a", hh, h, qc, vC, v_qh)
            return vctx

        def run_attn_inner(kind, hh, h, qc, vC, v_qsrc, kv_marks=None):
            tiles = attn_tiles(kind, qc)
            qbase = QH + h * (1024 if kind == "a" else 2048)
            qoff = (qc - 2 * hh) * 512 if kind == "a" else qc * 512
            qap = ar[:, qbase + qoff:qbase + qoff + 512]
            psl = PSL if kind == "a" else PSL3
            pslot_war = attn_war["p"]
            sbank_war = attn_war["s"]
            n = len(tiles)
            def kap_of(src):
                if kind == "a":
                    return ar[:, KA + src * 128:KA + (src + 1) * 128]
                ph, m_ = src
                return kv_buf_ap("k", ph, m_, kv_marks.get("base"))

            def vap_of(src):
                if kind == "a":
                    return ar[:, VA + src * 128:VA + (src + 1) * 128]
                ph, m_ = src
                return kv_buf_ap("v", ph, m_, kv_marks.get("base"))

            vs_marks = [0] * n
            v_avlast = 0
            for ii in range(n):
                src, msk = tiles[ii]
                sb = 1024 + (ii % 4) * 512
                W("tensor", sS, sbank_war[ii % 4])
                if kind == "a":
                    W("tensor", sS, v_qsrc)
                else:
                    W("tensor", dQ, max(kv_marks[("k", 0)], kv_marks[("k", 1)]))
                vs_marks[ii] = mm(sb, 512, kap_of(src), qap, True, True)
                if ii % 2 == 1 or ii == n - 1:
                    lo = ii - (ii % 2)
                    wdt = 512 * (ii - lo + 1)
                    pap = ar[:, psl + (lo % 8) * 512:psl + (lo % 8) * 512 + wdt]
                    W("scalar", sT, vs_marks[ii])
                    W("scalar", sT, max(pslot_war[lo % 8], pslot_war[ii % 8]))
                    W("scalar", sV, vC)
                    ve = op("scalar", sS, lambda eng, p=pap, sb=1024 + (lo % 4) * 512,
                            wdt=wdt, h=h:
                            eng.activation(p, psum[:, sb:sb + wdt], AF.Exp,
                                           bias=negC[:, h:h + 1], scale=SCALE))
                    for jj in range(lo, ii + 1):
                        sbank_war[jj % 4] = ve
                    for jj in range(lo, ii + 1):
                        src2, msk2 = tiles[jj]
                        pap1 = ar[:, psl + (jj % 8) * 512:psl + (jj % 8) * 512 + 512]
                        if msk2 is not None:
                            W("vector", sS, ve)
                            vmk = op("vector", sV, lambda eng, p=pap1, r=msk2:
                                     eng.tensor_tensor(
                                         p, p, masks[:, r * 512:(r + 1) * 512],
                                         ALU.mult))
                            W("tensor", sV, vmk)
                        else:
                            W("tensor", sS, ve)
                        st = (jj == 0)
                        if st:
                            bank_wait("tensor", 512)
                            bank_wait("tensor", 0)
                        last = (jj == n - 1)
                        mm(512, 512, onesm[:], pap1, st, last, inc=False)
                        if kind == "b":
                            W("tensor", dS, max(kv_marks[("v", 0)],
                                                kv_marks[("v", 1)]))
                        vav = mm(0, 512, vap_of(src2), pap1, st, last, inc=True)
                        pslot_war[jj % 8] = vav
                        if last:
                            v_avlast = vav
            W("vector", sT, v_avlast)
            op("vector", sV, lambda eng: eng.reciprocal(
                dscr[:, 0:512], psum[:, 512:1024]), inc=False)
            b.drain("vector")
            cdst = ar[:, qbase + qoff:qbase + qoff + 512]
            vnorm = op("vector", sV, lambda eng, cd=cdst:
                       eng.tensor_tensor(cd, psum[:, 0:512], dscr[:, 0:512],
                                         ALU.mult))
            war_bank[0] = (sV, vnorm)
            war_bank[512] = (sV, vnorm)
            return vnorm

        def kv_buf_ap(which, ph, m_, kvbase=None):
            base = (KVB if kvbase is None else kvbase) + (0 if which == "k" else 4096)
            return ar[:, base + ph * 2048 + m_ * 128:
                      base + ph * 2048 + (m_ + 1) * 128]

        war_wout = [0]

        def out_proj(wdram_fn, brow_idx, tiles_, qstride, tcol_fn, vctx,
                     vx_extra=None, wready=0, load_q="sync"):
            war_wout[0] = max(war_wout[0], sT.n)
            lsem = dQ if load_q == "sync" else dG
            vload_b = dma("scalar", dS, b1row[0:1, 256:2304],
                          brows_in[brow_idx:brow_idx + 1, :])
            for n_ in range(4):
                wsl_ap = ar[:, WOUT:WOUT + 8192]
                W(load_q, sT, war_wout[0])
                W(load_q, dS, wready)
                vw = dma(load_q, lsem, wsl_ap.rearrange("p (c e) -> p c e", e=512),
                         wdram_fn(n_))
                for ji, ti in enumerate(tiles_):
                    pc = (ji % 4) * 512
                    W("tensor", lsem, vw)
                    W("tensor", sV, vctx)
                    W("tensor", dS, vload_b)
                    bank_wait("tensor", pc)
                    for c in range(16):
                        mm(pc, 512, ar[:, QH + c * qstride + tcol_fn(ti) * 128:
                                       QH + c * qstride + (tcol_fn(ti) + 1) * 128],
                           wsl_ap[:, c * 512:(c + 1) * 512],
                           start=(c == 0), stop=False, inc=False)
                    vb = biasmm(pc, 512,
                                b1row[0:1, 256 + n_ * 512:768 + n_ * 512])
                    W("vector", sT, vb)
                    if vx_extra is not None:
                        W("vector", vx_extra[0], vx_extra[1])
                    xsl = x_sb[:, ti * D + n_ * 512:ti * D + (n_ + 1) * 512]
                    vep = op("vector", sV, lambda eng, xs=xsl, pc=pc:
                             eng.tensor_tensor(xs, xs, psum[:, pc:pc + 512],
                                               ALU.add))
                    war_bank[pc] = (sV, vep)
                war_wout[0] = sT.n
            return sV.n

        def s1_out(hh, vctx):
            return out_proj(
                lambda n_: outwT_d.rearrange("(c dd) e -> dd c e", dd=128)
                [:, :, n_ * 512:(n_ + 1) * 512],
                0, list(range(8 * hh, 8 * hh + 8)), 1024,
                lambda ti: ti - 8 * hh, vctx, wready=v_outwT)

        fp16_war = [0]
        x_mark = [0] * NT

        def ffn_mm(hh, vxt):
            war_wout[0] = max(war_wout[0], sT.n)
            vload_b = dma("scalar", dS, b1row[0:1, 256:2304],
                          brows_in[1:2, :])
            for n_ in range(4):
                wsl_ap = ar[:, WOUT:WOUT + 8192]
                W("sync", sT, war_wout[0])
                W("sync", dS, v_ffnwT[0])
                vw = dma("sync", dQ, wsl_ap.rearrange("p (c e) -> p c e", e=512),
                         ffnwT_d.rearrange("(c dd) e -> dd c e", dd=128)
                         [:, :, n_ * 512:(n_ + 1) * 512])
                for ji in range(8):
                    ti = 8 * hh + ji
                    pc = (ji % 4) * 512
                    W("tensor", dQ, vw)
                    W("tensor", sV, vxt)
                    W("tensor", dS, vload_b)
                    bank_wait("tensor", pc)
                    for c in range(16):
                        mm(pc, 512, ar[:, XT + c * 1024 + ji * 128:
                                       XT + c * 1024 + (ji + 1) * 128],
                           wsl_ap[:, c * 512:(c + 1) * 512],
                           start=(c == 0), stop=False, inc=False)
                    vb = biasmm(pc, 512,
                                b1row[0:1, 256 + n_ * 512:768 + n_ * 512])
                    W("scalar", sT, vb)
                    W("scalar", sV, fp16_war[0])
                    vf = op("scalar", sS, lambda eng, pc=pc: eng.copy(
                        fp16t[:, 0:512], psum[:, pc:pc + 512]))
                    W("vector", sS, vf)
                    xsl = x_sb[:, ti * D + n_ * 512:ti * D + (n_ + 1) * 512]
                    vep = op("vector", sV, lambda eng, xs=xsl: eng.tensor_tensor(
                        xs, xs, fp16t[:, 0:512], ALU.add))
                    fp16_war[0] = vep
                    x_mark[ti] = vep
                    war_bank[pc] = (sV, vep)
                war_wout[0] = sT.n
            return sV.n

        quant_war = {"xh": [0, 0], "i16": 0}

        def quant_tile(src, dst, sl):
            W("scalar", sV, quant_war["xh"][sl % 2])
            vn = layernorm_tile(src, ar[:, SM3 + (sl % 2) * D:
                                        SM3 + (sl % 2) * D + D], sl)
            xh = ar[:, SM3 + (sl % 2) * D:SM3 + (sl % 2) * D + D]
            W("vector", sS, vn)
            vam = op("vector", sV, lambda eng, x=xh, sl=sl: eng.tensor_reduce(
                sc1[:, 56 + sl:57 + sl], x, axis=mybir.AxisListType.X,
                op=ALU.max, apply_absolute_value=True))
            b.drain("vector")
            op("vector", sV, lambda eng, sl=sl: eng.tensor_scalar(
                sc1[:, 56 + sl:57 + sl], sc1[:, 56 + sl:57 + sl], 1e-7, 0.0,
                ALU.max, ALU.bypass), inc=False)
            b.drain("vector")
            op("vector", sV, lambda eng, sl=sl: eng.reciprocal(
                sc1[:, 48 + sl:49 + sl], sc1[:, 56 + sl:57 + sl]), inc=False)
            b.drain("vector")
            vsq_ = op("vector", sV, lambda eng, sl=sl: eng.tensor_scalar(
                sc1[:, 48 + sl:49 + sl], sc1[:, 48 + sl:49 + sl], 127.0, 0.0,
                ALU.mult, ALU.bypass))
            b.drain("vector")
            vnis = op("vector", sV, lambda eng, sl=sl: eng.tensor_scalar(
                sc1[:, 56 + sl:57 + sl], sc1[:, 56 + sl:57 + sl],
                float(-1.0 / 127.0), 0.0, ALU.mult, ALU.bypass))
            vq = 0
            for hb in range(2):
                xhh = xh[:, hb * 1024:(hb + 1) * 1024]
                dsth = dst[:, hb * 1024:(hb + 1) * 1024]
                W("scalar", sV, vsq_ if hb == 0 else vq)
                W("scalar", sV, quant_war["i16"])
                vi = op("scalar", sS, lambda eng, x=xhh, sl=sl: eng.activation(
                    i16t[:], x, AF.Identity, scale=sc1[:, 48 + sl:49 + sl]))
                W("vector", sS, vi)
                op("vector", sV, lambda eng, x=xhh, sl=sl:
                   eng.scalar_tensor_tensor(
                       dscr[:], x, sc1[:, 48 + sl:49 + sl], i16t[:],
                       ALU.mult, ALU.subtract), inc=False)
                b.drain("vector")
                op("vector", sV, lambda eng: eng.scalar_tensor_tensor(
                    sqs[:, 0:1024], dscr[:], 0.5, i16t[:], ALU.is_ge, ALU.add),
                   inc=False)
                b.drain("vector")
                op("vector", sV, lambda eng: eng.scalar_tensor_tensor(
                    i16t[:], dscr[:], -0.5, sqs[:, 0:1024], ALU.is_le,
                    ALU.subtract), inc=False)
                b.drain("vector")
                vq = op("vector", sV, lambda eng, d=dsth, sl=sl:
                        eng.tensor_scalar(d, i16t[:], sc1[:, 56 + sl:57 + sl],
                                          0.0, ALU.mult, ALU.bypass))
                quant_war["i16"] = vq
            quant_war["xh"][sl % 2] = vq
            return vq

        xt_war = [0]

        xq_war = [0]

        def ln8(hh, base, war_mark):
            W("vector", sT, war_mark)
            W("scalar", sT, war_mark)
            for j in range(8):
                src_ = x_sb[:, (8 * hh + j) * D:(8 * hh + j + 1) * D]
                for c in range(4):
                    op("vector", sV, lambda eng, s=src_, j=j, c=c: eng.bn_stats(
                        sc2[:, 24 * j + 6 * c:24 * j + 6 * c + 6],
                        s[:, c * 512:(c + 1) * 512]), inc=False)
            b.drain("vector")
            for j in range(8):
                op("vector", sV, lambda eng, j=j: eng.bn_aggr(
                    sc1[:, 2 * j:2 * j + 2], sc2[:, 24 * j:24 * j + 24]),
                   inc=False)
            b.drain("vector")
            vst = op("vector", sV, lambda eng: eng.memset(sc1[:, 126:127], 0.0))
            W("scalar", sV, vst)
            vsd = op("scalar", sS, lambda eng: eng.activation(
                sc1[:, 16:24], sc1[:, 1:16:2], AF.Sqrt, bias=consts[:, 34:35]))
            W("vector", sS, vsd)
            op("vector", sV, lambda eng: eng.reciprocal(
                sc1[:, 24:32], sc1[:, 16:24]), inc=False)
            b.drain("vector")
            op("vector", sV, lambda eng: eng.tensor_tensor(
                sc1[:, 32:40], sc1[:, 0:15:2], sc1[:, 24:32], ALU.mult),
               inc=False)
            b.drain("vector")
            vnm = op("vector", sV, lambda eng: eng.tensor_scalar(
                sc1[:, 32:40], sc1[:, 32:40], -1.0, 0.0, ALU.mult, ALU.bypass))
            W("scalar", sV, vnm)
            vn8 = 0
            for j in range(8):
                src_ = x_sb[:, (8 * hh + j) * D:(8 * hh + j + 1) * D]
                vn8 = op("scalar", sS, lambda eng, s=src_, j=j: eng.activation(
                    ar[:, base + j * D:base + (j + 1) * D], s, AF.Identity,
                    bias=sc1[:, 32 + j:33 + j], scale=sc1[:, 24 + j:25 + j]))
            return vn8

        def quant_half(hh):
            XQ = 0
            XH8 = 0
            W("vector", sT, xq_war[0])
            W("scalar", sT, xq_war[0])
            for j in range(8):
                src_ = x_sb[:, (8 * hh + j) * D:(8 * hh + j + 1) * D]
                for c in range(4):
                    op("vector", sV, lambda eng, s=src_, j=j, c=c: eng.bn_stats(
                        sc2[:, 24 * j + 6 * c:24 * j + 6 * c + 6],
                        s[:, c * 512:(c + 1) * 512]), inc=False)
            b.drain("vector")
            for j in range(8):
                op("vector", sV, lambda eng, j=j: eng.bn_aggr(
                    sc1[:, 2 * j:2 * j + 2], sc2[:, 24 * j:24 * j + 24]),
                   inc=False)
            b.drain("vector")
            vst = op("vector", sV, lambda eng: eng.memset(sc1[:, 126:127], 0.0))
            W("scalar", sV, vst)
            vsd = op("scalar", sS, lambda eng: eng.activation(
                sc1[:, 16:24], sc1[:, 1:17].rearrange("p (a b) -> p (a b)")
                [:, 0:16:2] if False else sc1[:, 1:16:2], AF.Sqrt,
                bias=consts[:, 34:35]))
            W("vector", sS, vsd)
            op("vector", sV, lambda eng: eng.reciprocal(
                sc1[:, 24:32], sc1[:, 16:24]), inc=False)
            b.drain("vector")
            op("vector", sV, lambda eng: eng.tensor_tensor(
                sc1[:, 32:40], sc1[:, 0:15:2], sc1[:, 24:32], ALU.mult),
               inc=False)
            b.drain("vector")
            vnm = op("vector", sV, lambda eng: eng.tensor_scalar(
                sc1[:, 32:40], sc1[:, 32:40], -1.0, 0.0, ALU.mult, ALU.bypass))
            W("scalar", sV, vnm)
            vnorm8 = 0
            for j in range(8):
                src_ = x_sb[:, (8 * hh + j) * D:(8 * hh + j + 1) * D]
                vnorm8 = op("scalar", sS, lambda eng, s=src_, j=j: eng.activation(
                    ar[:, XH8 + j * D:XH8 + (j + 1) * D], s, AF.Identity,
                    bias=sc1[:, 32 + j:33 + j], scale=sc1[:, 24 + j:25 + j]))
            W("vector", sS, vnorm8)
            for j in range(8):
                op("vector", sV, lambda eng, j=j: eng.tensor_reduce(
                    sc1[:, 56 + j:57 + j], ar[:, XH8 + j * D:XH8 + (j + 1) * D],
                    axis=mybir.AxisListType.X, op=ALU.max,
                    apply_absolute_value=True), inc=False)
            b.drain("vector")
            op("vector", sV, lambda eng: eng.tensor_scalar(
                sc1[:, 56:64], sc1[:, 56:64], 1e-7, 0.0, ALU.max, ALU.bypass),
               inc=False)
            b.drain("vector")
            op("vector", sV, lambda eng: eng.reciprocal(
                sc1[:, 48:56], sc1[:, 56:64]), inc=False)
            b.drain("vector")
            op("vector", sV, lambda eng: eng.tensor_scalar(
                sc1[:, 48:56], sc1[:, 48:56], 127.0, 0.0, ALU.mult, ALU.bypass),
               inc=False)
            b.drain("vector")
            vsc_ = op("vector", sV, lambda eng: eng.tensor_scalar(
                sc1[:, 56:64], sc1[:, 56:64], float(-1.0 / 127.0), 0.0,
                ALU.mult, ALU.bypass))
            vq = vsc_
            vi_prev = 0
            for j in range(8):
                xh = ar[:, XH8 + j * D:XH8 + (j + 1) * D]
                dst = xh
                for hb in range(2):
                    xhh = xh[:, hb * 1024:(hb + 1) * 1024]
                    dsth = dst[:, hb * 1024:(hb + 1) * 1024]
                    W("scalar", sV, vq)
                    vi = op("scalar", sS, lambda eng, x=xhh, j=j: eng.activation(
                        i16t[:], x, AF.Identity, scale=sc1[:, 48 + j:49 + j]))
                    W("vector", sS, vi)
                    op("vector", sV, lambda eng, x=xhh, j=j:
                       eng.scalar_tensor_tensor(
                           dscr[:], x, sc1[:, 48 + j:49 + j], i16t[:],
                           ALU.mult, ALU.subtract), inc=False)
                    b.drain("vector")
                    op("vector", sV, lambda eng: eng.scalar_tensor_tensor(
                        sqs[:, 0:1024], dscr[:], 0.5, i16t[:], ALU.is_ge,
                        ALU.add), inc=False)
                    b.drain("vector")
                    op("vector", sV, lambda eng: eng.scalar_tensor_tensor(
                        i16t[:], dscr[:], -0.5, sqs[:, 0:1024], ALU.is_le,
                        ALU.subtract), inc=False)
                    b.drain("vector")
                    vq = op("vector", sV, lambda eng, d=dsth, j=j:
                            eng.tensor_scalar(d, i16t[:],
                                              sc1[:, 56 + j:57 + j], 0.0,
                                              ALU.mult, ALU.bypass))
            return vq

        def ffn_sublayer():
            emit_late_preps()
            xq_war[0] = max(xq_war[0], sT.n)
            vdone = 0
            vq = [0, 0]
            vq[0] = quant_half(0)
            for hh in range(2):
                vxt = 0
                for j in range(8):
                    vxt = tile_to_xT(ar[:, j * D:(j + 1) * D], j,
                                     [(sV, vq[hh])], war_v=xt_war[0])
                xq_war[0] = sT.n
                if hh == 0:
                    vq[1] = quant_half(1)
                vdone = ffn_mm(hh, vxt)
                xt_war[0] = sT.n
            return vdone

        def s3_sublayer():
            W("gpsimd", sS, sS.n)
            W("gpsimd", sV, sV.n)
            kq_war = [sT.n]
            kq_war2 = [sT.n, sT.n]
            s3_w_done = [0]
            kst_war = [0, 0]
            knorm_done = 0
            s3ln_war = [sT.n]
            for hh in range(2):
                vxt = 0
                vn8 = ln8(hh, 12288, s3ln_war[0])
                for j in range(8):
                    vxt = tile_to_xT(ar[:, 12288 + j * D:12288 + (j + 1) * D],
                                     j, [(sS, vn8)], war_v=xt_war[0])
                s3ln_war[0] = sT.n
                xt_war[0] = 0
                for which in ("k",):
                    for h in range(H):
                        wc = ar[:, 8192 + (h % 2) * 2048:
                                8192 + (h % 2) * 2048 + 2048]
                        W("gpsimd", sT, kq_war2[h % 2])
                        vw = dma("gpsimd", dG,
                                 wc.rearrange("p (c e) -> p c e", e=128),
                                 wk_in[h].rearrange("(c dd) e -> dd c e", dd=128))
                        pc = (h % 2) * 1024
                        W("tensor", dG, vw)
                        W("tensor", sV, vxt)
                        bank_wait("tensor", pc)
                        bank_wait("tensor", pc + 512)
                        for d in range(16):
                            xc = ar[:, XT + d * 1024:XT + (d + 1) * 1024]
                            mm(pc, 512, wc[:, d * 128:(d + 1) * 128],
                               xc[:, 0:512], start=(d == 0), stop=(d == 15),
                               inc=False)
                            vmm = mm(pc + 512, 512, wc[:, d * 128:(d + 1) * 128],
                                     xc[:, 512:1024], start=(d == 0),
                                     stop=(d == 15), inc=(d == 15))
                        kq_war2[h % 2] = vmm
                        W("scalar", sT, vmm)
                        if True:
                            sl2 = 0
                            kst = ar[:, KST:KST + 1024]
                            W("scalar", dQ, dQ.n)
                            vev = op("scalar", sS, lambda eng, ks=kst, pc=pc, h=h:
                                     eng.activation(ks, psum[:, pc:pc + 1024],
                                                    AF.Identity,
                                                    bias=bk_sb[:, h:h + 1]))
                            war_bank[pc] = (sS, vev)
                            war_bank[pc + 512] = (sS, vev)
                            W("scalar", sT, sqs_war[0])
                            vsq2 = op("scalar", sS, lambda eng, ks=kst:
                                      eng.activation(sqs[:, 0:1024], ks,
                                                     AF.Square))
                            for cch in range(2):
                                W("tensor", sS, vsq2)
                                bank_wait("tensor", 2048)
                                vmm2 = mm(2048, 512, onesm[:],
                                          sqs[:, cch * 512:(cch + 1) * 512],
                                          True, True)
                                sqs_war[0] = vmm2
                                W("vector", sT, vmm2)
                                b.drain("vector")
                                vrd = op("vector", sV, lambda eng, cch=cch:
                                         eng.reduce_max(sc1[:, 64 + cch:65 + cch],
                                                        psum[:, 2048:2560],
                                                        axis=mybir.AxisListType.X))
                                war_bank[2048] = (sV, vrd)
                            b.drain("vector")
                            if hh == 0:
                                vkn = op("vector", sV, lambda eng, h=h:
                                         eng.tensor_reduce(
                                             sc1[:, 96 + h:97 + h],
                                             sc1[:, 64:66],
                                             axis=mybir.AxisListType.X,
                                             op=ALU.max))
                            else:
                                op("vector", sV, lambda eng, h=h:
                                   eng.tensor_reduce(
                                       sc1[:, 64:65], sc1[:, 64:66],
                                       axis=mybir.AxisListType.X, op=ALU.max),
                                   inc=False)
                                b.drain("vector")
                                vkn = op("vector", sV, lambda eng, h=h:
                                         eng.tensor_tensor(
                                             sc1[:, 96 + h:97 + h],
                                             sc1[:, 96 + h:97 + h],
                                             sc1[:, 64:65], ALU.max))
                            knorm_done = vkn
                            W("sync", sS, vev)
                            kst_war[sl2] = dma("sync", dQ,
                                               cc_kv_in[h * 128:(h + 1) * 128,
                                                        hh * 1024:(hh + 1) * 1024],
                                               kst)
                vload_bv = dma("scalar", dS, b1row[0:1, 256:2304],
                               brows_in[2:3, :])
                for hq in range(4):
                    wvb = ar[:, 0:8192]
                    W("gpsimd", sT, kq_war[0])
                    for h4 in range(4):
                        vw = dma("gpsimd", dG,
                                 wvb.rearrange("p (c e) -> p c e", e=512)
                                 [:, :, h4 * 128:(h4 + 1) * 128],
                                 wv_in[hq * 4 + h4]
                                 .rearrange("(c dd) e -> dd c e", dd=128))
                    for ttg in range(2):
                        for tt in range(4):
                            bank_wait("tensor", tt * 512)
                        W("tensor", dG, vw)
                        W("tensor", sV, vxt)
                        W("tensor", dS, vload_bv)
                        for d in range(16):
                            xc = ar[:, XT + d * 1024:XT + (d + 1) * 1024]
                            for tt in range(4):
                                j2 = ttg * 4 + tt
                                vmm = mm(tt * 512, 512,
                                         xc[:, j2 * 128:(j2 + 1) * 128],
                                         wvb[:, d * 512:(d + 1) * 512],
                                         start=(d == 0), stop=False,
                                         inc=(tt == 3))
                        kq_war[0] = sT.n
                        for tt in range(4):
                            ti = 8 * hh + ttg * 4 + tt
                            pc = tt * 512
                            vb = biasmm(pc, 512,
                                        b1row[0:1, 256 + hq * 512:
                                              768 + hq * 512])
                            W("vector", sT, vb)
                            vst_ap = ar[:, KST + 1024:KST + 1536]
                            W("vector", dS, dS.n)
                            vev = op("vector", sV, lambda eng, v=vst_ap, pc=pc:
                                     eng.tensor_copy(v, psum[:, pc:pc + 512]))
                            war_bank[pc] = (sV, vev)
                            W("scalar", sV, vev)
                            kst_war[1] = dma(
                                "scalar", dS,
                                cc_kv_in[2048 + ti * 128:2048 + (ti + 1) * 128,
                                         hq * 512:(hq + 1) * 512],
                                vst_ap)
                xt_war[0] = sT.n
            s3_w_done[0] = sT.n
            W("sync", sV, knorm_done)
            v_nrm_w = dma("sync", dQ, cc_nrm_in[:], sc1[0:1, 96:112])
            W("gpsimd", dQ, dQ.n)
            W("gpsimd", dS, dS.n)
            cC.n += 1
            b.emit("gpsimd", lambda eng, s=cC.h: eng.collective_compute(
                "AllGather", ALU.bypass, replica_groups=RG,
                ins=[cc_kv_in[:]], outs=[cc_kv_out[:]]).then_inc(s, 1))
            v_ag_kv = cC.n
            cC.n += 1
            b.emit("gpsimd", lambda eng, s=cC.h: eng.collective_compute(
                "AllGather", ALU.bypass, replica_groups=RG,
                ins=[cc_nrm_in[:]], outs=[cc_nrm_out[:]]).then_inc(s, 1))
            v_ag_nrm = cC.n

            for hh in range(2):
                vxt = 0
                vn8 = ln8(hh, 12288, s3ln_war[0])
                for j in range(8):
                    vxt = tile_to_xT(ar[:, 12288 + j * D:12288 + (j + 1) * D],
                                     j, [(sS, vn8)], war_v=xt_war[0])
                s3ln_war[0] = sT.n
                xt_war[0] = 0
                for h in range(H):
                    wc = ar[:, WC3:WC3 + 2048]
                    W("gpsimd", sT, kq_war[0])
                    vw = dma("gpsimd", dG,
                             wc.rearrange("p (c e) -> p c e", e=128),
                             wq_in[h].rearrange("(c dd) e -> dd c e", dd=128))
                    pc = (h % 2) * 1024
                    W("tensor", dG, vw)
                    W("tensor", sV, vxt)
                    bank_wait("tensor", pc)
                    bank_wait("tensor", pc + 512)
                    for d in range(16):
                        xc = ar[:, XT + d * 1024:XT + (d + 1) * 1024]
                        mm(pc, 512, wc[:, d * 128:(d + 1) * 128],
                           xc[:, 0:512], start=(d == 0), stop=(d == 15),
                           inc=False)
                        vmm = mm(pc + 512, 512, wc[:, d * 128:(d + 1) * 128],
                                 xc[:, 512:1024], start=(d == 0),
                                 stop=(d == 15), inc=(d == 15))
                    kq_war[0] = vmm
                    W("scalar", sT, max(vmm, s3_w_done[0]))
                    qdst = ar[:, QH + h * 2048 + hh * 1024:
                              QH + h * 2048 + hh * 1024 + 1024]
                    vev = op("scalar", sS, lambda eng, dd=qdst, pc=pc, h=h:
                             eng.activation(dd, psum[:, pc:pc + 1024],
                                            AF.Identity,
                                            bias=bq_sb[:, h:h + 1]))
                    war_bank[pc] = (sS, vev)
                    war_bank[pc + 512] = (sS, vev)
                xt_war[0] = sT.n
            v_qfull = sS.n

            for h in range(H):
                qsrc = ar[:, QH + h * 2048:QH + (h + 1) * 2048]
                normmax_to(qsrc, 2048, 2048, 32 + h, h % 2, [(sS, v_qfull)])
            W("sync", cC, v_ag_nrm)
            vn0 = dma("sync", dQ, nrm2[0:1, 0:16], cc_nrm_out[0:1, :])
            vn1 = dma("sync", dQ, nrm2[0:1, 16:32], cc_nrm_out[1:2, :])
            W("vector", dQ, vn1)
            b.drain("vector")
            vnk = op("vector", sV, lambda eng: eng.tensor_tensor(
                nrm2[0:1, 0:16], nrm2[0:1, 0:16], nrm2[0:1, 16:32], ALU.max))
            W("tensor", sV, vnk)
            bank_wait("tensor", 2048)
            vbb = mm(2048, 16, b1row[0:1, 0:128], nrm2[0:1, 0:16], True, True)
            W("vector", sT, vbb)
            b.drain("vector")
            vkb = op("vector", sV, lambda eng: eng.tensor_copy(
                sc1[:, 96:112], psum[:, 2048:2064]))
            war_bank[2048] = (sV, vkb)
            b.drain("vector")
            op("vector", sV, lambda eng: eng.scalar_tensor_tensor(
                sc1[:, 48:64], sc1[:, 32:48], 1.0, sc1[:, 96:112],
                ALU.mult, ALU.mult), inc=False)
            b.drain("vector")
            vpr = op("vector", sV, lambda eng: eng.tensor_scalar(
                sc1[:, 48:64], sc1[:, 48:64], 1e-20, 0.0, ALU.max, ALU.bypass))
            W("scalar", sV, vpr)
            vsq4 = op("scalar", sS, lambda eng: eng.activation(
                sc1[:, 80:96], sc1[:, 48:64], AF.Sqrt))
            W("vector", sS, vsq4)
            b.drain("vector")
            vC_all = op("vector", sV, lambda eng: eng.tensor_scalar(
                negC[:, 0:16], sc1[:, 80:96], -SCALE, 0.0,
                ALU.mult, ALU.bypass))

            kvload_war = [sT.n, sT.n]
            attn_war["p"] = [max(v, sT.n) for v in attn_war["p"]]
            vctx = 0
            it = 0
            for h in range(H):
                for qc in range(4):
                    ext = 4 * qc + 4
                    buf = it % 2
                    base = KVB + buf * 8192
                    it += 1
                    kv_marks = {"base": base}
                    W("sync", cC, v_ag_kv)
                    W("scalar", cC, v_ag_kv)
                    W("sync", sT, kvload_war[buf])
                    W("scalar", sT, kvload_war[buf])
                    for ph in range(2):
                        kv_marks[("k", ph)] = dma(
                            "sync", dQ, ar[:, base + ph * 2048:
                                           base + ph * 2048 + ext * 128],
                            cc_kv_out[ph * 4096 + h * 128:
                                      ph * 4096 + (h + 1) * 128, 0:ext * 128])
                        kv_marks[("v", ph)] = dma(
                            "scalar", dS,
                            ar[:, base + 4096 + ph * 2048:
                               base + 4096 + ph * 2048 + ext * 128],
                            cc_kv_out[ph * 4096 + 2048:ph * 4096 + 2048
                                      + ext * 128, h * 128:(h + 1) * 128]
                            .rearrange("(m tt) e -> tt m e", tt=128))
                    vctx = run_attn_inner("b", 0, h, qc, vC_all, v_qfull,
                                          kv_marks=kv_marks)
                    kvload_war[buf] = sT.n
            vdone = out_proj(
                lambda n_: wo_in.rearrange("h e d -> e h d")
                [:, :, n_ * 512:(n_ + 1) * 512],
                3, list(range(16)), 2048, lambda ti: ti, vctx,
                load_q="gpsimd")
            return vdone

        for gq in range(2):
            v_qh = s1_gq(gq)
        ka_ready[0] = sS.n
        vctx0 = s1_attention(0, sS.n)
        s1_out(0, vctx0)
        for gq in range(2, 4):
            v_qh = s1_gq(gq)
        ka_ready[1] = sS.n
        vctx1 = s1_attention(1, ka_ready[1], filler=emit_ffnwT_part)
        v_ffnwT[0] = ckpt("scalar", dS)
        v_s1 = s1_out(1, vctx1)

        if dbg_stage != 1:
            W("sync", sV, v_s1)
            for r in range(8):
                dma("sync", dQ, masks[:, r * 512:(r + 1) * 512], maskb_in[r])
            v_s2 = ffn_sublayer()
            if dbg_stage != 2:
                v_s3 = s3_sublayer()
                if dbg_stage != 3:
                    v_s4 = ffn_sublayer()

        b.drain("vector")
        vfin = op("vector", sV, lambda eng: eng.memset(sc1[:, 127:128], 0.0))
        dO = mksem("dO")
        for i in range(NT):
            mk = x_mark[i] if (dbg_stage == 0 and x_mark[i] > 0) else vfin
            W("gpsimd", sV, mk)
            b.emit("gpsimd", lambda eng, i=i, s=dO.h: eng.dma_start(
                out=y[i * 128:(i + 1) * 128, :],
                in_=x_sb[:, i * D:(i + 1) * D]).then_inc(s, 16))
            dO.n += 16
        W("gpsimd", dO, dO.n)

        with nc.Block() as block:
            b.run(block)

    return nc


_NC_CACHE = {}


def _get_nc():
    key = DBG_STAGE
    if key not in _NC_CACHE:
        _NC_CACHE[key] = build_nc(DBG_STAGE)
    return _NC_CACHE[key]


def _build_masks(p):
    trikc = (np.arange(128)[:, None] <= np.arange(128)[None, :]).astype(np.float32)
    maskq = np.zeros((8, 128, 512), np.float32)
    for r in range(8):
        for s_ in range(4):
            js_even = (r % 2 == 0)
            gk = r + p if js_even else r - p
            g = 2 * s_ + p
            blk = maskq[r][:, s_ * 128:(s_ + 1) * 128]
            if gk < g:
                blk[:] = 1.0
            elif gk == g:
                blk[:] = trikc
    maskb = np.zeros((8, 128, 512), np.float32)
    for ph in range(2):
        for r in range(4):
            for s_ in range(4):
                gk = 2 * r + ph
                g = 2 * s_ + p
                blk = maskb[ph * 4 + r][:, s_ * 128:(s_ + 1) * 128]
                if gk < g:
                    blk[:] = 1.0
                elif gk == g:
                    blk[:] = trikc
    return (maskq.astype(ml_dtypes.bfloat16), maskb.astype(ml_dtypes.bfloat16))


def kernel(L0_attn_a_x, L0_attn_a_Wqkv_w, L0_attn_a_Wqkv_b, L0_attn_a_out_w,
           L0_attn_a_out_b, L0_attn_b_W_Q, L0_attn_b_W_K, L0_attn_b_W_V,
           L0_attn_b_W_O, L0_attn_b_b_Q, L0_attn_b_b_K, L0_attn_b_b_V,
           L0_attn_b_b_O, L0_ffn_weight, L0_ffn_w_scale, L0_ffn_bias):
    nc = _get_nc()
    x = np.ascontiguousarray(L0_attn_a_x)
    wqkv = np.ascontiguousarray(L0_attn_a_Wqkv_w)
    bqkv = np.ascontiguousarray(L0_attn_a_Wqkv_b)
    outw = np.ascontiguousarray(L0_attn_a_out_w)
    bouta = np.ascontiguousarray(L0_attn_a_out_b)
    wq = np.ascontiguousarray(L0_attn_b_W_Q)
    wk = np.ascontiguousarray(L0_attn_b_W_K)
    wv = np.ascontiguousarray(L0_attn_b_W_V)
    wo = np.ascontiguousarray(L0_attn_b_W_O)
    bQ = np.ascontiguousarray(L0_attn_b_b_Q)
    bK = np.ascontiguousarray(L0_attn_b_b_K)
    bV = np.ascontiguousarray(L0_attn_b_b_V)
    bO = np.ascontiguousarray(L0_attn_b_b_O)
    fw = np.ascontiguousarray(L0_ffn_weight)
    ws = np.ascontiguousarray(L0_ffn_w_scale)
    fb = np.ascontiguousarray(L0_ffn_bias)

    bqkv_f = bqkv.reshape(NEC, 128).T.copy()
    wscale_t = ws[:, 0].reshape(16, 128).T.copy()
    bq_t = bQ.T.copy()
    bk_t = bK.T.copy()
    brows = np.stack([bouta, fb, bV.reshape(-1), bO]).astype(np.float32)
    bva = bqkv[2176:2304].reshape(1, 128).copy()
    ones512 = np.ones((1, 128), np.float32)
    epscol = np.full((128, 1), 1e-5, np.float32)
    ident = np.eye(128, dtype=ml_dtypes.bfloat16)
    onesm = np.ones((128, 128), dtype=ml_dtypes.bfloat16)

    in_maps = []
    for c in range(8):
        bb, p = c // 2, c % 2
        xb = x[bb]
        tiles = xb.reshape(32, 128, D)
        own = tiles[p::2]
        x0own = own.reshape(2048, D).copy()
        order = []
        for i in range(16):
            order.append(2 * i + p)
            order.append(2 * i + (1 - p))
        x0stg = tiles[order].reshape(S, D).copy()
        maskq, maskb = _build_masks(p)
        in_maps.append({
            "x0own": x0own, "x0stg": x0stg, "wqkv": wqkv, "bqkv_f": bqkv_f,
            "outw": outw, "ffnw": fw, "wscale_t": wscale_t,
            "wq": wq, "wk": wk, "wv": wv, "wo": wo,
            "bq_t": bq_t, "bk_t": bk_t, "brows": brows, "bva": bva,
            "ones_f32": ones512, "epscol": epscol, "ident": ident, "onesm": onesm,
            "maskq": maskq, "maskb": maskb,
        })
    res = run_bass_kernel_spmd(nc, in_maps, core_ids=list(range(8)))
    out = np.empty((4, S, D), np.float32)
    for c in range(8):
        bb, p = c // 2, c % 2
        yt = res.results[c]["y"].reshape(16, 128, D)
        ot = out[bb].reshape(32, 128, D)
        ot[p::2] = yt
    return out



# version 15 (final)
# speedup vs baseline: 1.0557x; 1.0045x over previous
import numpy as np
import ml_dtypes
import concourse.bass as bass
import concourse.mybir as mybir
from concourse.bass_utils import run_bass_kernel_spmd

F32 = mybir.dt.float32
BF16 = mybir.dt.bfloat16
F16 = mybir.dt.float16
I8 = mybir.dt.int8
I16 = mybir.dt.int16
AF = mybir.ActivationFunctionType
ALU = mybir.AluOpType

D = 2048
S = 4096
H = 16
NT = 16
E_QKV = 2304
NEC = 18
SCALE = float(1.0 / np.sqrt(128.0))

ENGS = ("sync", "scalar", "vector", "tensor", "gpsimd")
DBG_STAGE = 0


class Sem:
    def __init__(self, h):
        self.h = h
        self.n = 0
        self.wmax = 0
        self.guarded = 0
        self.queue = None


class Bld:
    def __init__(self):
        self.th = {e: [] for e in ENGS}

    def emit(self, e, fn):
        self.th[e].append(fn)

    def wait(self, e, sem, val):
        if val > 0:
            sem.wmax = max(sem.wmax, val)
            self.th[e].append(lambda eng, s=sem.h, v=val: eng.wait_ge(s, v))

    def drain(self, e):
        self.th[e].append(lambda eng: eng.drain())

    def run(self, block):
        for e in ENGS:
            def mk(ths):
                def body(eng):
                    for f in ths:
                        f(eng)
                return body
            getattr(block, e)(mk(self.th[e]))


def attn_tiles(kind, qc):
    out = []
    if kind == "a":
        for js in range(8 * qc + 8):
            msk = js - 8 * qc if js >= 8 * qc else None
            out.append((js, msk))
    else:
        for ph in range(2):
            for m_ in range(4 * qc + 4):
                msk = ph * 4 + (m_ - 4 * qc) if m_ >= 4 * qc else None
                out.append(((ph, m_), msk))
    return out


def build_nc(dbg_stage=0):
    nc = bass.Bass("TRN2", target_bir_lowering=False)

    def din(name, shape, dt):
        return nc.dram_tensor(name, shape, dt, kind="ExternalInput").ap()

    x0own = din("x0own", [2048, D], F32)
    x0stg = din("x0stg", [S, D], F32)
    wqkv = din("wqkv", [E_QKV, D], F32)
    bqkv_f = din("bqkv_f", [128, NEC], F32)
    outw = din("outw", [D, D], F32)
    ffnw = din("ffnw", [D, D], I8)
    wscale_t = din("wscale_t", [128, 16], F32)
    wq_in = din("wq", [H, D, 128], F32)
    wk_in = din("wk", [H, D, 128], F32)
    wv_in = din("wv", [H, D, 128], F32)
    wo_in = din("wo", [H, 128, D], F32)
    bq_t = din("bq_t", [128, H], F32)
    bk_t = din("bk_t", [128, H], F32)
    brows_in = din("brows", [4, D], F32)
    bva_in = din("bva", [1, 128], F32)
    ones_in = din("ones_f32", [1, 128], F32)
    eps_in = din("epscol", [128, 1], F32)
    ident_in = din("ident", [128, 128], BF16)
    onesm_in = din("onesm", [128, 128], BF16)
    maskq_in = din("maskq", [8, 128, 512], BF16)
    maskb_in = din("maskb", [8, 128, 512], BF16)

    y = nc.dram_tensor("y", [2048, D], F32, kind="ExternalOutput").ap()

    cc_kv_in = nc.dram_tensor("cc_kv_in", [4096, D], BF16, kind="Internal").ap()
    cc_kv_out = nc.dram_tensor("cc_kv_out", [8192, D], BF16, kind="Internal",
                               addr_space="Local").ap()
    cc_nrm_in = nc.dram_tensor("cc_nrm_in", [1, H], F32, kind="Internal").ap()
    cc_nrm_out = nc.dram_tensor("cc_nrm_out", [2, H], F32, kind="Internal",
                                addr_space="Local").ap()
    wqkvT_d = nc.dram_tensor("wqkvT_d", [D, E_QKV], BF16, kind="Internal").ap()
    outwT_d = nc.dram_tensor("outwT_d", [D, D], BF16, kind="Internal").ap()
    ffnwT_d = nc.dram_tensor("ffnwT_d", [D, D], BF16, kind="Internal").ap()
    wqb_d = nc.dram_tensor("wqb_d", [H * D, 128], BF16, kind="Internal").ap()
    wkb_d = nc.dram_tensor("wkb_d", [H * D, 128], BF16, kind="Internal").ap()
    wvg_d = nc.dram_tensor("wvg_d", [D, D], BF16, kind="Internal").ap()
    wob_d = nc.dram_tensor("wob_d", [H * 128, D], BF16, kind="Internal").ap()
    RG = [[0, 1], [2, 3], [4, 5], [6, 7]]

    import contextlib
    with contextlib.ExitStack() as stk:
        def sbuf(name, shape, dt):
            return stk.enter_context(nc.sbuf_tensor(name, shape, dt))

        def mksem(name):
            return Sem(stk.enter_context(nc.semaphore(name)))

        x_sb = sbuf("x_sb", [128, NT * D], BF16)
        masks = sbuf("masksb", [128, 8 * 512], BF16)
        ident = sbuf("identb", [128, 128], BF16)
        onesm = sbuf("onesmb", [128, 128], BF16)
        b1row = sbuf("b1row", [1, 2304], F32)
        bq_sb = sbuf("bq_sbb", [128, H], F32)
        bk_sb = sbuf("bk_sbb", [128, H], F32)
        consts = sbuf("constsb", [128, 40], F32)
        sc1 = sbuf("sc1b", [128, 128], F32)
        sc2 = sbuf("sc2b", [128, 192], F32)
        nrm2 = sbuf("nrm2b", [1, 32], F32)
        negC = sbuf("negCb", [128, H], F32)
        fp16t = sbuf("fp16tb", [128, 512], F16)
        i16t = sbuf("i16tb", [128, 1024], I16)
        i8st = sbuf("i8stb", [128, D], I8)
        dscr = sbuf("dscrb", [128, 1024], F32)
        sqs = sbuf("sqsb", [128, D], BF16)
        AR = 56832
        ar = sbuf("arena", [128, AR], BF16)
        QH = 0
        WCOL = 16384
        PSL = 16384
        WOUT = 16384
        XHO = 20480
        VA = 24576
        KA = 28672
        XT = 32768
        KVB = 32768
        PSL3 = 49152
        XHI = 49152
        SM3 = 49152
        WC3 = 53248
        KST = 55296

        psum = stk.enter_context(nc.psum_tensor("psumT", [128, 3072], F32))
        psT = stk.enter_context(nc.psum_tensor("psumB", [128, 2048], BF16))

        sT = mksem("sT")
        sV = mksem("sV")
        sS = mksem("sS")
        dQ = mksem("dQ")
        dS = mksem("dS2")
        dV = mksem("dV2")
        dG = mksem("dG2")
        cC = mksem("cC")

        b = Bld()

        def dma(q, qsem, out, in_):
            qsem.queue = q
            if qsem.wmax > qsem.guarded:
                g = qsem.wmax
                b.th[q].append(lambda eng, s=qsem.h, v=g: eng.wait_ge(s, v))
                qsem.guarded = g
            b.emit(q, lambda eng, o=out, i=in_, s=qsem.h:
                   eng.dma_start(out=o, in_=i).then_inc(s, 16))
            qsem.n += 16
            return qsem.n

        def op(e, esem, f, inc=True):
            if inc:
                esem.n += 1
                b.emit(e, lambda eng, f=f, s=esem.h: f(eng).then_inc(s, 1))
                return esem.n
            b.emit(e, lambda eng, f=f: f(eng))
            return esem.n

        W = b.wait

        b1_war = [0]

        def premark(qsem):
            qsem.wmax = max(qsem.wmax, qsem.n)

        def ckpt(qname, qsem):
            v = qsem.n
            if v > qsem.guarded:
                b.th[qname].append(lambda eng, s=qsem.h, vv=v: eng.wait_ge(s, vv))
                qsem.guarded = v
            qsem.wmax = max(qsem.wmax, v)
            return v

        def mm(col, w, lhsT, rhs, start, stop, inc=None):
            if inc is None:
                inc = stop
            return op("tensor", sT,
                      lambda eng, c=col, w=w, l=lhsT, r=rhs, st=start, sp=stop:
                      eng.matmul(psum[:, c:c + w], l, r, start=st, stop=sp),
                      inc=inc)

        def biasmm(col, w, rhs_bias, stop=True, inc=None):
            if inc is None:
                inc = stop
            v = op("tensor", sT,
                   lambda eng, c=col, w=w, r=rhs_bias:
                   eng.matmul(psum[:, c:c + w], b1row[0:1, 0:128], r,
                              start=False, stop=stop), inc=inc)
            b1_war[0] = max(b1_war[0], v)
            return v

        dma("sync", dQ, ident[:], ident_in[:])
        dma("sync", dQ, onesm[:], onesm_in[:])
        dma("sync", dQ, b1row[0:1, 0:128], ones_in[:])
        for r in range(8):
            dma("sync", dQ, masks[:, r * 512:(r + 1) * 512], maskq_in[r])
        dma("sync", dQ, consts[:, 0:NEC], bqkv_f[:])
        dma("sync", dQ, consts[:, 18:34], wscale_t[:])
        dma("sync", dQ, bq_sb[:], bq_t[:])
        dma("sync", dQ, bk_sb[:], bk_t[:])
        dma("sync", dQ, consts[:, 34:35], eps_in[:])
        v_consts = ckpt("sync", dQ)
        for e_ in ("tensor", "vector", "scalar", "gpsimd"):
            W(e_, dQ, v_consts)
        dma("sync", dQ, b1row[0:1, 128:256], bva_in[:])
        v_bva = ckpt("sync", dQ)
        for i in range(NT):
            dma("gpsimd", dG, x_sb[:, i * D:(i + 1) * D],
                x0own[i * 128:(i + 1) * 128, :])
        v_xown = ckpt("gpsimd", dG)

        RAW = [0, 2304]
        EVA = [4608, 6912]
        war_raw = [0, 0]
        war_i8 = [0, 0]
        war_eva = [0, 0]
        war_bank = {}

        def bank_wait(e, col):
            if col in war_bank:
                s_, v_ = war_bank[col]
                W(e, s_, v_)

        def prep_chunk(src_row_ap, dst_ap, ncols_c, sl, load_q, load_sem,
                       scale_col=None, rawbase=None, evabase=None):
            rb_ = RAW[sl] if rawbase is None else rawbase[sl]
            eb_ = EVA[sl] if evabase is None else evabase[sl]
            raw = ar[:, rb_:rb_ + ncols_c * 128]
            eva = ar[:, eb_:eb_ + ncols_c * 128]
            if scale_col is None:
                W(load_q, sT, war_raw[sl])
                vld = dma(load_q, load_sem, raw, src_row_ap)
                vsc = None
            else:
                io = (0 if sl == 0 else 1024)
                W(load_q, sS, war_i8[sl])
                vld = dma(load_q, load_sem,
                          i8st[:, io:io + ncols_c * 128], src_row_ap)
                W("scalar", load_sem, vld)
                W("scalar", sT, war_raw[sl])
                vsc = op("scalar", sS, lambda eng, raw=raw, c=scale_col, io=io,
                         n=ncols_c * 128:
                         eng.activation(raw, i8st[:, io:io + n], AF.Identity,
                                        scale=consts[:, c:c + 1]))
            vv_last = 0
            for g in range((ncols_c + 3) // 4):
                bank = g % 2
                n_in = min(4, ncols_c - 4 * g)
                if vsc is None:
                    W("tensor", load_sem, vld)
                else:
                    W("tensor", sS, vsc)
                bank_wait("tensor", ("t", bank))
                for k in range(n_in):
                    c = 4 * g + k
                    vt = op("tensor", sT, lambda eng, raw=raw, c=c, bank=bank, k=k:
                            eng.transpose(psT[:, bank * 1024 + k * 128:
                                              bank * 1024 + (k + 1) * 128],
                                          raw[:, c * 128:(c + 1) * 128], ident[:]),
                            inc=(k == n_in - 1))
                war_raw[sl] = vt
                if scale_col is not None:
                    war_i8[sl] = vsc
                W("vector", sT, vt)
                W("vector", dS, dS.n)
                for k in range(n_in):
                    c = 4 * g + k
                    vv = op("vector", sV, lambda eng, eva=eva, c=c, bank=bank, k=k:
                            eng.tensor_copy(eva[:, c * 128:(c + 1) * 128],
                                            psT[:, bank * 1024 + k * 128:
                                                bank * 1024 + (k + 1) * 128]),
                            inc=(k == n_in - 1))
                war_bank[("t", bank)] = (sV, vv)
                vv_last = vv
            W("scalar", sV, vv_last)
            war_eva[sl] = dma("scalar", dS, dst_ap,
                              eva.rearrange("p (c e) -> p c e", e=128))
            return war_eva[sl]

        for rb in range(NEC):
            prep_chunk(wqkv[rb * 128:(rb + 1) * 128, :],
                       wqkvT_d.rearrange("(c dd) e -> dd c e", dd=128)
                       [:, :, rb * 128:(rb + 1) * 128], 16, rb % 2, "gpsimd", dG)
        v_wqkvT = ckpt("scalar", dS)
        for rb in range(16):
            prep_chunk(outw[rb * 128:(rb + 1) * 128, :],
                       outwT_d.rearrange("(c dd) e -> dd c e", dd=128)
                       [:, :, rb * 128:(rb + 1) * 128], 16, rb % 2, "gpsimd", dG)
        v_outwT = ckpt("scalar", dS)
        v_wprep = [0]
        v_ffnwT = [0]
        late_done = [False]

        def emit_late_preps():
            if late_done[0]:
                return
            late_done[0] = True
        v_prep_arena_free = 0

        ffnwT_started = [False]

        def emit_ffnwT_part(i):
            if not ffnwT_started[0]:
                ffnwT_started[0] = True
                W("sync", sS, sS.n)
                W("sync", sV, sV.n)
                W("sync", dQ, v_consts)
            rb2 = [XHI, XHI + 1024]
            eb2 = [XHI + 2048, XHI + 3072]
            rb, hb = i // 2, i % 2
            prep_chunk(ffnw[rb * 128:(rb + 1) * 128,
                            hb * 1024:(hb + 1) * 1024],
                       ffnwT_d[hb * 1024:, :]
                       .rearrange("(c dd) e -> dd c e", dd=128)
                       [:, 0:8, rb * 128:(rb + 1) * 128],
                       8, i % 2, "sync", dQ,
                       scale_col=18 + rb, rawbase=rb2, evabase=eb2)

        def layernorm_tile(src, dst, sl):
            for c in range(4):
                op("vector", sV, lambda eng, s=src, sl=sl, c=c: eng.bn_stats(
                    sc2[:, 24 * sl + 6 * c:24 * sl + 6 * c + 6],
                    s[:, c * 512:(c + 1) * 512]), inc=False)
            b.drain("vector")
            vst = op("vector", sV, lambda eng, sl=sl: eng.bn_aggr(
                sc1[:, 2 * sl:2 * sl + 2], sc2[:, 24 * sl:24 * sl + 24]))
            W("scalar", sV, vst)
            sst = op("scalar", sS, lambda eng, sl=sl: eng.activation(
                sc1[:, 16 + sl:17 + sl], sc1[:, 2 * sl + 1:2 * sl + 2],
                AF.Sqrt, bias=consts[:, 34:35]))
            W("vector", sS, sst)
            op("vector", sV, lambda eng, sl=sl: eng.reciprocal(
                sc1[:, 24 + sl:25 + sl], sc1[:, 16 + sl:17 + sl]), inc=False)
            b.drain("vector")
            vr = op("vector", sV, lambda eng, sl=sl: eng.tensor_scalar(
                sc1[:, 32 + sl:33 + sl], sc1[:, 2 * sl:2 * sl + 1],
                sc1[:, 24 + sl:25 + sl], -1.0, ALU.mult, ALU.mult))
            W("scalar", sV, vr)
            return op("scalar", sS, lambda eng, s=src, d=dst, sl=sl: eng.activation(
                d, s, AF.Identity, bias=sc1[:, 32 + sl:33 + sl],
                scale=sc1[:, 24 + sl:25 + sl]))

        def tile_to_xT(src, tcol, ready_pairs, war_v=0):
            vv_last = 0
            for g in range(4):
                bank = g % 2
                for (s_, v_) in ready_pairs:
                    W("tensor", s_, v_)
                bank_wait("tensor", ("t", bank))
                for k in range(4):
                    c = 4 * g + k
                    vt = op("tensor", sT, lambda eng, s=src, c=c, bank=bank, k=k:
                            eng.transpose(psT[:, bank * 1024 + k * 128:
                                              bank * 1024 + (k + 1) * 128],
                                          s[:, c * 128:(c + 1) * 128], ident[:]),
                            inc=(k == 3))
                W("vector", sT, max(vt, war_v))
                for k in range(4):
                    c = 4 * g + k
                    vv = op("vector", sV, lambda eng, c=c, bank=bank, k=k, t=tcol:
                            eng.tensor_copy(
                                ar[:, XT + c * 1024 + t * 128:
                                   XT + c * 1024 + (t + 1) * 128],
                                psT[:, bank * 1024 + k * 128:
                                    bank * 1024 + (k + 1) * 128]),
                            inc=(k == 3))
                war_bank[("t", bank)] = (sV, vv)
                vv_last = vv
            return vv_last

        def normmax(src, ncols, pcol, out_col, extra_sq_wait=()):
            nch = ncols // 512
            cdone = 0
            for big in range(0, ncols, 2048):
                bw = min(2048, ncols - big)
                for (s_, v_) in extra_sq_wait:
                    W("scalar", s_, v_)
                W("scalar", sT, sqs_war[0])
                vsq = op("scalar", sS, lambda eng, s=src, b0=big, bw=bw:
                         eng.activation(sqs[:, 0:bw], s[:, b0:b0 + bw], AF.Square))
                for cc2 in range(bw // 512):
                    cch = cdone
                    cdone += 1
                    W("tensor", sS, vsq)
                    bank_wait("tensor", pcol)
                    vmm = mm(pcol, 512, onesm[:],
                             sqs[:, cc2 * 512:(cc2 + 1) * 512], True, True)
                    sqs_war[0] = vmm
                    W("vector", sT, vmm)
                    b.drain("vector")
                    vrd = op("vector", sV,
                             lambda eng, cch=cch, pc=pcol, oc=out_col:
                             eng.reduce_max(
                                 sc1[:, 64 + oc * 8 + cch:65 + oc * 8 + cch],
                                 psum[:, pc:pc + 512],
                                 axis=mybir.AxisListType.X))
                    war_bank[pcol] = (sV, vrd)
            b.drain("vector")
            vfin = op("vector", sV, lambda eng, oc=out_col, n=nch:
                      eng.tensor_reduce(sc1[:, 40 + oc:41 + oc],
                                        sc1[:, 64 + oc * 8:64 + oc * 8 + n],
                                        axis=mybir.AxisListType.X, op=ALU.max))
            return 40 + out_col, vfin

        xhi_war = [0, 0]
        xho_war = [0, 0]
        wcol_war = [0, 0]
        v_out_a = [0, 0]

        def s1_gq(gq):
            hh = gq // 2
            vxt = 0
            for j in range(8):
                js = gq * 8 + j
                sl = js % 2
                xin = ar[:, XHI + sl * D:XHI + sl * D + D]
                W("gpsimd", sS, xhi_war[sl])
                vld = dma("gpsimd", dG, xin, x0stg[js * 128:(js + 1) * 128, :])
                W("vector", dG, vld)
                xout = ar[:, XHO + sl * D:XHO + sl * D + D]
                W("vector", sT, max(xho_war[sl], war_wout[0]))
                W("scalar", sT, max(xho_war[sl], war_wout[0]))
                vn = layernorm_tile(xin, xout, js % 8)
                xhi_war[sl] = vn
                vxt = tile_to_xT(xout, j, [(sS, vn)])
                xho_war[sl] = sT.n
            for e in range(16):
                wsl = e % 2
                wc = ar[:, WCOL + wsl * 2048:WCOL + wsl * 2048 + 2048]
                W("sync", sT, max(wcol_war[wsl], war_wout[0]))
                W("sync", dS, v_wqkvT)
                vw = dma("sync", dQ, wc.rearrange("p (c e) -> p c e", e=128),
                         wqkvT_d.rearrange("(c dd) e -> dd c e", dd=128)
                         [:, :, e * 128:(e + 1) * 128])
                pc = (e % 2) * 512
                W("tensor", dQ, vw)
                W("tensor", sV, vxt)
                bank_wait("tensor", pc)
                for d in range(16):
                    xc = ar[:, XT + d * 1024:XT + (d + 1) * 1024]
                    rhs = xc.rearrange("p (a c) -> p a c", c=128)[:, 0:8:2, :]
                    vmm = mm(pc, 512, wc[:, d * 128:(d + 1) * 128], rhs,
                             start=(d == 0), stop=(d == 15))
                wcol_war[wsl] = vmm
                W("scalar", sT, vmm)
                if gq < 2 and e <= 8:
                    W("scalar", dQ, v_prep_arena_free)
                W("scalar", sT, war_wout[0])
                qdst = ar[:, QH + e * 1024 + (gq % 2) * 512:
                          QH + e * 1024 + (gq % 2) * 512 + 512]
                vev = op("scalar", sS, lambda eng, dd=qdst, pc=pc, e=e:
                         eng.activation(dd, psum[:, pc:pc + 512], AF.Identity,
                                        bias=consts[:, e:e + 1]))
                war_bank[pc] = (sS, vev)
            v_qh = sS.n
            wc = ar[:, WCOL:WCOL + 2048]
            W("sync", sT, max(wcol_war[0], war_wout[0]))
            W("sync", dS, v_wqkvT)
            vw = dma("sync", dQ, wc.rearrange("p (c e) -> p c e", e=128),
                     wqkvT_d.rearrange("(c dd) e -> dd c e", dd=128)
                     [:, :, 16 * 128:17 * 128])
            W("tensor", dQ, vw)
            W("tensor", sV, vxt)
            bank_wait("tensor", 1024)
            bank_wait("tensor", 1536)
            for d in range(16):
                xc = ar[:, XT + d * 1024:XT + (d + 1) * 1024]
                mm(1024, 512, wc[:, d * 128:(d + 1) * 128], xc[:, 0:512],
                   start=(d == 0), stop=(d == 15), inc=False)
                vmm = mm(1536, 512, wc[:, d * 128:(d + 1) * 128], xc[:, 512:1024],
                         start=(d == 0), stop=(d == 15), inc=(d == 15))
            wcol_war[0] = vmm
            W("scalar", sT, vmm)
            vk = op("scalar", sS, lambda eng, gq=gq: eng.activation(
                ar[:, KA + gq * 1024:KA + (gq + 1) * 1024],
                psum[:, 1024:2048], AF.Identity, bias=consts[:, 16:17]))
            war_bank[1024] = (sS, vk)
            war_bank[1536] = (sS, vk)
            wc2 = ar[:, WCOL + 2048:WCOL + 4096]
            W("sync", sT, max(wcol_war[1], war_wout[0]))
            vw2 = dma("sync", dQ, wc2.rearrange("p (c e) -> p c e", e=128),
                      wqkvT_d.rearrange("(c dd) e -> dd c e", dd=128)
                      [:, :, 17 * 128:18 * 128])
            W("tensor", dQ, vw2)
            W("tensor", dQ, v_bva)
            for tt in range(8):
                pc = 2048 + (tt % 2) * 512
                bank_wait("tensor", pc)
                for d in range(16):
                    xc = ar[:, XT + d * 1024:XT + (d + 1) * 1024]
                    mm(pc, 128, xc[:, tt * 128:(tt + 1) * 128],
                       wc2[:, d * 128:(d + 1) * 128],
                       start=(d == 0), stop=False, inc=False)
                vb = biasmm(pc, 128, b1row[0:1, 128:256], stop=True)
                wcol_war[1] = vb
                W("vector", sT, vb)
                vva = op("vector", sV, lambda eng, gq=gq, tt=tt, pc=pc:
                         eng.tensor_copy(
                             ar[:, VA + (gq * 8 + tt) * 128:
                                VA + (gq * 8 + tt + 1) * 128],
                             psum[:, pc:pc + 128]))
                war_bank[pc] = (sV, vva)
            return v_qh

        ka_ready = {}
        sqs_war = [0]
        attn_war = {"p": [0] * 8, "s": [0] * 4}

        def normmax_to(src_ap, ncols, pcol, dst_col, sq_slot, wait_pairs):
            nch = ncols // 512
            cdone = 0
            red_marks = []
            for big in range(0, ncols, 1024):
                bw = min(1024, ncols - big)
                for (s_, v_) in wait_pairs:
                    W("scalar", s_, v_)
                W("scalar", sT, sqs_war[0])
                W("scalar", sV, sqs_rd[sq_slot])
                vsq = op("scalar", sS, lambda eng, s=src_ap, b0=big, bw=bw,
                         o=sq_slot * 1024:
                         eng.activation(sqs[:, o:o + bw], s[:, b0:b0 + bw],
                                        AF.Square))
                for cc2 in range(bw // 512):
                    cch = cdone
                    cdone += 1
                    W("tensor", sS, vsq)
                    bank_wait("tensor", pcol)
                    vmm = mm(pcol, 512, onesm[:],
                             sqs[:, sq_slot * 1024 + cc2 * 512:
                                 sq_slot * 1024 + (cc2 + 1) * 512], True, True)
                    sqs_war[0] = vmm
                    sqs_rd[sq_slot] = 0
                    W("vector", sT, vmm)
                    b.drain("vector")
                    vrd = op("vector", sV, lambda eng, cch=cch, pc=pcol,
                             dc=64 + sq_slot * 8 + cch % 8:
                             eng.reduce_max(sc1[:, dc:dc + 1],
                                            psum[:, pc:pc + 512],
                                            axis=mybir.AxisListType.X))
                    war_bank[pcol] = (sV, vrd)
                    red_marks.append(vrd)
            b.drain("vector")
            vfin = op("vector", sV, lambda eng, dc=dst_col, n=nch,
                      o=64 + sq_slot * 8:
                      eng.tensor_reduce(sc1[:, dc:dc + 1], sc1[:, o:o + n],
                                        axis=mybir.AxisListType.X, op=ALU.max))
            return vfin

        sqs_rd = [0, 0]

        def s1_attention(hh, v_qh, filler=None):
            attn_war["p"] = [max(v, war_wout[0]) for v in attn_war["p"]]
            ksrc = ar[:, KA:KA + 16 * (hh + 1) * 128]
            vnk = normmax_to(ksrc, 16 * (hh + 1) * 128, 512, 96, 0,
                             [(sS, ka_ready[hh])])
            for h in range(H):
                qsrc = ar[:, QH + h * 1024:QH + (h + 1) * 1024]
                vq_ = normmax_to(qsrc, 1024, 512, 100 + h, h % 2,
                                 [(sS, v_qh)])
            b.drain("vector")
            for h in range(H):
                op("vector", sV, lambda eng, h=h: eng.tensor_scalar(
                    sc1[:, 100 + h:101 + h], sc1[:, 100 + h:101 + h],
                    sc1[:, 96:97], 1e-20, ALU.mult, ALU.max), inc=False)
            b.drain("vector")
            vpr = op("vector", sV, lambda eng: eng.memset(sc1[:, 127:128], 0.0))
            W("scalar", sV, vpr)
            vsq = op("scalar", sS, lambda eng: eng.activation(
                sc1[:, 80:96], sc1[:, 100:116], AF.Sqrt))
            W("vector", sS, vsq)
            b.drain("vector")
            vC = op("vector", sV, lambda eng: eng.tensor_scalar(
                negC[:, 0:16], sc1[:, 80:96], -SCALE, 0.0,
                ALU.mult, ALU.bypass))
            vctx = 0
            for h in range(H):
                if filler is not None:
                    filler(2 * h)
                    filler(2 * h + 1)
                for qc in (2 * hh, 2 * hh + 1):
                    vctx = run_attn_inner("a", hh, h, qc, vC, v_qh)
            return vctx

        def run_attn_inner(kind, hh, h, qc, vC, v_qsrc, kv_marks=None):
            tiles = attn_tiles(kind, qc)
            qbase = QH + h * (1024 if kind == "a" else 2048)
            qoff = (qc - 2 * hh) * 512 if kind == "a" else qc * 512
            qap = ar[:, qbase + qoff:qbase + qoff + 512]
            psl = PSL if kind == "a" else PSL3
            pslot_war = attn_war["p"]
            sbank_war = attn_war["s"]
            n = len(tiles)
            def kap_of(src):
                if kind == "a":
                    return ar[:, KA + src * 128:KA + (src + 1) * 128]
                ph, m_ = src
                return kv_buf_ap("k", ph, m_, kv_marks.get("base"))

            def vap_of(src):
                if kind == "a":
                    return ar[:, VA + src * 128:VA + (src + 1) * 128]
                ph, m_ = src
                return kv_buf_ap("v", ph, m_, kv_marks.get("base"))

            vs_marks = [0] * n
            v_avlast = 0
            for ii in range(n):
                src, msk = tiles[ii]
                sb = 1024 + (ii % 4) * 512
                W("tensor", sS, sbank_war[ii % 4])
                if kind == "a":
                    W("tensor", sS, v_qsrc)
                else:
                    W("tensor", dQ, max(kv_marks[("k", 0)], kv_marks[("k", 1)]))
                vs_marks[ii] = mm(sb, 512, kap_of(src), qap, True, True)
                if ii % 2 == 1 or ii == n - 1:
                    lo = ii - (ii % 2)
                    wdt = 512 * (ii - lo + 1)
                    pap = ar[:, psl + (lo % 8) * 512:psl + (lo % 8) * 512 + wdt]
                    W("scalar", sT, vs_marks[ii])
                    W("scalar", sT, max(pslot_war[lo % 8], pslot_war[ii % 8]))
                    W("scalar", sV, vC)
                    ve = op("scalar", sS, lambda eng, p=pap, sb=1024 + (lo % 4) * 512,
                            wdt=wdt, h=h:
                            eng.activation(p, psum[:, sb:sb + wdt], AF.Exp,
                                           bias=negC[:, h:h + 1], scale=SCALE))
                    for jj in range(lo, ii + 1):
                        sbank_war[jj % 4] = ve
                    for jj in range(lo, ii + 1):
                        src2, msk2 = tiles[jj]
                        pap1 = ar[:, psl + (jj % 8) * 512:psl + (jj % 8) * 512 + 512]
                        if msk2 is not None:
                            W("vector", sS, ve)
                            vmk = op("vector", sV, lambda eng, p=pap1, r=msk2:
                                     eng.tensor_tensor(
                                         p, p, masks[:, r * 512:(r + 1) * 512],
                                         ALU.mult))
                            W("tensor", sV, vmk)
                        else:
                            W("tensor", sS, ve)
                        st = (jj == 0)
                        if st:
                            bank_wait("tensor", 512)
                            bank_wait("tensor", 0)
                        last = (jj == n - 1)
                        mm(512, 512, onesm[:], pap1, st, last, inc=False)
                        if kind == "b":
                            W("tensor", dG, max(kv_marks[("v", 0)],
                                                kv_marks[("v", 1)]))
                        vav = mm(0, 512, vap_of(src2), pap1, st, last, inc=True)
                        pslot_war[jj % 8] = vav
                        if last:
                            v_avlast = vav
            W("vector", sT, v_avlast)
            op("vector", sV, lambda eng: eng.reciprocal(
                dscr[:, 0:512], psum[:, 512:1024]), inc=False)
            b.drain("vector")
            cdst = ar[:, qbase + qoff:qbase + qoff + 512]
            vnorm = op("vector", sV, lambda eng, cd=cdst:
                       eng.tensor_tensor(cd, psum[:, 0:512], dscr[:, 0:512],
                                         ALU.mult))
            war_bank[0] = (sV, vnorm)
            war_bank[512] = (sV, vnorm)
            return vnorm

        def kv_buf_ap(which, ph, m_, kvbase=None):
            base = (KVB if kvbase is None else kvbase) + (0 if which == "k" else 4096)
            return ar[:, base + ph * 2048 + m_ * 128:
                      base + ph * 2048 + (m_ + 1) * 128]

        war_wout = [0]

        def out_proj(wdram_fn, brow_idx, tiles_, qstride, tcol_fn, vctx,
                     vx_extra=None, wready=0, load_q="sync", dbuf=False):
            war_wout[0] = max(war_wout[0], sT.n)
            wout2_war = [war_wout[0], war_wout[0]]
            lsem = dQ if load_q == "sync" else dG
            W("sync", sT, b1_war[0])
            vload_b = dma("sync", dQ, b1row[0:1, 256:2304],
                          brows_in[brow_idx:brow_idx + 1, :])
            premark(dQ)
            for n_ in range(4):
                base_w = WOUT + (8192 * (n_ % 2) if dbuf else 0)
                wsl_ap = ar[:, base_w:base_w + 8192]
                W(load_q, sT, wout2_war[n_ % 2] if dbuf else war_wout[0])
                W(load_q, dS, wready)
                vw = dma(load_q, lsem, wsl_ap.rearrange("p (c e) -> p c e", e=512),
                         wdram_fn(n_))
                for ji, ti in enumerate(tiles_):
                    pc = (ji % 4) * 512
                    W("tensor", lsem, vw)
                    W("tensor", sV, vctx)
                    W("tensor", dQ, vload_b)
                    bank_wait("tensor", pc)
                    for c in range(16):
                        mm(pc, 512, ar[:, QH + c * qstride + tcol_fn(ti) * 128:
                                       QH + c * qstride + (tcol_fn(ti) + 1) * 128],
                           wsl_ap[:, c * 512:(c + 1) * 512],
                           start=(c == 0), stop=False, inc=False)
                    vb = biasmm(pc, 512,
                                b1row[0:1, 256 + n_ * 512:768 + n_ * 512])
                    W("vector", sT, vb)
                    if vx_extra is not None:
                        W("vector", vx_extra[0], vx_extra[1])
                    xsl = x_sb[:, ti * D + n_ * 512:ti * D + (n_ + 1) * 512]
                    vep = op("vector", sV, lambda eng, xs=xsl, pc=pc:
                             eng.tensor_tensor(xs, xs, psum[:, pc:pc + 512],
                                               ALU.add))
                    war_bank[pc] = (sV, vep)
                war_wout[0] = sT.n
                wout2_war[n_ % 2] = sT.n
            return sV.n

        def s1_out(hh, vctx):
            return out_proj(
                lambda n_: outwT_d.rearrange("(c dd) e -> dd c e", dd=128)
                [:, :, n_ * 512:(n_ + 1) * 512],
                0, list(range(8 * hh, 8 * hh + 8)), 1024,
                lambda ti: ti - 8 * hh, vctx, wready=v_outwT,
                dbuf=(hh == 1))

        fp16_war = [0]
        x_mark = [0] * NT

        def ffn_mm(hh, vxt):
            war_wout[0] = max(war_wout[0], sT.n)
            W("sync", sT, b1_war[0])
            vload_b = dma("sync", dQ, b1row[0:1, 256:2304],
                          brows_in[1:2, :])
            premark(dQ)
            wf2_war = [war_wout[0], war_wout[0]]
            for n_ in range(4):
                base_w = WOUT + 8192 * (n_ % 2)
                wsl_ap = ar[:, base_w:base_w + 8192]
                W("sync", sT, wf2_war[n_ % 2])
                W("sync", dS, v_ffnwT[0])
                vw = dma("sync", dQ, wsl_ap.rearrange("p (c e) -> p c e", e=512),
                         ffnwT_d.rearrange("(c dd) e -> dd c e", dd=128)
                         [:, :, n_ * 512:(n_ + 1) * 512])
                for ji in range(8):
                    ti = 8 * hh + ji
                    pc = (ji % 4) * 512
                    W("tensor", dQ, vw)
                    W("tensor", sV, vxt)
                    W("tensor", dQ, vload_b)
                    bank_wait("tensor", pc)
                    for c in range(16):
                        mm(pc, 512, ar[:, XT + c * 1024 + ji * 128:
                                       XT + c * 1024 + (ji + 1) * 128],
                           wsl_ap[:, c * 512:(c + 1) * 512],
                           start=(c == 0), stop=False, inc=False)
                    vb = biasmm(pc, 512,
                                b1row[0:1, 256 + n_ * 512:768 + n_ * 512])
                    W("scalar", sT, vb)
                    W("scalar", sV, fp16_war[0])
                    vf = op("scalar", sS, lambda eng, pc=pc: eng.copy(
                        fp16t[:, 0:512], psum[:, pc:pc + 512]))
                    W("vector", sS, vf)
                    xsl = x_sb[:, ti * D + n_ * 512:ti * D + (n_ + 1) * 512]
                    vep = op("vector", sV, lambda eng, xs=xsl: eng.tensor_tensor(
                        xs, xs, fp16t[:, 0:512], ALU.add))
                    fp16_war[0] = vep
                    x_mark[ti] = vep
                    war_bank[pc] = (sV, vep)
                war_wout[0] = sT.n
                wf2_war[n_ % 2] = sT.n
            return sV.n

        quant_war = {"xh": [0, 0], "i16": 0}

        def quant_tile(src, dst, sl):
            W("scalar", sV, quant_war["xh"][sl % 2])
            vn = layernorm_tile(src, ar[:, SM3 + (sl % 2) * D:
                                        SM3 + (sl % 2) * D + D], sl)
            xh = ar[:, SM3 + (sl % 2) * D:SM3 + (sl % 2) * D + D]
            W("vector", sS, vn)
            vam = op("vector", sV, lambda eng, x=xh, sl=sl: eng.tensor_reduce(
                sc1[:, 56 + sl:57 + sl], x, axis=mybir.AxisListType.X,
                op=ALU.max, apply_absolute_value=True))
            b.drain("vector")
            op("vector", sV, lambda eng, sl=sl: eng.tensor_scalar(
                sc1[:, 56 + sl:57 + sl], sc1[:, 56 + sl:57 + sl], 1e-7, 0.0,
                ALU.max, ALU.bypass), inc=False)
            b.drain("vector")
            op("vector", sV, lambda eng, sl=sl: eng.reciprocal(
                sc1[:, 48 + sl:49 + sl], sc1[:, 56 + sl:57 + sl]), inc=False)
            b.drain("vector")
            vsq_ = op("vector", sV, lambda eng, sl=sl: eng.tensor_scalar(
                sc1[:, 48 + sl:49 + sl], sc1[:, 48 + sl:49 + sl], 127.0, 0.0,
                ALU.mult, ALU.bypass))
            b.drain("vector")
            vnis = op("vector", sV, lambda eng, sl=sl: eng.tensor_scalar(
                sc1[:, 56 + sl:57 + sl], sc1[:, 56 + sl:57 + sl],
                float(-1.0 / 127.0), 0.0, ALU.mult, ALU.bypass))
            vq = 0
            for hb in range(2):
                xhh = xh[:, hb * 1024:(hb + 1) * 1024]
                dsth = dst[:, hb * 1024:(hb + 1) * 1024]
                W("scalar", sV, vsq_ if hb == 0 else vq)
                W("scalar", sV, quant_war["i16"])
                vi = op("scalar", sS, lambda eng, x=xhh, sl=sl: eng.activation(
                    i16t[:], x, AF.Identity, scale=sc1[:, 48 + sl:49 + sl]))
                W("vector", sS, vi)
                op("vector", sV, lambda eng, x=xhh, sl=sl:
                   eng.scalar_tensor_tensor(
                       dscr[:], x, sc1[:, 48 + sl:49 + sl], i16t[:],
                       ALU.mult, ALU.subtract), inc=False)
                b.drain("vector")
                op("vector", sV, lambda eng: eng.scalar_tensor_tensor(
                    sqs[:, 0:1024], dscr[:], 0.5, i16t[:], ALU.is_ge, ALU.add),
                   inc=False)
                b.drain("vector")
                op("vector", sV, lambda eng: eng.scalar_tensor_tensor(
                    i16t[:], dscr[:], -0.5, sqs[:, 0:1024], ALU.is_le,
                    ALU.subtract), inc=False)
                b.drain("vector")
                vq = op("vector", sV, lambda eng, d=dsth, sl=sl:
                        eng.tensor_scalar(d, i16t[:], sc1[:, 56 + sl:57 + sl],
                                          0.0, ALU.mult, ALU.bypass))
                quant_war["i16"] = vq
            quant_war["xh"][sl % 2] = vq
            return vq

        xt_war = [0]

        xq_war = [0]

        def ln8(hh, base, war_mark):
            W("vector", sT, war_mark)
            W("scalar", sT, war_mark)
            for j in range(8):
                src_ = x_sb[:, (8 * hh + j) * D:(8 * hh + j + 1) * D]
                for c in range(4):
                    op("vector", sV, lambda eng, s=src_, j=j, c=c: eng.bn_stats(
                        sc2[:, 24 * j + 6 * c:24 * j + 6 * c + 6],
                        s[:, c * 512:(c + 1) * 512]), inc=False)
            b.drain("vector")
            for j in range(8):
                op("vector", sV, lambda eng, j=j: eng.bn_aggr(
                    sc1[:, 2 * j:2 * j + 2], sc2[:, 24 * j:24 * j + 24]),
                   inc=False)
            b.drain("vector")
            vst = op("vector", sV, lambda eng: eng.memset(sc1[:, 126:127], 0.0))
            W("scalar", sV, vst)
            vsd = op("scalar", sS, lambda eng: eng.activation(
                sc1[:, 16:24], sc1[:, 1:16:2], AF.Sqrt, bias=consts[:, 34:35]))
            W("vector", sS, vsd)
            op("vector", sV, lambda eng: eng.reciprocal(
                sc1[:, 24:32], sc1[:, 16:24]), inc=False)
            b.drain("vector")
            op("vector", sV, lambda eng: eng.tensor_tensor(
                sc1[:, 32:40], sc1[:, 0:15:2], sc1[:, 24:32], ALU.mult),
               inc=False)
            b.drain("vector")
            vnm = op("vector", sV, lambda eng: eng.tensor_scalar(
                sc1[:, 32:40], sc1[:, 32:40], -1.0, 0.0, ALU.mult, ALU.bypass))
            W("scalar", sV, vnm)
            vn8 = 0
            for j in range(8):
                src_ = x_sb[:, (8 * hh + j) * D:(8 * hh + j + 1) * D]
                vn8 = op("scalar", sS, lambda eng, s=src_, j=j: eng.activation(
                    ar[:, base + j * D:base + (j + 1) * D], s, AF.Identity,
                    bias=sc1[:, 32 + j:33 + j], scale=sc1[:, 24 + j:25 + j]))
            return vn8

        def quant_half(hh):
            XQ = 0
            XH8 = 0
            W("vector", sT, xq_war[0])
            W("scalar", sT, xq_war[0])
            for j in range(8):
                src_ = x_sb[:, (8 * hh + j) * D:(8 * hh + j + 1) * D]
                for c in range(4):
                    op("vector", sV, lambda eng, s=src_, j=j, c=c: eng.bn_stats(
                        sc2[:, 24 * j + 6 * c:24 * j + 6 * c + 6],
                        s[:, c * 512:(c + 1) * 512]), inc=False)
            b.drain("vector")
            for j in range(8):
                op("vector", sV, lambda eng, j=j: eng.bn_aggr(
                    sc1[:, 2 * j:2 * j + 2], sc2[:, 24 * j:24 * j + 24]),
                   inc=False)
            b.drain("vector")
            vst = op("vector", sV, lambda eng: eng.memset(sc1[:, 126:127], 0.0))
            W("scalar", sV, vst)
            vsd = op("scalar", sS, lambda eng: eng.activation(
                sc1[:, 16:24], sc1[:, 1:17].rearrange("p (a b) -> p (a b)")
                [:, 0:16:2] if False else sc1[:, 1:16:2], AF.Sqrt,
                bias=consts[:, 34:35]))
            W("vector", sS, vsd)
            op("vector", sV, lambda eng: eng.reciprocal(
                sc1[:, 24:32], sc1[:, 16:24]), inc=False)
            b.drain("vector")
            op("vector", sV, lambda eng: eng.tensor_tensor(
                sc1[:, 32:40], sc1[:, 0:15:2], sc1[:, 24:32], ALU.mult),
               inc=False)
            b.drain("vector")
            vnm = op("vector", sV, lambda eng: eng.tensor_scalar(
                sc1[:, 32:40], sc1[:, 32:40], -1.0, 0.0, ALU.mult, ALU.bypass))
            W("scalar", sV, vnm)
            vnorm8 = 0
            for j in range(8):
                src_ = x_sb[:, (8 * hh + j) * D:(8 * hh + j + 1) * D]
                vnorm8 = op("scalar", sS, lambda eng, s=src_, j=j: eng.activation(
                    ar[:, XH8 + j * D:XH8 + (j + 1) * D], s, AF.Identity,
                    bias=sc1[:, 32 + j:33 + j], scale=sc1[:, 24 + j:25 + j]))
            W("vector", sS, vnorm8)
            for j in range(8):
                op("vector", sV, lambda eng, j=j: eng.tensor_reduce(
                    sc1[:, 56 + j:57 + j], ar[:, XH8 + j * D:XH8 + (j + 1) * D],
                    axis=mybir.AxisListType.X, op=ALU.max,
                    apply_absolute_value=True), inc=False)
            b.drain("vector")
            op("vector", sV, lambda eng: eng.tensor_scalar(
                sc1[:, 56:64], sc1[:, 56:64], 1e-7, 0.0, ALU.max, ALU.bypass),
               inc=False)
            b.drain("vector")
            op("vector", sV, lambda eng: eng.reciprocal(
                sc1[:, 48:56], sc1[:, 56:64]), inc=False)
            b.drain("vector")
            op("vector", sV, lambda eng: eng.tensor_scalar(
                sc1[:, 48:56], sc1[:, 48:56], 127.0, 0.0, ALU.mult, ALU.bypass),
               inc=False)
            b.drain("vector")
            vsc_ = op("vector", sV, lambda eng: eng.tensor_scalar(
                sc1[:, 56:64], sc1[:, 56:64], float(-1.0 / 127.0), 0.0,
                ALU.mult, ALU.bypass))
            vq = vsc_
            vi_prev = 0
            for j in range(8):
                xh = ar[:, XH8 + j * D:XH8 + (j + 1) * D]
                dst = xh
                for hb in range(2):
                    xhh = xh[:, hb * 1024:(hb + 1) * 1024]
                    dsth = dst[:, hb * 1024:(hb + 1) * 1024]
                    W("scalar", sV, vq)
                    vi = op("scalar", sS, lambda eng, x=xhh, j=j: eng.activation(
                        i16t[:], x, AF.Identity, scale=sc1[:, 48 + j:49 + j]))
                    W("vector", sS, vi)
                    op("vector", sV, lambda eng, x=xhh, j=j:
                       eng.scalar_tensor_tensor(
                           dscr[:], x, sc1[:, 48 + j:49 + j], i16t[:],
                           ALU.mult, ALU.subtract), inc=False)
                    b.drain("vector")
                    op("vector", sV, lambda eng: eng.scalar_tensor_tensor(
                        sqs[:, 0:1024], dscr[:], 0.5, i16t[:], ALU.is_ge,
                        ALU.add), inc=False)
                    b.drain("vector")
                    op("vector", sV, lambda eng: eng.scalar_tensor_tensor(
                        i16t[:], dscr[:], -0.5, sqs[:, 0:1024], ALU.is_le,
                        ALU.subtract), inc=False)
                    b.drain("vector")
                    vq = op("vector", sV, lambda eng, d=dsth, j=j:
                            eng.tensor_scalar(d, i16t[:],
                                              sc1[:, 56 + j:57 + j], 0.0,
                                              ALU.mult, ALU.bypass))
            return vq

        def ffn_sublayer():
            emit_late_preps()
            xq_war[0] = max(xq_war[0], sT.n)
            vdone = 0
            vq = [0, 0]
            vq[0] = quant_half(0)
            for hh in range(2):
                vxt = 0
                for j in range(8):
                    vxt = tile_to_xT(ar[:, j * D:(j + 1) * D], j,
                                     [(sV, vq[hh])], war_v=xt_war[0])
                xq_war[0] = sT.n
                if hh == 0:
                    vq[1] = quant_half(1)
                vdone = ffn_mm(hh, vxt)
                xt_war[0] = sT.n
            return vdone

        def s3_sublayer():
            W("gpsimd", sS, sS.n)
            W("gpsimd", sV, sV.n)
            kq_war = [sT.n]
            kq_war2 = [sT.n, sT.n]
            s3_w_done = [0]
            kst_war = [0, 0]
            knorm_done = 0
            s3ln_war = [sT.n]
            for hh in range(2):
                vxt = 0
                vn8 = ln8(hh, 12288, s3ln_war[0])
                for j in range(8):
                    vxt = tile_to_xT(ar[:, 12288 + j * D:12288 + (j + 1) * D],
                                     j, [(sS, vn8)], war_v=xt_war[0])
                s3ln_war[0] = sT.n
                xt_war[0] = 0
                for which in ("k",):
                    for h in range(H):
                        wc = ar[:, 8192 + (h % 2) * 2048:
                                8192 + (h % 2) * 2048 + 2048]
                        W("gpsimd", sT, kq_war2[h % 2])
                        vw = dma("gpsimd", dG,
                                 wc.rearrange("p (c e) -> p c e", e=128),
                                 wk_in[h].rearrange("(c dd) e -> dd c e", dd=128))
                        pc = (h % 2) * 1024
                        W("tensor", dG, vw)
                        W("tensor", sV, vxt)
                        bank_wait("tensor", pc)
                        bank_wait("tensor", pc + 512)
                        for d in range(16):
                            xc = ar[:, XT + d * 1024:XT + (d + 1) * 1024]
                            mm(pc, 512, wc[:, d * 128:(d + 1) * 128],
                               xc[:, 0:512], start=(d == 0), stop=(d == 15),
                               inc=False)
                            vmm = mm(pc + 512, 512, wc[:, d * 128:(d + 1) * 128],
                                     xc[:, 512:1024], start=(d == 0),
                                     stop=(d == 15), inc=(d == 15))
                        kq_war2[h % 2] = vmm
                        W("scalar", sT, vmm)
                        if True:
                            sl2 = 0
                            kst = ar[:, KST:KST + 1024]
                            W("scalar", dQ, dQ.n)
                            vev = op("scalar", sS, lambda eng, ks=kst, pc=pc, h=h:
                                     eng.activation(ks, psum[:, pc:pc + 1024],
                                                    AF.Identity,
                                                    bias=bk_sb[:, h:h + 1]))
                            war_bank[pc] = (sS, vev)
                            war_bank[pc + 512] = (sS, vev)
                            W("scalar", sT, sqs_war[0])
                            vsq2 = op("scalar", sS, lambda eng, ks=kst:
                                      eng.activation(sqs[:, 0:1024], ks,
                                                     AF.Square))
                            for cch in range(2):
                                W("tensor", sS, vsq2)
                                bank_wait("tensor", 2048)
                                vmm2 = mm(2048, 512, onesm[:],
                                          sqs[:, cch * 512:(cch + 1) * 512],
                                          True, True)
                                sqs_war[0] = vmm2
                                W("vector", sT, vmm2)
                                b.drain("vector")
                                vrd = op("vector", sV, lambda eng, cch=cch:
                                         eng.reduce_max(sc1[:, 64 + cch:65 + cch],
                                                        psum[:, 2048:2560],
                                                        axis=mybir.AxisListType.X))
                                war_bank[2048] = (sV, vrd)
                            b.drain("vector")
                            if hh == 0:
                                vkn = op("vector", sV, lambda eng, h=h:
                                         eng.tensor_reduce(
                                             sc1[:, 96 + h:97 + h],
                                             sc1[:, 64:66],
                                             axis=mybir.AxisListType.X,
                                             op=ALU.max))
                            else:
                                op("vector", sV, lambda eng, h=h:
                                   eng.tensor_reduce(
                                       sc1[:, 64:65], sc1[:, 64:66],
                                       axis=mybir.AxisListType.X, op=ALU.max),
                                   inc=False)
                                b.drain("vector")
                                vkn = op("vector", sV, lambda eng, h=h:
                                         eng.tensor_tensor(
                                             sc1[:, 96 + h:97 + h],
                                             sc1[:, 96 + h:97 + h],
                                             sc1[:, 64:65], ALU.max))
                            knorm_done = vkn
                            W("sync", sS, vev)
                            kst_war[sl2] = dma("sync", dQ,
                                               cc_kv_in[h * 128:(h + 1) * 128,
                                                        hh * 1024:(hh + 1) * 1024],
                                               kst)
                W("sync", sT, b1_war[0])
                vload_bv = dma("sync", dQ, b1row[0:1, 256:2304],
                               brows_in[2:3, :])
                premark(dQ)
                for hq in range(4):
                    wvb = ar[:, 0:8192]
                    W("gpsimd", sT, kq_war[0])
                    for h4 in range(4):
                        vw = dma("gpsimd", dG,
                                 wvb.rearrange("p (c e) -> p c e", e=512)
                                 [:, :, h4 * 128:(h4 + 1) * 128],
                                 wv_in[hq * 4 + h4]
                                 .rearrange("(c dd) e -> dd c e", dd=128))
                    for ttg in range(2):
                        for tt in range(4):
                            bank_wait("tensor", tt * 512)
                        W("tensor", dG, vw)
                        W("tensor", sV, vxt)
                        W("tensor", dQ, vload_bv)
                        for d in range(16):
                            xc = ar[:, XT + d * 1024:XT + (d + 1) * 1024]
                            for tt in range(4):
                                j2 = ttg * 4 + tt
                                vmm = mm(tt * 512, 512,
                                         xc[:, j2 * 128:(j2 + 1) * 128],
                                         wvb[:, d * 512:(d + 1) * 512],
                                         start=(d == 0), stop=False,
                                         inc=(tt == 3))
                        kq_war[0] = sT.n
                        for tt in range(4):
                            ti = 8 * hh + ttg * 4 + tt
                            pc = tt * 512
                            vb = biasmm(pc, 512,
                                        b1row[0:1, 256 + hq * 512:
                                              768 + hq * 512])
                            W("vector", sT, vb)
                            vst_ap = ar[:, KST + 1024:KST + 1536]
                            W("vector", dS, dS.n)
                            vev = op("vector", sV, lambda eng, v=vst_ap, pc=pc:
                                     eng.tensor_copy(v, psum[:, pc:pc + 512]))
                            war_bank[pc] = (sV, vev)
                            W("scalar", sV, vev)
                            kst_war[1] = dma(
                                "scalar", dS,
                                cc_kv_in[2048 + ti * 128:2048 + (ti + 1) * 128,
                                         hq * 512:(hq + 1) * 512],
                                vst_ap)
                xt_war[0] = sT.n
            s3_w_done[0] = sT.n
            W("sync", sV, knorm_done)
            v_nrm_w = dma("sync", dQ, cc_nrm_in[:], sc1[0:1, 96:112])
            W("gpsimd", dQ, dQ.n)
            W("gpsimd", dS, dS.n)
            cC.n += 1
            b.emit("gpsimd", lambda eng, s=cC.h: eng.collective_compute(
                "AllGather", ALU.bypass, replica_groups=RG,
                ins=[cc_kv_in[:]], outs=[cc_kv_out[:]]).then_inc(s, 1))
            v_ag_kv = cC.n
            cC.n += 1
            b.emit("gpsimd", lambda eng, s=cC.h: eng.collective_compute(
                "AllGather", ALU.bypass, replica_groups=RG,
                ins=[cc_nrm_in[:]], outs=[cc_nrm_out[:]]).then_inc(s, 1))
            v_ag_nrm = cC.n

            for hh in range(2):
                vxt = 0
                vn8 = ln8(hh, 12288, s3ln_war[0])
                for j in range(8):
                    vxt = tile_to_xT(ar[:, 12288 + j * D:12288 + (j + 1) * D],
                                     j, [(sS, vn8)], war_v=xt_war[0])
                s3ln_war[0] = sT.n
                xt_war[0] = 0
                for h in range(H):
                    wc = ar[:, WC3:WC3 + 2048]
                    W("gpsimd", sT, kq_war[0])
                    vw = dma("gpsimd", dG,
                             wc.rearrange("p (c e) -> p c e", e=128),
                             wq_in[h].rearrange("(c dd) e -> dd c e", dd=128))
                    pc = (h % 2) * 1024
                    W("tensor", dG, vw)
                    W("tensor", sV, vxt)
                    bank_wait("tensor", pc)
                    bank_wait("tensor", pc + 512)
                    for d in range(16):
                        xc = ar[:, XT + d * 1024:XT + (d + 1) * 1024]
                        mm(pc, 512, wc[:, d * 128:(d + 1) * 128],
                           xc[:, 0:512], start=(d == 0), stop=(d == 15),
                           inc=False)
                        vmm = mm(pc + 512, 512, wc[:, d * 128:(d + 1) * 128],
                                 xc[:, 512:1024], start=(d == 0),
                                 stop=(d == 15), inc=(d == 15))
                    kq_war[0] = vmm
                    W("scalar", sT, max(vmm, s3_w_done[0]))
                    qdst = ar[:, QH + h * 2048 + hh * 1024:
                              QH + h * 2048 + hh * 1024 + 1024]
                    vev = op("scalar", sS, lambda eng, dd=qdst, pc=pc, h=h:
                             eng.activation(dd, psum[:, pc:pc + 1024],
                                            AF.Identity,
                                            bias=bq_sb[:, h:h + 1]))
                    war_bank[pc] = (sS, vev)
                    war_bank[pc + 512] = (sS, vev)
                xt_war[0] = sT.n
            v_qfull = sS.n

            for h in range(H):
                qsrc = ar[:, QH + h * 2048:QH + (h + 1) * 2048]
                normmax_to(qsrc, 2048, 2048, 32 + h, h % 2, [(sS, v_qfull)])
            W("sync", cC, v_ag_nrm)
            vn0 = dma("sync", dQ, nrm2[0:1, 0:16], cc_nrm_out[0:1, :])
            vn1 = dma("sync", dQ, nrm2[0:1, 16:32], cc_nrm_out[1:2, :])
            W("vector", dQ, vn1)
            b.drain("vector")
            vnk = op("vector", sV, lambda eng: eng.tensor_tensor(
                nrm2[0:1, 0:16], nrm2[0:1, 0:16], nrm2[0:1, 16:32], ALU.max))
            W("tensor", sV, vnk)
            bank_wait("tensor", 2048)
            vbb = mm(2048, 16, b1row[0:1, 0:128], nrm2[0:1, 0:16], True, True)
            W("vector", sT, vbb)
            b.drain("vector")
            vkb = op("vector", sV, lambda eng: eng.tensor_copy(
                sc1[:, 96:112], psum[:, 2048:2064]))
            war_bank[2048] = (sV, vkb)
            b.drain("vector")
            op("vector", sV, lambda eng: eng.scalar_tensor_tensor(
                sc1[:, 48:64], sc1[:, 32:48], 1.0, sc1[:, 96:112],
                ALU.mult, ALU.mult), inc=False)
            b.drain("vector")
            vpr = op("vector", sV, lambda eng: eng.tensor_scalar(
                sc1[:, 48:64], sc1[:, 48:64], 1e-20, 0.0, ALU.max, ALU.bypass))
            W("scalar", sV, vpr)
            vsq4 = op("scalar", sS, lambda eng: eng.activation(
                sc1[:, 80:96], sc1[:, 48:64], AF.Sqrt))
            W("vector", sS, vsq4)
            b.drain("vector")
            vC_all = op("vector", sV, lambda eng: eng.tensor_scalar(
                negC[:, 0:16], sc1[:, 80:96], -SCALE, 0.0,
                ALU.mult, ALU.bypass))

            kvload_war = [sT.n, sT.n]
            attn_war["p"] = [max(v, sT.n) for v in attn_war["p"]]
            vctx = 0
            it = 0
            for h in range(H):
                for qc in range(4):
                    ext = 4 * qc + 4
                    buf = it % 2
                    base = KVB + buf * 8192
                    it += 1
                    kv_marks = {"base": base}
                    W("sync", cC, v_ag_kv)
                    W("gpsimd", cC, v_ag_kv)
                    W("sync", sT, kvload_war[buf])
                    W("gpsimd", sT, kvload_war[buf])
                    for ph in range(2):
                        kv_marks[("k", ph)] = dma(
                            "sync", dQ, ar[:, base + ph * 2048:
                                           base + ph * 2048 + ext * 128],
                            cc_kv_out[ph * 4096 + h * 128:
                                      ph * 4096 + (h + 1) * 128, 0:ext * 128])
                        kv_marks[("v", ph)] = dma(
                            "gpsimd", dG,
                            ar[:, base + 4096 + ph * 2048:
                               base + 4096 + ph * 2048 + ext * 128],
                            cc_kv_out[ph * 4096 + 2048:ph * 4096 + 2048
                                      + ext * 128, h * 128:(h + 1) * 128]
                            .rearrange("(m tt) e -> tt m e", tt=128))
                    vctx = run_attn_inner("b", 0, h, qc, vC_all, v_qfull,
                                          kv_marks=kv_marks)
                    kvload_war[buf] = sT.n
            vdone = out_proj(
                lambda n_: wo_in.rearrange("h e d -> e h d")
                [:, :, n_ * 512:(n_ + 1) * 512],
                3, list(range(16)), 2048, lambda ti: ti, vctx,
                load_q="gpsimd")
            return vdone

        for gq in range(2):
            v_qh = s1_gq(gq)
        ka_ready[0] = sS.n
        vctx0 = s1_attention(0, sS.n)
        s1_out(0, vctx0)
        for gq in range(2, 4):
            v_qh = s1_gq(gq)
        ka_ready[1] = sS.n
        vctx1 = s1_attention(1, ka_ready[1], filler=emit_ffnwT_part)
        v_ffnwT[0] = ckpt("scalar", dS)
        v_s1 = s1_out(1, vctx1)

        if dbg_stage != 1:
            W("sync", sV, v_s1)
            for r in range(8):
                dma("sync", dQ, masks[:, r * 512:(r + 1) * 512], maskb_in[r])
            v_s2 = ffn_sublayer()
            if dbg_stage != 2:
                v_s3 = s3_sublayer()
                if dbg_stage != 3:
                    v_s4 = ffn_sublayer()

        b.drain("vector")
        vfin = op("vector", sV, lambda eng: eng.memset(sc1[:, 127:128], 0.0))
        dO = mksem("dO")
        for i in range(NT):
            mk = x_mark[i] if (dbg_stage == 0 and x_mark[i] > 0) else vfin
            W("gpsimd", sV, mk)
            b.emit("gpsimd", lambda eng, i=i, s=dO.h: eng.dma_start(
                out=y[i * 128:(i + 1) * 128, :],
                in_=x_sb[:, i * D:(i + 1) * D]).then_inc(s, 16))
            dO.n += 16
        W("gpsimd", dO, dO.n)

        with nc.Block() as block:
            b.run(block)

    return nc


_NC_CACHE = {}


def _get_nc():
    key = DBG_STAGE
    if key not in _NC_CACHE:
        _NC_CACHE[key] = build_nc(DBG_STAGE)
    return _NC_CACHE[key]


def _build_masks(p):
    trikc = (np.arange(128)[:, None] <= np.arange(128)[None, :]).astype(np.float32)
    maskq = np.zeros((8, 128, 512), np.float32)
    for r in range(8):
        for s_ in range(4):
            js_even = (r % 2 == 0)
            gk = r + p if js_even else r - p
            g = 2 * s_ + p
            blk = maskq[r][:, s_ * 128:(s_ + 1) * 128]
            if gk < g:
                blk[:] = 1.0
            elif gk == g:
                blk[:] = trikc
    maskb = np.zeros((8, 128, 512), np.float32)
    for ph in range(2):
        for r in range(4):
            for s_ in range(4):
                gk = 2 * r + ph
                g = 2 * s_ + p
                blk = maskb[ph * 4 + r][:, s_ * 128:(s_ + 1) * 128]
                if gk < g:
                    blk[:] = 1.0
                elif gk == g:
                    blk[:] = trikc
    return (maskq.astype(ml_dtypes.bfloat16), maskb.astype(ml_dtypes.bfloat16))


def kernel(L0_attn_a_x, L0_attn_a_Wqkv_w, L0_attn_a_Wqkv_b, L0_attn_a_out_w,
           L0_attn_a_out_b, L0_attn_b_W_Q, L0_attn_b_W_K, L0_attn_b_W_V,
           L0_attn_b_W_O, L0_attn_b_b_Q, L0_attn_b_b_K, L0_attn_b_b_V,
           L0_attn_b_b_O, L0_ffn_weight, L0_ffn_w_scale, L0_ffn_bias):
    nc = _get_nc()
    x = np.ascontiguousarray(L0_attn_a_x)
    wqkv = np.ascontiguousarray(L0_attn_a_Wqkv_w)
    bqkv = np.ascontiguousarray(L0_attn_a_Wqkv_b)
    outw = np.ascontiguousarray(L0_attn_a_out_w)
    bouta = np.ascontiguousarray(L0_attn_a_out_b)
    wq = np.ascontiguousarray(L0_attn_b_W_Q)
    wk = np.ascontiguousarray(L0_attn_b_W_K)
    wv = np.ascontiguousarray(L0_attn_b_W_V)
    wo = np.ascontiguousarray(L0_attn_b_W_O)
    bQ = np.ascontiguousarray(L0_attn_b_b_Q)
    bK = np.ascontiguousarray(L0_attn_b_b_K)
    bV = np.ascontiguousarray(L0_attn_b_b_V)
    bO = np.ascontiguousarray(L0_attn_b_b_O)
    fw = np.ascontiguousarray(L0_ffn_weight)
    ws = np.ascontiguousarray(L0_ffn_w_scale)
    fb = np.ascontiguousarray(L0_ffn_bias)

    bqkv_f = bqkv.reshape(NEC, 128).T.copy()
    wscale_t = ws[:, 0].reshape(16, 128).T.copy()
    bq_t = bQ.T.copy()
    bk_t = bK.T.copy()
    brows = np.stack([bouta, fb, bV.reshape(-1), bO]).astype(np.float32)
    bva = bqkv[2176:2304].reshape(1, 128).copy()
    ones512 = np.ones((1, 128), np.float32)
    epscol = np.full((128, 1), 1e-5, np.float32)
    ident = np.eye(128, dtype=ml_dtypes.bfloat16)
    onesm = np.ones((128, 128), dtype=ml_dtypes.bfloat16)

    in_maps = []
    for c in range(8):
        bb, p = c // 2, c % 2
        xb = x[bb]
        tiles = xb.reshape(32, 128, D)
        own = tiles[p::2]
        x0own = own.reshape(2048, D).copy()
        order = []
        for i in range(16):
            order.append(2 * i + p)
            order.append(2 * i + (1 - p))
        x0stg = tiles[order].reshape(S, D).copy()
        maskq, maskb = _build_masks(p)
        in_maps.append({
            "x0own": x0own, "x0stg": x0stg, "wqkv": wqkv, "bqkv_f": bqkv_f,
            "outw": outw, "ffnw": fw, "wscale_t": wscale_t,
            "wq": wq, "wk": wk, "wv": wv, "wo": wo,
            "bq_t": bq_t, "bk_t": bk_t, "brows": brows, "bva": bva,
            "ones_f32": ones512, "epscol": epscol, "ident": ident, "onesm": onesm,
            "maskq": maskq, "maskb": maskb,
        })
    res = run_bass_kernel_spmd(nc, in_maps, core_ids=list(range(8)))
    out = np.empty((4, S, D), np.float32)
    for c in range(8):
        bb, p = c // 2, c % 2
        yt = res.results[c]["y"].reshape(16, 128, D)
        ot = out[bb].reshape(32, 128, D)
        ot[p::2] = yt
    return out



# version 16
# speedup vs baseline: 1.0618x; 1.0058x over previous
import numpy as np
import ml_dtypes
import concourse.bass as bass
import concourse.mybir as mybir
from concourse.bass_utils import run_bass_kernel_spmd

F32 = mybir.dt.float32
BF16 = mybir.dt.bfloat16
F16 = mybir.dt.float16
I8 = mybir.dt.int8
I16 = mybir.dt.int16
AF = mybir.ActivationFunctionType
ALU = mybir.AluOpType

D = 2048
S = 4096
H = 16
NT = 16
E_QKV = 2304
NEC = 18
SCALE = float(1.0 / np.sqrt(128.0))

ENGS = ("sync", "scalar", "vector", "tensor", "gpsimd")
DBG_STAGE = 0


class Sem:
    def __init__(self, h):
        self.h = h
        self.n = 0
        self.wmax = 0
        self.guarded = 0
        self.queue = None


class Bld:
    def __init__(self):
        self.th = {e: [] for e in ENGS}

    def emit(self, e, fn):
        self.th[e].append(fn)

    def wait(self, e, sem, val):
        if val > 0:
            sem.wmax = max(sem.wmax, val)
            self.th[e].append(lambda eng, s=sem.h, v=val: eng.wait_ge(s, v))

    def drain(self, e):
        self.th[e].append(lambda eng: eng.drain())

    def run(self, block):
        for e in ENGS:
            def mk(ths):
                def body(eng):
                    for f in ths:
                        f(eng)
                return body
            getattr(block, e)(mk(self.th[e]))


def attn_tiles(kind, qc):
    out = []
    if kind == "a":
        for js in range(8 * qc + 8):
            msk = js - 8 * qc if js >= 8 * qc else None
            out.append((js, msk))
    else:
        for ph in range(2):
            for m_ in range(4 * qc + 4):
                msk = ph * 4 + (m_ - 4 * qc) if m_ >= 4 * qc else None
                out.append(((ph, m_), msk))
    return out


def build_nc(dbg_stage=0):
    nc = bass.Bass("TRN2", target_bir_lowering=False)

    def din(name, shape, dt):
        return nc.dram_tensor(name, shape, dt, kind="ExternalInput").ap()

    x0own = din("x0own", [2048, D], F32)
    x0stg = din("x0stg", [S, D], F32)
    wqkv = din("wqkv", [E_QKV, D], F32)
    bqkv_f = din("bqkv_f", [128, NEC], F32)
    outw = din("outw", [D, D], F32)
    ffnw = din("ffnw", [D, D], I8)
    wscale_t = din("wscale_t", [128, 16], F32)
    wq_in = din("wq", [H, D, 128], F32)
    wk_in = din("wk", [H, D, 128], F32)
    wv_in = din("wv", [H, D, 128], F32)
    wo_in = din("wo", [H, 128, D], F32)
    bq_t = din("bq_t", [128, H], F32)
    bk_t = din("bk_t", [128, H], F32)
    brows_in = din("brows", [4, D], F32)
    bva_in = din("bva", [1, 128], F32)
    ones_in = din("ones_f32", [1, 128], F32)
    eps_in = din("epscol", [128, 1], F32)
    ident_in = din("ident", [128, 128], BF16)
    onesm_in = din("onesm", [128, 128], BF16)
    maskq_in = din("maskq", [8, 128, 512], BF16)
    maskb_in = din("maskb", [8, 128, 512], BF16)

    y = nc.dram_tensor("y", [2048, D], F32, kind="ExternalOutput").ap()

    cc_kv_in = nc.dram_tensor("cc_kv_in", [4096, D], BF16, kind="Internal").ap()
    cc_kv_out = nc.dram_tensor("cc_kv_out", [8192, D], BF16, kind="Internal",
                               addr_space="Local").ap()
    cc_nrm_in = nc.dram_tensor("cc_nrm_in", [1, H], F32, kind="Internal").ap()
    cc_nrm_out = nc.dram_tensor("cc_nrm_out", [2, H], F32, kind="Internal",
                                addr_space="Local").ap()
    wqkvT_d = nc.dram_tensor("wqkvT_d", [D, E_QKV], BF16, kind="Internal").ap()
    outwT_d = nc.dram_tensor("outwT_d", [D, D], BF16, kind="Internal").ap()
    ffnwT_d = nc.dram_tensor("ffnwT_d", [D, D], BF16, kind="Internal").ap()
    wqb_d = nc.dram_tensor("wqb_d", [H * D, 128], BF16, kind="Internal").ap()
    wkb_d = nc.dram_tensor("wkb_d", [H * D, 128], BF16, kind="Internal").ap()
    wvg_d = nc.dram_tensor("wvg_d", [D, D], BF16, kind="Internal").ap()
    wob_d = nc.dram_tensor("wob_d", [H * 128, D], BF16, kind="Internal").ap()
    RG = [[0, 1], [2, 3], [4, 5], [6, 7]]

    import contextlib
    with contextlib.ExitStack() as stk:
        def sbuf(name, shape, dt):
            return stk.enter_context(nc.sbuf_tensor(name, shape, dt))

        def mksem(name):
            return Sem(stk.enter_context(nc.semaphore(name)))

        x_sb = sbuf("x_sb", [128, NT * D], BF16)
        masks = sbuf("masksb", [128, 8 * 512], BF16)
        ident = sbuf("identb", [128, 128], BF16)
        onesm = sbuf("onesmb", [128, 128], BF16)
        b1row = sbuf("b1row", [1, 2304], F32)
        bq_sb = sbuf("bq_sbb", [128, H], F32)
        bk_sb = sbuf("bk_sbb", [128, H], F32)
        consts = sbuf("constsb", [128, 40], F32)
        sc1 = sbuf("sc1b", [128, 128], F32)
        sc2 = sbuf("sc2b", [128, 192], F32)
        nrm2 = sbuf("nrm2b", [1, 32], F32)
        negC = sbuf("negCb", [128, H], F32)
        fp16t = sbuf("fp16tb", [128, 512], F16)
        i16t = sbuf("i16tb", [128, 1024], I16)
        i8st = sbuf("i8stb", [128, D], I8)
        dscr = sbuf("dscrb", [128, 1024], F32)
        sqs = sbuf("sqsb", [128, D], BF16)
        AR = 56832
        ar = sbuf("arena", [128, AR], BF16)
        QH = 0
        WCOL = 16384
        PSL = 16384
        WOUT = 16384
        XHO = 20480
        VA = 24576
        KA = 28672
        XT = 32768
        KVB = 32768
        PSL3 = 49152
        XHI = 49152
        SM3 = 49152
        WC3 = 53248
        KST = 55296

        psum = stk.enter_context(nc.psum_tensor("psumT", [128, 3072], F32))
        psT = stk.enter_context(nc.psum_tensor("psumB", [128, 2048], BF16))

        sT = mksem("sT")
        sV = mksem("sV")
        sS = mksem("sS")
        dQ = mksem("dQ")
        dS = mksem("dS2")
        dV = mksem("dV2")
        dG = mksem("dG2")
        cC = mksem("cC")

        b = Bld()

        def dma(q, qsem, out, in_):
            qsem.queue = q
            if qsem.wmax > qsem.guarded:
                g = qsem.wmax
                b.th[q].append(lambda eng, s=qsem.h, v=g: eng.wait_ge(s, v))
                qsem.guarded = g
            b.emit(q, lambda eng, o=out, i=in_, s=qsem.h:
                   eng.dma_start(out=o, in_=i).then_inc(s, 16))
            qsem.n += 16
            return qsem.n

        def op(e, esem, f, inc=True):
            if inc:
                esem.n += 1
                b.emit(e, lambda eng, f=f, s=esem.h: f(eng).then_inc(s, 1))
                return esem.n
            b.emit(e, lambda eng, f=f: f(eng))
            return esem.n

        W = b.wait

        b1_war = [0]

        def premark(qsem):
            qsem.wmax = max(qsem.wmax, qsem.n)

        def ckpt(qname, qsem):
            v = qsem.n
            if v > qsem.guarded:
                b.th[qname].append(lambda eng, s=qsem.h, vv=v: eng.wait_ge(s, vv))
                qsem.guarded = v
            qsem.wmax = max(qsem.wmax, v)
            return v

        def mm(col, w, lhsT, rhs, start, stop, inc=None):
            if inc is None:
                inc = stop
            return op("tensor", sT,
                      lambda eng, c=col, w=w, l=lhsT, r=rhs, st=start, sp=stop:
                      eng.matmul(psum[:, c:c + w], l, r, start=st, stop=sp),
                      inc=inc)

        def biasmm(col, w, rhs_bias, stop=True, inc=None):
            if inc is None:
                inc = stop
            v = op("tensor", sT,
                   lambda eng, c=col, w=w, r=rhs_bias:
                   eng.matmul(psum[:, c:c + w], b1row[0:1, 0:128], r,
                              start=False, stop=stop), inc=inc)
            b1_war[0] = max(b1_war[0], v)
            return v

        dma("sync", dQ, ident[:], ident_in[:])
        dma("sync", dQ, onesm[:], onesm_in[:])
        dma("sync", dQ, b1row[0:1, 0:128], ones_in[:])
        for r in range(8):
            dma("sync", dQ, masks[:, r * 512:(r + 1) * 512], maskq_in[r])
        dma("sync", dQ, consts[:, 0:NEC], bqkv_f[:])
        dma("sync", dQ, consts[:, 18:34], wscale_t[:])
        dma("sync", dQ, bq_sb[:], bq_t[:])
        dma("sync", dQ, bk_sb[:], bk_t[:])
        dma("sync", dQ, consts[:, 34:35], eps_in[:])
        v_consts = ckpt("sync", dQ)
        for e_ in ("tensor", "vector", "scalar", "gpsimd"):
            W(e_, dQ, v_consts)
        dma("sync", dQ, b1row[0:1, 128:256], bva_in[:])
        v_bva = ckpt("sync", dQ)
        for i in range(NT):
            dma("gpsimd", dG, x_sb[:, i * D:(i + 1) * D],
                x0own[i * 128:(i + 1) * 128, :])
        v_xown = ckpt("gpsimd", dG)

        RAW = [0, 2304]
        EVA = [4608, 6912]
        war_raw = [0, 0]
        war_i8 = [0, 0]
        war_eva = [0, 0]
        war_bank = {}

        def bank_wait(e, col):
            if col in war_bank:
                s_, v_ = war_bank[col]
                W(e, s_, v_)

        def prep_chunk(src_row_ap, dst_ap, ncols_c, sl, load_q, load_sem,
                       scale_col=None, rawbase=None, evabase=None):
            rb_ = RAW[sl] if rawbase is None else rawbase[sl]
            eb_ = EVA[sl] if evabase is None else evabase[sl]
            raw = ar[:, rb_:rb_ + ncols_c * 128]
            eva = ar[:, eb_:eb_ + ncols_c * 128]
            if scale_col is None:
                W(load_q, sT, war_raw[sl])
                vld = dma(load_q, load_sem, raw, src_row_ap)
                vsc = None
            else:
                io = (0 if sl == 0 else 1024)
                W(load_q, sS, war_i8[sl])
                vld = dma(load_q, load_sem,
                          i8st[:, io:io + ncols_c * 128], src_row_ap)
                W("scalar", load_sem, vld)
                W("scalar", sT, war_raw[sl])
                vsc = op("scalar", sS, lambda eng, raw=raw, c=scale_col, io=io,
                         n=ncols_c * 128:
                         eng.activation(raw, i8st[:, io:io + n], AF.Identity,
                                        scale=consts[:, c:c + 1]))
            vv_last = 0
            for g in range((ncols_c + 3) // 4):
                bank = g % 2
                n_in = min(4, ncols_c - 4 * g)
                if vsc is None:
                    W("tensor", load_sem, vld)
                else:
                    W("tensor", sS, vsc)
                bank_wait("tensor", ("t", bank))
                for k in range(n_in):
                    c = 4 * g + k
                    vt = op("tensor", sT, lambda eng, raw=raw, c=c, bank=bank, k=k:
                            eng.transpose(psT[:, bank * 1024 + k * 128:
                                              bank * 1024 + (k + 1) * 128],
                                          raw[:, c * 128:(c + 1) * 128], ident[:]),
                            inc=(k == n_in - 1))
                war_raw[sl] = vt
                if scale_col is not None:
                    war_i8[sl] = vsc
                W("vector", sT, vt)
                W("vector", dS, dS.n)
                for k in range(n_in):
                    c = 4 * g + k
                    vv = op("vector", sV, lambda eng, eva=eva, c=c, bank=bank, k=k:
                            eng.tensor_copy(eva[:, c * 128:(c + 1) * 128],
                                            psT[:, bank * 1024 + k * 128:
                                                bank * 1024 + (k + 1) * 128]),
                            inc=(k == n_in - 1))
                war_bank[("t", bank)] = (sV, vv)
                vv_last = vv
            W("scalar", sV, vv_last)
            war_eva[sl] = dma("scalar", dS, dst_ap,
                              eva.rearrange("p (c e) -> p c e", e=128))
            return war_eva[sl]

        for rb in range(NEC):
            prep_chunk(wqkv[rb * 128:(rb + 1) * 128, :],
                       wqkvT_d.rearrange("(c dd) e -> dd c e", dd=128)
                       [:, :, rb * 128:(rb + 1) * 128], 16, rb % 2, "gpsimd", dG)
        v_wqkvT = ckpt("scalar", dS)
        for rb in range(16):
            prep_chunk(outw[rb * 128:(rb + 1) * 128, :],
                       outwT_d.rearrange("(c dd) e -> dd c e", dd=128)
                       [:, :, rb * 128:(rb + 1) * 128], 16, rb % 2, "gpsimd", dG)
        v_outwT = ckpt("scalar", dS)
        v_wprep = [0]
        v_ffnwT = [0]
        late_done = [False]

        def emit_late_preps():
            if late_done[0]:
                return
            late_done[0] = True
        v_prep_arena_free = 0

        ffnwT_started = [False]

        def emit_ffnwT_part(i):
            if not ffnwT_started[0]:
                ffnwT_started[0] = True
                W("sync", sS, sS.n)
                W("sync", sV, sV.n)
                W("sync", dQ, v_consts)
            rb2 = [XHI, XHI + 1024]
            eb2 = [XHI + 2048, XHI + 3072]
            rb, hb = i // 2, i % 2
            prep_chunk(ffnw[rb * 128:(rb + 1) * 128,
                            hb * 1024:(hb + 1) * 1024],
                       ffnwT_d[hb * 1024:, :]
                       .rearrange("(c dd) e -> dd c e", dd=128)
                       [:, 0:8, rb * 128:(rb + 1) * 128],
                       8, i % 2, "sync", dQ,
                       scale_col=18 + rb, rawbase=rb2, evabase=eb2)

        def layernorm_tile(src, dst, sl):
            for c in range(4):
                op("vector", sV, lambda eng, s=src, sl=sl, c=c: eng.bn_stats(
                    sc2[:, 24 * sl + 6 * c:24 * sl + 6 * c + 6],
                    s[:, c * 512:(c + 1) * 512]), inc=False)
            b.drain("vector")
            vst = op("vector", sV, lambda eng, sl=sl: eng.bn_aggr(
                sc1[:, 2 * sl:2 * sl + 2], sc2[:, 24 * sl:24 * sl + 24]))
            W("scalar", sV, vst)
            sst = op("scalar", sS, lambda eng, sl=sl: eng.activation(
                sc1[:, 16 + sl:17 + sl], sc1[:, 2 * sl + 1:2 * sl + 2],
                AF.Sqrt, bias=consts[:, 34:35]))
            W("vector", sS, sst)
            op("vector", sV, lambda eng, sl=sl: eng.reciprocal(
                sc1[:, 24 + sl:25 + sl], sc1[:, 16 + sl:17 + sl]), inc=False)
            b.drain("vector")
            vr = op("vector", sV, lambda eng, sl=sl: eng.tensor_scalar(
                sc1[:, 32 + sl:33 + sl], sc1[:, 2 * sl:2 * sl + 1],
                sc1[:, 24 + sl:25 + sl], -1.0, ALU.mult, ALU.mult))
            W("scalar", sV, vr)
            return op("scalar", sS, lambda eng, s=src, d=dst, sl=sl: eng.activation(
                d, s, AF.Identity, bias=sc1[:, 32 + sl:33 + sl],
                scale=sc1[:, 24 + sl:25 + sl]))

        def tile_to_xT(src, tcol, ready_pairs, war_v=0):
            vv_last = 0
            for g in range(4):
                bank = g % 2
                for (s_, v_) in ready_pairs:
                    W("tensor", s_, v_)
                bank_wait("tensor", ("t", bank))
                for k in range(4):
                    c = 4 * g + k
                    vt = op("tensor", sT, lambda eng, s=src, c=c, bank=bank, k=k:
                            eng.transpose(psT[:, bank * 1024 + k * 128:
                                              bank * 1024 + (k + 1) * 128],
                                          s[:, c * 128:(c + 1) * 128], ident[:]),
                            inc=(k == 3))
                W("vector", sT, max(vt, war_v))
                for k in range(4):
                    c = 4 * g + k
                    vv = op("vector", sV, lambda eng, c=c, bank=bank, k=k, t=tcol:
                            eng.tensor_copy(
                                ar[:, XT + c * 1024 + t * 128:
                                   XT + c * 1024 + (t + 1) * 128],
                                psT[:, bank * 1024 + k * 128:
                                    bank * 1024 + (k + 1) * 128]),
                            inc=(k == 3))
                war_bank[("t", bank)] = (sV, vv)
                vv_last = vv
            return vv_last

        def normmax(src, ncols, pcol, out_col, extra_sq_wait=()):
            nch = ncols // 512
            cdone = 0
            for big in range(0, ncols, 2048):
                bw = min(2048, ncols - big)
                for (s_, v_) in extra_sq_wait:
                    W("scalar", s_, v_)
                W("scalar", sT, sqs_war[0])
                vsq = op("scalar", sS, lambda eng, s=src, b0=big, bw=bw:
                         eng.activation(sqs[:, 0:bw], s[:, b0:b0 + bw], AF.Square))
                for cc2 in range(bw // 512):
                    cch = cdone
                    cdone += 1
                    W("tensor", sS, vsq)
                    bank_wait("tensor", pcol)
                    vmm = mm(pcol, 512, onesm[:],
                             sqs[:, cc2 * 512:(cc2 + 1) * 512], True, True)
                    sqs_war[0] = vmm
                    W("vector", sT, vmm)
                    b.drain("vector")
                    vrd = op("vector", sV,
                             lambda eng, cch=cch, pc=pcol, oc=out_col:
                             eng.reduce_max(
                                 sc1[:, 64 + oc * 8 + cch:65 + oc * 8 + cch],
                                 psum[:, pc:pc + 512],
                                 axis=mybir.AxisListType.X))
                    war_bank[pcol] = (sV, vrd)
            b.drain("vector")
            vfin = op("vector", sV, lambda eng, oc=out_col, n=nch:
                      eng.tensor_reduce(sc1[:, 40 + oc:41 + oc],
                                        sc1[:, 64 + oc * 8:64 + oc * 8 + n],
                                        axis=mybir.AxisListType.X, op=ALU.max))
            return 40 + out_col, vfin

        xhi_war = [0, 0]
        xho_war = [0, 0]
        wcol_war = [0, 0]
        v_out_a = [0, 0]

        def s1_gq(gq):
            hh = gq // 2
            vxt = 0
            for j in range(8):
                js = gq * 8 + j
                sl = js % 2
                xin = ar[:, XHI + sl * D:XHI + sl * D + D]
                W("gpsimd", sS, xhi_war[sl])
                vld = dma("gpsimd", dG, xin, x0stg[js * 128:(js + 1) * 128, :])
                W("vector", dG, vld)
                xout = ar[:, XHO + sl * D:XHO + sl * D + D]
                W("vector", sT, max(xho_war[sl], war_wout[0]))
                W("scalar", sT, max(xho_war[sl], war_wout[0]))
                vn = layernorm_tile(xin, xout, js % 8)
                xhi_war[sl] = vn
                vxt = tile_to_xT(xout, j, [(sS, vn)])
                xho_war[sl] = sT.n
            for e in range(16):
                wsl = e % 2
                wc = ar[:, WCOL + wsl * 2048:WCOL + wsl * 2048 + 2048]
                W("sync", sT, max(wcol_war[wsl], war_wout[0]))
                W("sync", dS, v_wqkvT)
                vw = dma("sync", dQ, wc.rearrange("p (c e) -> p c e", e=128),
                         wqkvT_d.rearrange("(c dd) e -> dd c e", dd=128)
                         [:, :, e * 128:(e + 1) * 128])
                pc = (e % 2) * 512
                W("tensor", dQ, vw)
                W("tensor", sV, vxt)
                bank_wait("tensor", pc)
                for d in range(16):
                    xc = ar[:, XT + d * 1024:XT + (d + 1) * 1024]
                    rhs = xc.rearrange("p (a c) -> p a c", c=128)[:, 0:8:2, :]
                    vmm = mm(pc, 512, wc[:, d * 128:(d + 1) * 128], rhs,
                             start=(d == 0), stop=(d == 15))
                wcol_war[wsl] = vmm
                W("scalar", sT, vmm)
                if gq < 2 and e <= 8:
                    W("scalar", dQ, v_prep_arena_free)
                W("scalar", sT, war_wout[0])
                qdst = ar[:, QH + e * 1024 + (gq % 2) * 512:
                          QH + e * 1024 + (gq % 2) * 512 + 512]
                vev = op("scalar", sS, lambda eng, dd=qdst, pc=pc, e=e:
                         eng.activation(dd, psum[:, pc:pc + 512], AF.Identity,
                                        bias=consts[:, e:e + 1]))
                war_bank[pc] = (sS, vev)
            v_qh = sS.n
            wc = ar[:, WCOL:WCOL + 2048]
            W("sync", sT, max(wcol_war[0], war_wout[0]))
            W("sync", dS, v_wqkvT)
            vw = dma("sync", dQ, wc.rearrange("p (c e) -> p c e", e=128),
                     wqkvT_d.rearrange("(c dd) e -> dd c e", dd=128)
                     [:, :, 16 * 128:17 * 128])
            W("tensor", dQ, vw)
            W("tensor", sV, vxt)
            bank_wait("tensor", 1024)
            bank_wait("tensor", 1536)
            for d in range(16):
                xc = ar[:, XT + d * 1024:XT + (d + 1) * 1024]
                mm(1024, 512, wc[:, d * 128:(d + 1) * 128], xc[:, 0:512],
                   start=(d == 0), stop=(d == 15), inc=False)
                vmm = mm(1536, 512, wc[:, d * 128:(d + 1) * 128], xc[:, 512:1024],
                         start=(d == 0), stop=(d == 15), inc=(d == 15))
            wcol_war[0] = vmm
            W("scalar", sT, vmm)
            vk = op("scalar", sS, lambda eng, gq=gq: eng.activation(
                ar[:, KA + gq * 1024:KA + (gq + 1) * 1024],
                psum[:, 1024:2048], AF.Identity, bias=consts[:, 16:17]))
            war_bank[1024] = (sS, vk)
            war_bank[1536] = (sS, vk)
            wc2 = ar[:, WCOL + 2048:WCOL + 4096]
            W("sync", sT, max(wcol_war[1], war_wout[0]))
            vw2 = dma("sync", dQ, wc2.rearrange("p (c e) -> p c e", e=128),
                      wqkvT_d.rearrange("(c dd) e -> dd c e", dd=128)
                      [:, :, 17 * 128:18 * 128])
            W("tensor", dQ, vw2)
            W("tensor", dQ, v_bva)
            for tt in range(8):
                pc = 2048 + (tt % 2) * 512
                bank_wait("tensor", pc)
                for d in range(16):
                    xc = ar[:, XT + d * 1024:XT + (d + 1) * 1024]
                    mm(pc, 128, xc[:, tt * 128:(tt + 1) * 128],
                       wc2[:, d * 128:(d + 1) * 128],
                       start=(d == 0), stop=False, inc=False)
                vb = biasmm(pc, 128, b1row[0:1, 128:256], stop=True)
                wcol_war[1] = vb
                W("vector", sT, vb)
                vva = op("vector", sV, lambda eng, gq=gq, tt=tt, pc=pc:
                         eng.tensor_copy(
                             ar[:, VA + (gq * 8 + tt) * 128:
                                VA + (gq * 8 + tt + 1) * 128],
                             psum[:, pc:pc + 128]))
                war_bank[pc] = (sV, vva)
            return v_qh

        ka_ready = {}
        sqs_war = [0]
        attn_war = {"p": [0] * 8, "s": [0] * 4}

        def normmax_to(src_ap, ncols, pcol, dst_col, sq_slot, wait_pairs):
            nch = ncols // 512
            cdone = 0
            red_marks = []
            for big in range(0, ncols, 1024):
                bw = min(1024, ncols - big)
                for (s_, v_) in wait_pairs:
                    W("scalar", s_, v_)
                W("scalar", sT, sqs_war[0])
                W("scalar", sV, sqs_rd[sq_slot])
                vsq = op("scalar", sS, lambda eng, s=src_ap, b0=big, bw=bw,
                         o=sq_slot * 1024:
                         eng.activation(sqs[:, o:o + bw], s[:, b0:b0 + bw],
                                        AF.Square))
                for cc2 in range(bw // 512):
                    cch = cdone
                    cdone += 1
                    W("tensor", sS, vsq)
                    bank_wait("tensor", pcol)
                    vmm = mm(pcol, 512, onesm[:],
                             sqs[:, sq_slot * 1024 + cc2 * 512:
                                 sq_slot * 1024 + (cc2 + 1) * 512], True, True)
                    sqs_war[0] = vmm
                    sqs_rd[sq_slot] = 0
                    W("vector", sT, vmm)
                    b.drain("vector")
                    vrd = op("vector", sV, lambda eng, cch=cch, pc=pcol,
                             dc=64 + sq_slot * 8 + cch % 8:
                             eng.reduce_max(sc1[:, dc:dc + 1],
                                            psum[:, pc:pc + 512],
                                            axis=mybir.AxisListType.X))
                    war_bank[pcol] = (sV, vrd)
                    red_marks.append(vrd)
            b.drain("vector")
            vfin = op("vector", sV, lambda eng, dc=dst_col, n=nch,
                      o=64 + sq_slot * 8:
                      eng.tensor_reduce(sc1[:, dc:dc + 1], sc1[:, o:o + n],
                                        axis=mybir.AxisListType.X, op=ALU.max))
            return vfin

        sqs_rd = [0, 0]

        def s1_attention(hh, v_qh, filler=None):
            attn_war["p"] = [max(v, war_wout[0]) for v in attn_war["p"]]
            ksrc = ar[:, KA:KA + 16 * (hh + 1) * 128]
            vnk = normmax_to(ksrc, 16 * (hh + 1) * 128, 512, 96, 0,
                             [(sS, ka_ready[hh])])
            for h in range(H):
                qsrc = ar[:, QH + h * 1024:QH + (h + 1) * 1024]
                vq_ = normmax_to(qsrc, 1024, 512, 100 + h, h % 2,
                                 [(sS, v_qh)])
            b.drain("vector")
            for h in range(H):
                op("vector", sV, lambda eng, h=h: eng.tensor_scalar(
                    sc1[:, 100 + h:101 + h], sc1[:, 100 + h:101 + h],
                    sc1[:, 96:97], 1e-20, ALU.mult, ALU.max), inc=False)
            b.drain("vector")
            vpr = op("vector", sV, lambda eng: eng.memset(sc1[:, 127:128], 0.0))
            W("scalar", sV, vpr)
            vsq = op("scalar", sS, lambda eng: eng.activation(
                sc1[:, 80:96], sc1[:, 100:116], AF.Sqrt))
            W("vector", sS, vsq)
            b.drain("vector")
            vC = op("vector", sV, lambda eng: eng.tensor_scalar(
                negC[:, 0:16], sc1[:, 80:96], -SCALE, 0.0,
                ALU.mult, ALU.bypass))
            vctx = 0
            for h in range(H):
                if filler is not None:
                    filler(2 * h)
                    filler(2 * h + 1)
                for qc in (2 * hh, 2 * hh + 1):
                    vctx = run_attn_inner("a", hh, h, qc, vC, v_qh)
            return vctx

        def run_attn_inner(kind, hh, h, qc, vC, v_qsrc, kv_marks=None):
            tiles = attn_tiles(kind, qc)
            qbase = QH + h * (1024 if kind == "a" else 2048)
            qoff = (qc - 2 * hh) * 512 if kind == "a" else qc * 512
            qap = ar[:, qbase + qoff:qbase + qoff + 512]
            psl = PSL if kind == "a" else PSL3
            pslot_war = attn_war["p"]
            sbank_war = attn_war["s"]
            n = len(tiles)
            def kap_of(src):
                if kind == "a":
                    return ar[:, KA + src * 128:KA + (src + 1) * 128]
                ph, m_ = src
                return kv_buf_ap("k", ph, m_, kv_marks.get("base"))

            def vap_of(src):
                if kind == "a":
                    return ar[:, VA + src * 128:VA + (src + 1) * 128]
                ph, m_ = src
                return kv_buf_ap("v", ph, m_, kv_marks.get("base"))

            vs_marks = [0] * n
            v_avlast = 0
            for ii in range(n):
                src, msk = tiles[ii]
                sb = 1024 + (ii % 4) * 512
                W("tensor", sS, sbank_war[ii % 4])
                if kind == "a":
                    W("tensor", sS, v_qsrc)
                else:
                    W("tensor", dQ, max(kv_marks[("k", 0)], kv_marks[("k", 1)]))
                vs_marks[ii] = mm(sb, 512, kap_of(src), qap, True, True)
                if ii % 2 == 1 or ii == n - 1:
                    lo = ii - (ii % 2)
                    wdt = 512 * (ii - lo + 1)
                    pap = ar[:, psl + (lo % 8) * 512:psl + (lo % 8) * 512 + wdt]
                    W("scalar", sT, vs_marks[ii])
                    W("scalar", sT, max(pslot_war[lo % 8], pslot_war[ii % 8]))
                    W("scalar", sV, vC)
                    ve = op("scalar", sS, lambda eng, p=pap, sb=1024 + (lo % 4) * 512,
                            wdt=wdt, h=h:
                            eng.activation(p, psum[:, sb:sb + wdt], AF.Exp,
                                           bias=negC[:, h:h + 1], scale=SCALE))
                    for jj in range(lo, ii + 1):
                        sbank_war[jj % 4] = ve
                    for jj in range(lo, ii + 1):
                        src2, msk2 = tiles[jj]
                        pap1 = ar[:, psl + (jj % 8) * 512:psl + (jj % 8) * 512 + 512]
                        if msk2 is not None:
                            W("vector", sS, ve)
                            vmk = op("vector", sV, lambda eng, p=pap1, r=msk2:
                                     eng.tensor_tensor(
                                         p, p, masks[:, r * 512:(r + 1) * 512],
                                         ALU.mult))
                            W("tensor", sV, vmk)
                        else:
                            W("tensor", sS, ve)
                        st = (jj == 0)
                        if st:
                            bank_wait("tensor", 512)
                            bank_wait("tensor", 0)
                        last = (jj == n - 1)
                        mm(512, 512, onesm[:], pap1, st, last, inc=False)
                        if kind == "b":
                            W("tensor", dG, max(kv_marks[("v", 0)],
                                                kv_marks[("v", 1)]))
                        vav = mm(0, 512, vap_of(src2), pap1, st, last, inc=True)
                        pslot_war[jj % 8] = vav
                        if last:
                            v_avlast = vav
            W("vector", sT, v_avlast)
            op("vector", sV, lambda eng: eng.reciprocal(
                dscr[:, 0:512], psum[:, 512:1024]), inc=False)
            b.drain("vector")
            cdst = ar[:, qbase + qoff:qbase + qoff + 512]
            vnorm = op("vector", sV, lambda eng, cd=cdst:
                       eng.tensor_tensor(cd, psum[:, 0:512], dscr[:, 0:512],
                                         ALU.mult))
            war_bank[0] = (sV, vnorm)
            war_bank[512] = (sV, vnorm)
            return vnorm

        def kv_buf_ap(which, ph, m_, kvbase=None):
            base = (KVB if kvbase is None else kvbase) + (0 if which == "k" else 4096)
            return ar[:, base + ph * 2048 + m_ * 128:
                      base + ph * 2048 + (m_ + 1) * 128]

        war_wout = [0]

        def out_proj(wdram_fn, brow_idx, tiles_, qstride, tcol_fn, vctx,
                     vx_extra=None, wready=0, load_q="sync", dbuf_base=None):
            war_wout[0] = max(war_wout[0], sT.n)
            wout2_war = [war_wout[0], war_wout[0]]
            lsem = dQ if load_q == "sync" else dG
            W("sync", sT, b1_war[0])
            vload_b = dma("sync", dQ, b1row[0:1, 256:2304],
                          brows_in[brow_idx:brow_idx + 1, :])
            premark(dQ)
            for n_ in range(4):
                base_w = dbuf_base if (dbuf_base is not None and n_ % 2) else WOUT
                wsl_ap = ar[:, base_w:base_w + 8192]
                W(load_q, sT, wout2_war[n_ % 2] if dbuf_base is not None
                  else war_wout[0])
                W(load_q, dS, wready)
                vw = dma(load_q, lsem, wsl_ap.rearrange("p (c e) -> p c e", e=512),
                         wdram_fn(n_))
                for ji, ti in enumerate(tiles_):
                    pc = (ji % 4) * 512
                    W("tensor", lsem, vw)
                    W("tensor", sV, vctx)
                    W("tensor", dQ, vload_b)
                    bank_wait("tensor", pc)
                    for c in range(16):
                        mm(pc, 512, ar[:, QH + c * qstride + tcol_fn(ti) * 128:
                                       QH + c * qstride + (tcol_fn(ti) + 1) * 128],
                           wsl_ap[:, c * 512:(c + 1) * 512],
                           start=(c == 0), stop=False, inc=False)
                    vb = biasmm(pc, 512,
                                b1row[0:1, 256 + n_ * 512:768 + n_ * 512])
                    W("vector", sT, vb)
                    if vx_extra is not None:
                        W("vector", vx_extra[0], vx_extra[1])
                    xsl = x_sb[:, ti * D + n_ * 512:ti * D + (n_ + 1) * 512]
                    vep = op("vector", sV, lambda eng, xs=xsl, pc=pc:
                             eng.tensor_tensor(xs, xs, psum[:, pc:pc + 512],
                                               ALU.add))
                    war_bank[pc] = (sV, vep)
                war_wout[0] = sT.n
                wout2_war[n_ % 2] = sT.n
            return sV.n

        def s1_out(hh, vctx):
            return out_proj(
                lambda n_: outwT_d.rearrange("(c dd) e -> dd c e", dd=128)
                [:, :, n_ * 512:(n_ + 1) * 512],
                0, list(range(8 * hh, 8 * hh + 8)), 1024,
                lambda ti: ti - 8 * hh, vctx, wready=v_outwT,
                dbuf_base=(WOUT + 8192 if hh == 1 else XT))

        fp16_war = [0]
        x_mark = [0] * NT

        def ffn_mm(hh, vxt):
            war_wout[0] = max(war_wout[0], sT.n)
            W("sync", sT, b1_war[0])
            vload_b = dma("sync", dQ, b1row[0:1, 256:2304],
                          brows_in[1:2, :])
            premark(dQ)
            wf2_war = [war_wout[0], war_wout[0]]
            for n_ in range(4):
                base_w = WOUT + 8192 * (n_ % 2)
                wsl_ap = ar[:, base_w:base_w + 8192]
                W("sync", sT, wf2_war[n_ % 2])
                W("sync", dS, v_ffnwT[0])
                vw = dma("sync", dQ, wsl_ap.rearrange("p (c e) -> p c e", e=512),
                         ffnwT_d.rearrange("(c dd) e -> dd c e", dd=128)
                         [:, :, n_ * 512:(n_ + 1) * 512])
                for ji in range(8):
                    ti = 8 * hh + ji
                    pc = (ji % 4) * 512
                    W("tensor", dQ, vw)
                    W("tensor", sV, vxt)
                    W("tensor", dQ, vload_b)
                    bank_wait("tensor", pc)
                    for c in range(16):
                        mm(pc, 512, ar[:, XT + c * 1024 + ji * 128:
                                       XT + c * 1024 + (ji + 1) * 128],
                           wsl_ap[:, c * 512:(c + 1) * 512],
                           start=(c == 0), stop=False, inc=False)
                    vb = biasmm(pc, 512,
                                b1row[0:1, 256 + n_ * 512:768 + n_ * 512])
                    W("scalar", sT, vb)
                    W("scalar", sV, fp16_war[0])
                    vf = op("scalar", sS, lambda eng, pc=pc: eng.copy(
                        fp16t[:, 0:512], psum[:, pc:pc + 512]))
                    W("vector", sS, vf)
                    xsl = x_sb[:, ti * D + n_ * 512:ti * D + (n_ + 1) * 512]
                    vep = op("vector", sV, lambda eng, xs=xsl: eng.tensor_tensor(
                        xs, xs, fp16t[:, 0:512], ALU.add))
                    fp16_war[0] = vep
                    x_mark[ti] = vep
                    war_bank[pc] = (sV, vep)
                war_wout[0] = sT.n
                wf2_war[n_ % 2] = sT.n
            return sV.n

        quant_war = {"xh": [0, 0], "i16": 0}

        def quant_tile(src, dst, sl):
            W("scalar", sV, quant_war["xh"][sl % 2])
            vn = layernorm_tile(src, ar[:, SM3 + (sl % 2) * D:
                                        SM3 + (sl % 2) * D + D], sl)
            xh = ar[:, SM3 + (sl % 2) * D:SM3 + (sl % 2) * D + D]
            W("vector", sS, vn)
            vam = op("vector", sV, lambda eng, x=xh, sl=sl: eng.tensor_reduce(
                sc1[:, 56 + sl:57 + sl], x, axis=mybir.AxisListType.X,
                op=ALU.max, apply_absolute_value=True))
            b.drain("vector")
            op("vector", sV, lambda eng, sl=sl: eng.tensor_scalar(
                sc1[:, 56 + sl:57 + sl], sc1[:, 56 + sl:57 + sl], 1e-7, 0.0,
                ALU.max, ALU.bypass), inc=False)
            b.drain("vector")
            op("vector", sV, lambda eng, sl=sl: eng.reciprocal(
                sc1[:, 48 + sl:49 + sl], sc1[:, 56 + sl:57 + sl]), inc=False)
            b.drain("vector")
            vsq_ = op("vector", sV, lambda eng, sl=sl: eng.tensor_scalar(
                sc1[:, 48 + sl:49 + sl], sc1[:, 48 + sl:49 + sl], 127.0, 0.0,
                ALU.mult, ALU.bypass))
            b.drain("vector")
            vnis = op("vector", sV, lambda eng, sl=sl: eng.tensor_scalar(
                sc1[:, 56 + sl:57 + sl], sc1[:, 56 + sl:57 + sl],
                float(-1.0 / 127.0), 0.0, ALU.mult, ALU.bypass))
            vq = 0
            for hb in range(2):
                xhh = xh[:, hb * 1024:(hb + 1) * 1024]
                dsth = dst[:, hb * 1024:(hb + 1) * 1024]
                W("scalar", sV, vsq_ if hb == 0 else vq)
                W("scalar", sV, quant_war["i16"])
                vi = op("scalar", sS, lambda eng, x=xhh, sl=sl: eng.activation(
                    i16t[:], x, AF.Identity, scale=sc1[:, 48 + sl:49 + sl]))
                W("vector", sS, vi)
                op("vector", sV, lambda eng, x=xhh, sl=sl:
                   eng.scalar_tensor_tensor(
                       dscr[:], x, sc1[:, 48 + sl:49 + sl], i16t[:],
                       ALU.mult, ALU.subtract), inc=False)
                b.drain("vector")
                op("vector", sV, lambda eng: eng.scalar_tensor_tensor(
                    sqs[:, 0:1024], dscr[:], 0.5, i16t[:], ALU.is_ge, ALU.add),
                   inc=False)
                b.drain("vector")
                op("vector", sV, lambda eng: eng.scalar_tensor_tensor(
                    i16t[:], dscr[:], -0.5, sqs[:, 0:1024], ALU.is_le,
                    ALU.subtract), inc=False)
                b.drain("vector")
                vq = op("vector", sV, lambda eng, d=dsth, sl=sl:
                        eng.tensor_scalar(d, i16t[:], sc1[:, 56 + sl:57 + sl],
                                          0.0, ALU.mult, ALU.bypass))
                quant_war["i16"] = vq
            quant_war["xh"][sl % 2] = vq
            return vq

        xt_war = [0]

        xq_war = [0]

        def ln8(hh, base, war_mark):
            W("vector", sT, war_mark)
            W("scalar", sT, war_mark)
            for j in range(8):
                src_ = x_sb[:, (8 * hh + j) * D:(8 * hh + j + 1) * D]
                for c in range(4):
                    op("vector", sV, lambda eng, s=src_, j=j, c=c: eng.bn_stats(
                        sc2[:, 24 * j + 6 * c:24 * j + 6 * c + 6],
                        s[:, c * 512:(c + 1) * 512]), inc=False)
            b.drain("vector")
            for j in range(8):
                op("vector", sV, lambda eng, j=j: eng.bn_aggr(
                    sc1[:, 2 * j:2 * j + 2], sc2[:, 24 * j:24 * j + 24]),
                   inc=False)
            b.drain("vector")
            vst = op("vector", sV, lambda eng: eng.memset(sc1[:, 126:127], 0.0))
            W("scalar", sV, vst)
            vsd = op("scalar", sS, lambda eng: eng.activation(
                sc1[:, 16:24], sc1[:, 1:16:2], AF.Sqrt, bias=consts[:, 34:35]))
            W("vector", sS, vsd)
            op("vector", sV, lambda eng: eng.reciprocal(
                sc1[:, 24:32], sc1[:, 16:24]), inc=False)
            b.drain("vector")
            op("vector", sV, lambda eng: eng.tensor_tensor(
                sc1[:, 32:40], sc1[:, 0:15:2], sc1[:, 24:32], ALU.mult),
               inc=False)
            b.drain("vector")
            vnm = op("vector", sV, lambda eng: eng.tensor_scalar(
                sc1[:, 32:40], sc1[:, 32:40], -1.0, 0.0, ALU.mult, ALU.bypass))
            W("scalar", sV, vnm)
            vn8 = 0
            for j in range(8):
                src_ = x_sb[:, (8 * hh + j) * D:(8 * hh + j + 1) * D]
                vn8 = op("scalar", sS, lambda eng, s=src_, j=j: eng.activation(
                    ar[:, base + j * D:base + (j + 1) * D], s, AF.Identity,
                    bias=sc1[:, 32 + j:33 + j], scale=sc1[:, 24 + j:25 + j]))
            return vn8

        def quant_half(hh):
            XQ = 0
            XH8 = 0
            W("vector", sT, xq_war[0])
            W("scalar", sT, xq_war[0])
            for j in range(8):
                src_ = x_sb[:, (8 * hh + j) * D:(8 * hh + j + 1) * D]
                for c in range(4):
                    op("vector", sV, lambda eng, s=src_, j=j, c=c: eng.bn_stats(
                        sc2[:, 24 * j + 6 * c:24 * j + 6 * c + 6],
                        s[:, c * 512:(c + 1) * 512]), inc=False)
            b.drain("vector")
            for j in range(8):
                op("vector", sV, lambda eng, j=j: eng.bn_aggr(
                    sc1[:, 2 * j:2 * j + 2], sc2[:, 24 * j:24 * j + 24]),
                   inc=False)
            b.drain("vector")
            vst = op("vector", sV, lambda eng: eng.memset(sc1[:, 126:127], 0.0))
            W("scalar", sV, vst)
            vsd = op("scalar", sS, lambda eng: eng.activation(
                sc1[:, 16:24], sc1[:, 1:17].rearrange("p (a b) -> p (a b)")
                [:, 0:16:2] if False else sc1[:, 1:16:2], AF.Sqrt,
                bias=consts[:, 34:35]))
            W("vector", sS, vsd)
            op("vector", sV, lambda eng: eng.reciprocal(
                sc1[:, 24:32], sc1[:, 16:24]), inc=False)
            b.drain("vector")
            op("vector", sV, lambda eng: eng.tensor_tensor(
                sc1[:, 32:40], sc1[:, 0:15:2], sc1[:, 24:32], ALU.mult),
               inc=False)
            b.drain("vector")
            vnm = op("vector", sV, lambda eng: eng.tensor_scalar(
                sc1[:, 32:40], sc1[:, 32:40], -1.0, 0.0, ALU.mult, ALU.bypass))
            W("scalar", sV, vnm)
            vnorm8 = 0
            for j in range(8):
                src_ = x_sb[:, (8 * hh + j) * D:(8 * hh + j + 1) * D]
                vnorm8 = op("scalar", sS, lambda eng, s=src_, j=j: eng.activation(
                    ar[:, XH8 + j * D:XH8 + (j + 1) * D], s, AF.Identity,
                    bias=sc1[:, 32 + j:33 + j], scale=sc1[:, 24 + j:25 + j]))
            W("vector", sS, vnorm8)
            for j in range(8):
                op("vector", sV, lambda eng, j=j: eng.tensor_reduce(
                    sc1[:, 56 + j:57 + j], ar[:, XH8 + j * D:XH8 + (j + 1) * D],
                    axis=mybir.AxisListType.X, op=ALU.max,
                    apply_absolute_value=True), inc=False)
            b.drain("vector")
            op("vector", sV, lambda eng: eng.tensor_scalar(
                sc1[:, 56:64], sc1[:, 56:64], 1e-7, 0.0, ALU.max, ALU.bypass),
               inc=False)
            b.drain("vector")
            op("vector", sV, lambda eng: eng.reciprocal(
                sc1[:, 48:56], sc1[:, 56:64]), inc=False)
            b.drain("vector")
            op("vector", sV, lambda eng: eng.tensor_scalar(
                sc1[:, 48:56], sc1[:, 48:56], 127.0, 0.0, ALU.mult, ALU.bypass),
               inc=False)
            b.drain("vector")
            vsc_ = op("vector", sV, lambda eng: eng.tensor_scalar(
                sc1[:, 56:64], sc1[:, 56:64], float(-1.0 / 127.0), 0.0,
                ALU.mult, ALU.bypass))
            vq = vsc_
            vi_prev = 0
            for j in range(8):
                xh = ar[:, XH8 + j * D:XH8 + (j + 1) * D]
                dst = xh
                for hb in range(2):
                    xhh = xh[:, hb * 1024:(hb + 1) * 1024]
                    dsth = dst[:, hb * 1024:(hb + 1) * 1024]
                    W("scalar", sV, vq)
                    vi = op("scalar", sS, lambda eng, x=xhh, j=j: eng.activation(
                        i16t[:], x, AF.Identity, scale=sc1[:, 48 + j:49 + j]))
                    W("vector", sS, vi)
                    op("vector", sV, lambda eng, x=xhh, j=j:
                       eng.scalar_tensor_tensor(
                           dscr[:], x, sc1[:, 48 + j:49 + j], i16t[:],
                           ALU.mult, ALU.subtract), inc=False)
                    b.drain("vector")
                    op("vector", sV, lambda eng: eng.scalar_tensor_tensor(
                        sqs[:, 0:1024], dscr[:], 0.5, i16t[:], ALU.is_ge,
                        ALU.add), inc=False)
                    b.drain("vector")
                    op("vector", sV, lambda eng: eng.scalar_tensor_tensor(
                        i16t[:], dscr[:], -0.5, sqs[:, 0:1024], ALU.is_le,
                        ALU.subtract), inc=False)
                    b.drain("vector")
                    vq = op("vector", sV, lambda eng, d=dsth, j=j:
                            eng.tensor_scalar(d, i16t[:],
                                              sc1[:, 56 + j:57 + j], 0.0,
                                              ALU.mult, ALU.bypass))
            return vq

        def ffn_sublayer():
            emit_late_preps()
            xq_war[0] = max(xq_war[0], sT.n)
            vdone = 0
            vq = [0, 0]
            vq[0] = quant_half(0)
            for hh in range(2):
                vxt = 0
                for j in range(8):
                    vxt = tile_to_xT(ar[:, j * D:(j + 1) * D], j,
                                     [(sV, vq[hh])], war_v=xt_war[0])
                xq_war[0] = sT.n
                if hh == 0:
                    vq[1] = quant_half(1)
                vdone = ffn_mm(hh, vxt)
                xt_war[0] = sT.n
            return vdone

        def s3_sublayer():
            W("gpsimd", sS, sS.n)
            W("gpsimd", sV, sV.n)
            kq_war = [sT.n]
            kq_war2 = [sT.n, sT.n]
            s3_w_done = [0]
            kst_war = [0, 0]
            knorm_done = 0
            s3ln_war = [sT.n]
            for hh in range(2):
                vxt = 0
                vn8 = ln8(hh, 12288, s3ln_war[0])
                for j in range(8):
                    vxt = tile_to_xT(ar[:, 12288 + j * D:12288 + (j + 1) * D],
                                     j, [(sS, vn8)], war_v=xt_war[0])
                s3ln_war[0] = sT.n
                xt_war[0] = 0
                for which in ("k",):
                    for h in range(H):
                        wc = ar[:, 8192 + (h % 2) * 2048:
                                8192 + (h % 2) * 2048 + 2048]
                        W("gpsimd", sT, kq_war2[h % 2])
                        vw = dma("gpsimd", dG,
                                 wc.rearrange("p (c e) -> p c e", e=128),
                                 wk_in[h].rearrange("(c dd) e -> dd c e", dd=128))
                        pc = (h % 2) * 1024
                        W("tensor", dG, vw)
                        W("tensor", sV, vxt)
                        bank_wait("tensor", pc)
                        bank_wait("tensor", pc + 512)
                        for d in range(16):
                            xc = ar[:, XT + d * 1024:XT + (d + 1) * 1024]
                            mm(pc, 512, wc[:, d * 128:(d + 1) * 128],
                               xc[:, 0:512], start=(d == 0), stop=(d == 15),
                               inc=False)
                            vmm = mm(pc + 512, 512, wc[:, d * 128:(d + 1) * 128],
                                     xc[:, 512:1024], start=(d == 0),
                                     stop=(d == 15), inc=(d == 15))
                        kq_war2[h % 2] = vmm
                        W("scalar", sT, vmm)
                        if True:
                            sl2 = 0
                            kst = ar[:, KST:KST + 1024]
                            W("scalar", dQ, dQ.n)
                            vev = op("scalar", sS, lambda eng, ks=kst, pc=pc, h=h:
                                     eng.activation(ks, psum[:, pc:pc + 1024],
                                                    AF.Identity,
                                                    bias=bk_sb[:, h:h + 1]))
                            war_bank[pc] = (sS, vev)
                            war_bank[pc + 512] = (sS, vev)
                            W("scalar", sT, sqs_war[0])
                            vsq2 = op("scalar", sS, lambda eng, ks=kst:
                                      eng.activation(sqs[:, 0:1024], ks,
                                                     AF.Square))
                            for cch in range(2):
                                W("tensor", sS, vsq2)
                                bank_wait("tensor", 2048)
                                vmm2 = mm(2048, 512, onesm[:],
                                          sqs[:, cch * 512:(cch + 1) * 512],
                                          True, True)
                                sqs_war[0] = vmm2
                                W("vector", sT, vmm2)
                                b.drain("vector")
                                vrd = op("vector", sV, lambda eng, cch=cch:
                                         eng.reduce_max(sc1[:, 64 + cch:65 + cch],
                                                        psum[:, 2048:2560],
                                                        axis=mybir.AxisListType.X))
                                war_bank[2048] = (sV, vrd)
                            b.drain("vector")
                            if hh == 0:
                                vkn = op("vector", sV, lambda eng, h=h:
                                         eng.tensor_reduce(
                                             sc1[:, 96 + h:97 + h],
                                             sc1[:, 64:66],
                                             axis=mybir.AxisListType.X,
                                             op=ALU.max))
                            else:
                                op("vector", sV, lambda eng, h=h:
                                   eng.tensor_reduce(
                                       sc1[:, 64:65], sc1[:, 64:66],
                                       axis=mybir.AxisListType.X, op=ALU.max),
                                   inc=False)
                                b.drain("vector")
                                vkn = op("vector", sV, lambda eng, h=h:
                                         eng.tensor_tensor(
                                             sc1[:, 96 + h:97 + h],
                                             sc1[:, 96 + h:97 + h],
                                             sc1[:, 64:65], ALU.max))
                            knorm_done = vkn
                            W("sync", sS, vev)
                            kst_war[sl2] = dma("sync", dQ,
                                               cc_kv_in[h * 128:(h + 1) * 128,
                                                        hh * 1024:(hh + 1) * 1024],
                                               kst)
                W("sync", sT, b1_war[0])
                vload_bv = dma("sync", dQ, b1row[0:1, 256:2304],
                               brows_in[2:3, :])
                premark(dQ)
                for hq in range(4):
                    wvb = ar[:, 0:8192]
                    W("gpsimd", sT, kq_war[0])
                    for h4 in range(4):
                        vw = dma("gpsimd", dG,
                                 wvb.rearrange("p (c e) -> p c e", e=512)
                                 [:, :, h4 * 128:(h4 + 1) * 128],
                                 wv_in[hq * 4 + h4]
                                 .rearrange("(c dd) e -> dd c e", dd=128))
                    for ttg in range(2):
                        for tt in range(4):
                            bank_wait("tensor", tt * 512)
                        W("tensor", dG, vw)
                        W("tensor", sV, vxt)
                        W("tensor", dQ, vload_bv)
                        for d in range(16):
                            xc = ar[:, XT + d * 1024:XT + (d + 1) * 1024]
                            for tt in range(4):
                                j2 = ttg * 4 + tt
                                vmm = mm(tt * 512, 512,
                                         xc[:, j2 * 128:(j2 + 1) * 128],
                                         wvb[:, d * 512:(d + 1) * 512],
                                         start=(d == 0), stop=False,
                                         inc=(tt == 3))
                        kq_war[0] = sT.n
                        for tt in range(4):
                            ti = 8 * hh + ttg * 4 + tt
                            pc = tt * 512
                            vb = biasmm(pc, 512,
                                        b1row[0:1, 256 + hq * 512:
                                              768 + hq * 512])
                            W("vector", sT, vb)
                            vst_ap = ar[:, KST + 1024:KST + 1536]
                            W("vector", dS, dS.n)
                            vev = op("vector", sV, lambda eng, v=vst_ap, pc=pc:
                                     eng.tensor_copy(v, psum[:, pc:pc + 512]))
                            war_bank[pc] = (sV, vev)
                            W("scalar", sV, vev)
                            kst_war[1] = dma(
                                "scalar", dS,
                                cc_kv_in[2048 + ti * 128:2048 + (ti + 1) * 128,
                                         hq * 512:(hq + 1) * 512],
                                vst_ap)
                xt_war[0] = sT.n
            s3_w_done[0] = sT.n
            W("sync", sV, knorm_done)
            v_nrm_w = dma("sync", dQ, cc_nrm_in[:], sc1[0:1, 96:112])
            W("gpsimd", dQ, dQ.n)
            W("gpsimd", dS, dS.n)
            cC.n += 1
            b.emit("gpsimd", lambda eng, s=cC.h: eng.collective_compute(
                "AllGather", ALU.bypass, replica_groups=RG,
                ins=[cc_kv_in[:]], outs=[cc_kv_out[:]]).then_inc(s, 1))
            v_ag_kv = cC.n
            cC.n += 1
            b.emit("gpsimd", lambda eng, s=cC.h: eng.collective_compute(
                "AllGather", ALU.bypass, replica_groups=RG,
                ins=[cc_nrm_in[:]], outs=[cc_nrm_out[:]]).then_inc(s, 1))
            v_ag_nrm = cC.n

            for hh in range(2):
                vxt = 0
                vn8 = ln8(hh, 12288, s3ln_war[0])
                for j in range(8):
                    vxt = tile_to_xT(ar[:, 12288 + j * D:12288 + (j + 1) * D],
                                     j, [(sS, vn8)], war_v=xt_war[0])
                s3ln_war[0] = sT.n
                xt_war[0] = 0
                for h in range(H):
                    wc = ar[:, WC3:WC3 + 2048]
                    W("gpsimd", sT, kq_war[0])
                    vw = dma("gpsimd", dG,
                             wc.rearrange("p (c e) -> p c e", e=128),
                             wq_in[h].rearrange("(c dd) e -> dd c e", dd=128))
                    pc = (h % 2) * 1024
                    W("tensor", dG, vw)
                    W("tensor", sV, vxt)
                    bank_wait("tensor", pc)
                    bank_wait("tensor", pc + 512)
                    for d in range(16):
                        xc = ar[:, XT + d * 1024:XT + (d + 1) * 1024]
                        mm(pc, 512, wc[:, d * 128:(d + 1) * 128],
                           xc[:, 0:512], start=(d == 0), stop=(d == 15),
                           inc=False)
                        vmm = mm(pc + 512, 512, wc[:, d * 128:(d + 1) * 128],
                                 xc[:, 512:1024], start=(d == 0),
                                 stop=(d == 15), inc=(d == 15))
                    kq_war[0] = vmm
                    W("scalar", sT, max(vmm, s3_w_done[0]))
                    qdst = ar[:, QH + h * 2048 + hh * 1024:
                              QH + h * 2048 + hh * 1024 + 1024]
                    vev = op("scalar", sS, lambda eng, dd=qdst, pc=pc, h=h:
                             eng.activation(dd, psum[:, pc:pc + 1024],
                                            AF.Identity,
                                            bias=bq_sb[:, h:h + 1]))
                    war_bank[pc] = (sS, vev)
                    war_bank[pc + 512] = (sS, vev)
                xt_war[0] = sT.n
            v_qfull = sS.n

            for h in range(H):
                qsrc = ar[:, QH + h * 2048:QH + (h + 1) * 2048]
                normmax_to(qsrc, 2048, 2048, 32 + h, h % 2, [(sS, v_qfull)])
            W("sync", cC, v_ag_nrm)
            vn0 = dma("sync", dQ, nrm2[0:1, 0:16], cc_nrm_out[0:1, :])
            vn1 = dma("sync", dQ, nrm2[0:1, 16:32], cc_nrm_out[1:2, :])
            W("vector", dQ, vn1)
            b.drain("vector")
            vnk = op("vector", sV, lambda eng: eng.tensor_tensor(
                nrm2[0:1, 0:16], nrm2[0:1, 0:16], nrm2[0:1, 16:32], ALU.max))
            W("tensor", sV, vnk)
            bank_wait("tensor", 2048)
            vbb = mm(2048, 16, b1row[0:1, 0:128], nrm2[0:1, 0:16], True, True)
            W("vector", sT, vbb)
            b.drain("vector")
            vkb = op("vector", sV, lambda eng: eng.tensor_copy(
                sc1[:, 96:112], psum[:, 2048:2064]))
            war_bank[2048] = (sV, vkb)
            b.drain("vector")
            op("vector", sV, lambda eng: eng.scalar_tensor_tensor(
                sc1[:, 48:64], sc1[:, 32:48], 1.0, sc1[:, 96:112],
                ALU.mult, ALU.mult), inc=False)
            b.drain("vector")
            vpr = op("vector", sV, lambda eng: eng.tensor_scalar(
                sc1[:, 48:64], sc1[:, 48:64], 1e-20, 0.0, ALU.max, ALU.bypass))
            W("scalar", sV, vpr)
            vsq4 = op("scalar", sS, lambda eng: eng.activation(
                sc1[:, 80:96], sc1[:, 48:64], AF.Sqrt))
            W("vector", sS, vsq4)
            b.drain("vector")
            vC_all = op("vector", sV, lambda eng: eng.tensor_scalar(
                negC[:, 0:16], sc1[:, 80:96], -SCALE, 0.0,
                ALU.mult, ALU.bypass))

            kvload_war = [sT.n, sT.n]
            attn_war["p"] = [max(v, sT.n) for v in attn_war["p"]]
            vctx = 0
            it = 0
            for h in range(H):
                for qc in range(4):
                    ext = 4 * qc + 4
                    buf = it % 2
                    base = KVB + buf * 8192
                    it += 1
                    kv_marks = {"base": base}
                    W("sync", cC, v_ag_kv)
                    W("gpsimd", cC, v_ag_kv)
                    W("sync", sT, kvload_war[buf])
                    W("gpsimd", sT, kvload_war[buf])
                    for ph in range(2):
                        kv_marks[("k", ph)] = dma(
                            "sync", dQ, ar[:, base + ph * 2048:
                                           base + ph * 2048 + ext * 128],
                            cc_kv_out[ph * 4096 + h * 128:
                                      ph * 4096 + (h + 1) * 128, 0:ext * 128])
                        kv_marks[("v", ph)] = dma(
                            "gpsimd", dG,
                            ar[:, base + 4096 + ph * 2048:
                               base + 4096 + ph * 2048 + ext * 128],
                            cc_kv_out[ph * 4096 + 2048:ph * 4096 + 2048
                                      + ext * 128, h * 128:(h + 1) * 128]
                            .rearrange("(m tt) e -> tt m e", tt=128))
                    vctx = run_attn_inner("b", 0, h, qc, vC_all, v_qfull,
                                          kv_marks=kv_marks)
                    kvload_war[buf] = sT.n
            vdone = out_proj(
                lambda n_: wo_in.rearrange("h e d -> e h d")
                [:, :, n_ * 512:(n_ + 1) * 512],
                3, list(range(16)), 2048, lambda ti: ti, vctx,
                load_q="gpsimd")
            return vdone

        for gq in range(2):
            v_qh = s1_gq(gq)
        ka_ready[0] = sS.n
        vctx0 = s1_attention(0, sS.n)
        s1_out(0, vctx0)
        for gq in range(2, 4):
            v_qh = s1_gq(gq)
        ka_ready[1] = sS.n
        vctx1 = s1_attention(1, ka_ready[1], filler=emit_ffnwT_part)
        v_ffnwT[0] = ckpt("scalar", dS)
        v_s1 = s1_out(1, vctx1)

        if dbg_stage != 1:
            W("sync", sV, v_s1)
            for r in range(8):
                dma("sync", dQ, masks[:, r * 512:(r + 1) * 512], maskb_in[r])
            v_s2 = ffn_sublayer()
            if dbg_stage != 2:
                v_s3 = s3_sublayer()
                if dbg_stage != 3:
                    v_s4 = ffn_sublayer()

        b.drain("vector")
        vfin = op("vector", sV, lambda eng: eng.memset(sc1[:, 127:128], 0.0))
        dO = mksem("dO")
        for i in range(NT):
            mk = x_mark[i] if (dbg_stage == 0 and x_mark[i] > 0) else vfin
            W("gpsimd", sV, mk)
            b.emit("gpsimd", lambda eng, i=i, s=dO.h: eng.dma_start(
                out=y[i * 128:(i + 1) * 128, :],
                in_=x_sb[:, i * D:(i + 1) * D]).then_inc(s, 16))
            dO.n += 16
        W("gpsimd", dO, dO.n)

        with nc.Block() as block:
            b.run(block)

    return nc


_NC_CACHE = {}


def _get_nc():
    key = DBG_STAGE
    if key not in _NC_CACHE:
        _NC_CACHE[key] = build_nc(DBG_STAGE)
    return _NC_CACHE[key]


def _build_masks(p):
    trikc = (np.arange(128)[:, None] <= np.arange(128)[None, :]).astype(np.float32)
    maskq = np.zeros((8, 128, 512), np.float32)
    for r in range(8):
        for s_ in range(4):
            js_even = (r % 2 == 0)
            gk = r + p if js_even else r - p
            g = 2 * s_ + p
            blk = maskq[r][:, s_ * 128:(s_ + 1) * 128]
            if gk < g:
                blk[:] = 1.0
            elif gk == g:
                blk[:] = trikc
    maskb = np.zeros((8, 128, 512), np.float32)
    for ph in range(2):
        for r in range(4):
            for s_ in range(4):
                gk = 2 * r + ph
                g = 2 * s_ + p
                blk = maskb[ph * 4 + r][:, s_ * 128:(s_ + 1) * 128]
                if gk < g:
                    blk[:] = 1.0
                elif gk == g:
                    blk[:] = trikc
    return (maskq.astype(ml_dtypes.bfloat16), maskb.astype(ml_dtypes.bfloat16))


def kernel(L0_attn_a_x, L0_attn_a_Wqkv_w, L0_attn_a_Wqkv_b, L0_attn_a_out_w,
           L0_attn_a_out_b, L0_attn_b_W_Q, L0_attn_b_W_K, L0_attn_b_W_V,
           L0_attn_b_W_O, L0_attn_b_b_Q, L0_attn_b_b_K, L0_attn_b_b_V,
           L0_attn_b_b_O, L0_ffn_weight, L0_ffn_w_scale, L0_ffn_bias):
    nc = _get_nc()
    x = np.ascontiguousarray(L0_attn_a_x)
    wqkv = np.ascontiguousarray(L0_attn_a_Wqkv_w)
    bqkv = np.ascontiguousarray(L0_attn_a_Wqkv_b)
    outw = np.ascontiguousarray(L0_attn_a_out_w)
    bouta = np.ascontiguousarray(L0_attn_a_out_b)
    wq = np.ascontiguousarray(L0_attn_b_W_Q)
    wk = np.ascontiguousarray(L0_attn_b_W_K)
    wv = np.ascontiguousarray(L0_attn_b_W_V)
    wo = np.ascontiguousarray(L0_attn_b_W_O)
    bQ = np.ascontiguousarray(L0_attn_b_b_Q)
    bK = np.ascontiguousarray(L0_attn_b_b_K)
    bV = np.ascontiguousarray(L0_attn_b_b_V)
    bO = np.ascontiguousarray(L0_attn_b_b_O)
    fw = np.ascontiguousarray(L0_ffn_weight)
    ws = np.ascontiguousarray(L0_ffn_w_scale)
    fb = np.ascontiguousarray(L0_ffn_bias)

    bqkv_f = bqkv.reshape(NEC, 128).T.copy()
    wscale_t = ws[:, 0].reshape(16, 128).T.copy()
    bq_t = bQ.T.copy()
    bk_t = bK.T.copy()
    brows = np.stack([bouta, fb, bV.reshape(-1), bO]).astype(np.float32)
    bva = bqkv[2176:2304].reshape(1, 128).copy()
    ones512 = np.ones((1, 128), np.float32)
    epscol = np.full((128, 1), 1e-5, np.float32)
    ident = np.eye(128, dtype=ml_dtypes.bfloat16)
    onesm = np.ones((128, 128), dtype=ml_dtypes.bfloat16)

    in_maps = []
    for c in range(8):
        bb, p = c // 2, c % 2
        xb = x[bb]
        tiles = xb.reshape(32, 128, D)
        own = tiles[p::2]
        x0own = own.reshape(2048, D).copy()
        order = []
        for i in range(16):
            order.append(2 * i + p)
            order.append(2 * i + (1 - p))
        x0stg = tiles[order].reshape(S, D).copy()
        maskq, maskb = _build_masks(p)
        in_maps.append({
            "x0own": x0own, "x0stg": x0stg, "wqkv": wqkv, "bqkv_f": bqkv_f,
            "outw": outw, "ffnw": fw, "wscale_t": wscale_t,
            "wq": wq, "wk": wk, "wv": wv, "wo": wo,
            "bq_t": bq_t, "bk_t": bk_t, "brows": brows, "bva": bva,
            "ones_f32": ones512, "epscol": epscol, "ident": ident, "onesm": onesm,
            "maskq": maskq, "maskb": maskb,
        })
    res = run_bass_kernel_spmd(nc, in_maps, core_ids=list(range(8)))
    out = np.empty((4, S, D), np.float32)
    for c in range(8):
        bb, p = c // 2, c % 2
        yt = res.results[c]["y"].reshape(16, 128, D)
        ot = out[bb].reshape(32, 128, D)
        ot[p::2] = yt
    return out

